# version 25
# baseline (speedup 1.0000x reference)
import numpy as np
import ml_dtypes
import concourse.bass as bass
import concourse.mybir as mybir
from concourse.bass_core_rust import IndirectOffsetOnAxis
from concourse.bass_utils import run_bass_kernel_spmd

BF16 = mybir.dt.bfloat16
F32 = mybir.dt.float32
U32 = mybir.dt.uint32
AF = mybir.ActivationFunctionType
A = mybir.AluOpType
AX = mybir.AxisListType
bf = ml_dtypes.bfloat16

D_MODEL, N_HEADS, D_H, D_HR, D_LAT = 1024, 16, 64, 32, 256
S, B, NC = 2048, 16, 8
SQ = 512
NKT, NQT, NCH = 16, 4, 8
NBLK, NPOS = 2, 256
NTOK = 16 * NPOS
INV_SQRT_DQK = 1.0 / np.sqrt(96.0)
CLIP_HI = float(np.exp(np.float32(80.0) * INV_SQRT_DQK))
CLIP_LO = float(np.exp(np.float32(-80.0) * INV_SQRT_DQK))
EPS = 1e-5
JBIG = 16.0


class Prog:
    ENGINES = ("sync", "scalar", "vector", "tensor", "gpsimd")

    def __init__(self, nc):
        self.nc = nc
        self.ops = {e: [] for e in self.ENGINES}
        self.sems = {}
        self.counts = {}
        self._rot = {}

    def sem(self, name):
        if name not in self.sems:
            self.sems[name] = self.nc.semaphore(name).__enter__()
            self.counts[name] = 0
        return self.sems[name]

    def emit(self, engine, fn):
        self.ops[engine].append(fn)

    def inc(self, name, by):
        h = self.sem(name)
        self.counts[name] += by
        return h, self.counts[name]

    def cur(self, name):
        self.sem(name)
        return self.counts[name]

    def wait(self, engine, name, val):
        h = self.sem(name)
        self.emit(engine, lambda e, h=h, v=val: e.wait_ge(h, v))

    def waitt(self, engine, token):
        name, v = token
        h = self.sem(name)
        self.emit(engine, lambda e, h=h, v=v: e.wait_ge(h, v))

    def wait_all(self, engine, family):
        for i in range(NROT):
            name = f"{family}.{i}"
            if name in self.sems and self.counts[name] > 0:
                h = self.sems[name]
                self.emit(engine, lambda e, h=h, v=self.counts[name]: e.wait_ge(h, v))

    def build(self, blk):
        for name in self.ENGINES:
            stream = self.ops[name]
            ser_sem = self.sem("_ser_" + name) if name in ("vector", "scalar") else None

            def make(stream, ser_sem):
                def run(eng):
                    e = SerEng(eng, ser_sem) if ser_sem is not None else eng
                    for fn in stream:
                        fn(e)
                return run
            getattr(blk, name)(make(stream, ser_sem))


_EMITTING = {
    "tensor_copy", "tensor_tensor", "tensor_add", "tensor_sub", "tensor_mul",
    "tensor_max", "tensor_scalar", "tensor_single_scalar", "tensor_scalar_add",
    "tensor_scalar_sub", "tensor_scalar_mul", "tensor_scalar_max",
    "tensor_scalar_min", "tensor_relu", "tensor_reduce", "reduce_max",
    "reduce_sum", "scalar_tensor_tensor", "tensor_tensor_scan", "memset",
    "memzero", "select", "copy_predicated", "transpose", "bn_stats", "bn_aggr",
    "max", "max_index", "pool", "pool_avg", "pool_max", "reciprocal",
    "stream_shuffle", "match_replace", "activation", "add", "mul", "sqrt",
    "square", "sign", "copy", "iota", "affine_select",
}


class SerEng:

    def __init__(self, eng, sem):
        self._eng = eng
        self._sem = sem
        self._n = 0
        self._waited = 0
        self._pend = []

    @staticmethod
    def _span(a):
        try:
            lst = a.ap
            if callable(lst):
                return None
            off = a.offset
            esz = mybir.dt.size(a.dtype)
            lo = off
            hi = off
            for st, cnt in lst[1:]:
                d = st * (cnt - 1)
                if d < 0:
                    lo += d
                else:
                    hi += d
            return (a.tensor.name, lo * esz, (hi + 1) * esz)
        except Exception:
            return None

    def _need(self, rds, wrs):
        need = 0
        for (nm, lo, hi) in wrs:
            for (pn, pl, ph, pw, seq) in self._pend:
                if pn == nm and lo < ph and pl < hi and seq > need:
                    need = seq
        for (nm, lo, hi) in rds:
            for (pn, pl, ph, pw, seq) in self._pend:
                if pw and pn == nm and lo < ph and pl < hi and seq > need:
                    need = seq
        return need

    def _wait_to(self, need):
        if need > self._waited:
            self._eng.wait_ge(self._sem, need)
            self._waited = need
            self._pend = [e for e in self._pend if e[4] > need]

    def flush(self):
        self._wait_to(self._n)

    def __getattr__(self, name):
        f = getattr(self._eng, name)
        if name not in _EMITTING:
            return f

        def wrapped(*a, **k):
            aps = []
            bad = False
            for x in a:
                if hasattr(x, "tensor"):
                    s = self._span(x)
                    if s is None:
                        bad = True
                    aps.append(s)
                else:
                    aps.append(None) if False else None
            kw_aps = []
            for kk, x in k.items():
                if hasattr(x, "tensor"):
                    s = self._span(x)
                    if s is None:
                        bad = True
                    kw_aps.append((kk, s))
            ap_pos = [s for s in aps if s is not None]
            if bad:
                self.flush()
                wrs, rds = [], []
            else:
                wrs = ap_pos[:1] + [s for kk, s in kw_aps if kk == "accum_out" and s]
                rds = ap_pos[1:] + [s for kk, s in kw_aps if kk != "accum_out" and s]
                self._wait_to(self._need(rds, wrs))
            inst = f(*a, **k)
            self._n += 1
            inst.then_inc(self._sem, 1)
            for s in wrs:
                self._pend.append((s[0], s[1], s[2], True, self._n))
            for s in rds:
                self._pend.append((s[0], s[1], s[2], False, self._n))
            if bad:
                self.flush()
            return inst
        return wrapped


NROT = 4


def _dma_common(p, engine, family, emit_fn):
    i = p._rot.get(family, 0)
    p._rot[family] = i + 1
    name = f"{family}.{i % NROT}"
    h, v = p.inc(name, 16)
    if v > 16:
        p.emit(engine, lambda e, h=h, w=v - 16: e.wait_ge(h, w))
    p.emit(engine, lambda e, h=h: emit_fn(e).then_inc(h, 16))
    return (name, v)


def dma(p, engine, out_ap, in_ap, family):
    return _dma_common(p, engine, family,
                       lambda e, o=out_ap, i=in_ap: e.dma_start(out=o, in_=i))


def dma_t(p, engine, out_ap, in_ap, family):
    return _dma_common(p, engine, family,
                       lambda e, o=out_ap, i=in_ap: e.dma_start_transpose(o, i))


def dma_ind(p, engine, out_ap, src_ap, idx_ap, family):
    return _dma_common(p, engine, family,
                       lambda e, o=out_ap, s=src_ap, x=idx_ap: e.indirect_dma_start(
                           o, None, s, IndirectOffsetOnAxis(x, 0)))


def step(p, engine, sem_name, fn, by=1):
    h, v = p.inc(sem_name, by)
    if engine in ("vector", "scalar"):
        def g(e, h=h, by=by, fn=fn):
            fn(e)
            e.flush()
            e.sem_inc(h, by)
        p.emit(engine, g)
    else:
        p.emit(engine, lambda e, h=h, fn=fn, by=by: fn(e).then_inc(h, by))
    return v


def host_constants():
    c = {}
    inv = 10000.0 ** (-np.arange(0, D_HR, 2, dtype=np.float64) / D_HR)
    ang = np.arange(S, dtype=np.float64)[:, None] * inv[None, :]
    cos = np.cos(ang).T.astype(np.float32)
    sin = np.sin(ang).T.astype(np.float32)
    cos32 = np.concatenate([cos, cos], 0)
    sin32 = np.concatenate([-sin, sin], 0)
    c["trig_c"] = np.tile(cos32, (4, 1)).astype(bf)
    c["trig_s"] = np.tile(sin32, (4, 1)).astype(bf)
    q = np.arange(SQ)[None, :]
    k = np.arange(128)[:, None]
    for j in range(4):
        c[f"dmask{j}"] = (q >= (k + 128 * j)).astype(bf)
    oh = np.zeros((16, 1024), bf)
    for h in range(16):
        oh[h, 64 * h:64 * (h + 1)] = 1
    c["onehot16"] = oh
    c["ident128"] = np.eye(128, dtype=bf)
    jt = np.zeros((16, NTOK), bf)
    for t in range(NTOK):
        jt[:, t] = (t // 128) % 16
    c["jtile"] = jt
    return c


def permute_qr_cols(w):
    nh = w.shape[1] // D_HR
    w = w.reshape(w.shape[0], nh, D_HR)
    perm = np.concatenate([np.arange(0, D_HR, 2), np.arange(1, D_HR, 2)])
    return np.ascontiguousarray(w[:, :, perm].reshape(w.shape[0], nh * D_HR))


def build(nc):
    p = Prog(nc)

    x_own = nc.dram_tensor("x_own", [2, S, D_MODEL], F32, kind="ExternalInput")
    x_pos = nc.dram_tensor("x_pos", [16, NBLK, 128, D_MODEL], F32, kind="ExternalInput")
    out_own = nc.dram_tensor("out_own", [2, S, D_MODEL], F32, kind="ExternalOutput")

    w_shapes = {"W_DQ": (D_MODEL, D_LAT), "W_UQP": (D_LAT, 1568),
                "W_DKV": (D_MODEL, D_LAT), "W_UKP": (D_LAT, 1088),
                "W_UV": (D_LAT, 1024),
                "W_KRP": (D_MODEL, 128), "W_O": (1024, D_MODEL),
                "Wg": (D_MODEL, 16), "Wq_r": (D_MODEL, 1024), "Wkv_r": (D_MODEL, 2048),
                "Wo_f": (16 * 64, 1024),
                "bq_td": (64, 16), "bkv_td": (128, 16),
                "g_rep": (128, 1024), "b_rep": (128, 1024), "bo_row": (1, 1024)}
    wd = {n: nc.dram_tensor(n, list(sh), F32, kind="ExternalInput")
          for n, sh in w_shapes.items()}
    cst = host_constants()
    cd = {k: nc.dram_tensor("c_" + k, list(v.shape), BF16, kind="ExternalInput")
          for k, v in cst.items()}
    base_d = nc.dram_tensor("c_base", [128, NBLK], F32, kind="ExternalInput")
    jcol_d = nc.dram_tensor("c_jcol", [16, 2], F32, kind="ExternalInput")

    xn_dram = nc.dram_tensor("xn_dram", [S, D_MODEL], BF16)
    xnF_dram = nc.dram_tensor("xnF_dram", [NTOK, D_MODEL], BF16)
    xnlo_dram = nc.dram_tensor("xnlo_dram", [NTOK, D_MODEL], BF16)
    lnz_dram = nc.dram_tensor("lnz_dram", [1, NTOK], F32)
    hg_dram = nc.dram_tensor("hg_dram", [NTOK, D_MODEL], BF16)
    idx_sc = nc.dram_tensor("idx_sc", [16, NPOS], BF16)
    attn_dram = nc.dram_tensor("attn_dram", [S, D_MODEL], F32)
    eq_dram = nc.dram_tensor("eq_dram", [16, S], BF16)
    rec_dram = nc.dram_tensor("rec_dram", [16, S], BF16)
    rsum_dram = nc.dram_tensor("rsum_dram", [16, S], BF16)
    ag1_in = nc.dram_tensor("ag1_in", [32, NPOS], F32)
    ag1_out = nc.dram_tensor("ag1_out", [NC * 32, NPOS], F32)
    ag2_in = nc.dram_tensor("ag2_in", [64, NTOK], BF16)
    ag2_out = nc.dram_tensor("ag2_out", [NC * 64, NTOK], BF16)

    ctx = {}

    def T(name, shape, dtype=BF16):
        h = nc.sbuf_tensor(name, shape, dtype).__enter__()
        ctx[name] = h
        return h

    for c in range(NCH):
        T(f"w_dq{c}", [128, D_LAT]); T(f"w_dkv{c}", [128, D_LAT])
    for l in range(2):
        T(f"w_qp{l}", [128, 1568])
        T(f"w_uk{l}", [128, 1088]); T(f"w_uv{l}", [128, 1024])
    T("wg_all", [128, 8 * 16])
    T("onesf", [16, 1], F32)
    for c in range(NCH):
        ctx[f"wg{c}"] = ctx["wg_all"].ap()[:, 16 * c:16 * (c + 1)]
    for j in range(4):
        T(f"dmask{j}", [128, SQ])
    T("onehot16", [16, 1024])
    T("onesc", [64, 1]); T("onesr", [1, 64])
    T("epsc", [128, 1], F32)
    T("base", [128, NBLK], F32)
    T("jcol", [16, 2], F32)
    T("bq_t", [64, 16], F32); T("bkv_t", [128, 16], F32)
    T("bo_sb", [1, 1024])
    T("ssum", [128, 1], F32); T("ssq", [128, 1], F32)
    T("tvar", [128, 1], F32); T("tln", [128, 1], F32); T("tmsq", [128, 1], F32)

    for b in range(2):
        T(f"stat_m{b}", [128, 16], F32); T(f"stat_r{b}", [128, 16], F32)
    T("mx", [16, NPOS], F32)
    T("idxf", [16, NPOS], F32)
    for b in range(NBLK):
        T(f"idxT{b}", [128, 16])
        T(f"rt{b}", [128, 16], F32)
        T(f"rtu{b}", [128, 16], U32)
    for par in range(2):
        rt = ctx[f"rt{par}"]
        ctx[f"ssum{par}"] = rt.ap()[:, 5:6]
        ctx[f"ssq{par}"] = rt.ap()[:, 6:7]
        ctx[f"tvar{par}"] = rt.ap()[:, 7:8]
        ctx[f"tln{par}"] = rt.ap()[:, 8:9]
        ctx[f"tmsq{par}"] = rt.ap()[:, 9:10]
        ctx[f"bias2{par}"] = rt.ap()[:, 10:11]
        ctx[f"lssum{par}"] = rt.ap()[:, 11:12]
        ctx[f"lssq{par}"] = rt.ap()[:, 12:13]
        ctx[f"ltvar{par}"] = rt.ap()[:, 13:14]
        ctx[f"ltln{par}"] = rt.ap()[:, 14:15]
        ctx[f"ltmsq{par}"] = rt.ap()[:, 15:16]
    ctx["bmr"] = ctx["rtu0"].ap().bitcast(F32)

    T("w16", [16, NPOS], F32)
    T("w16b", [16, NPOS])
    T("zf", [16, 1], F32)
    T("bias2", [128, 1], F32)

    T("AR1", [128, 8192])
    ctx["xa0"] = ctx["AR1"].ap().bitcast(F32)[:, 0:1024]
    ctx["xa1"] = ctx["AR1"].ap().bitcast(F32)[:, 1024:2048]
    ctx["zrow"] = ctx["AR1"].ap().bitcast(F32)[0:1, 0:4096]
    ctx["ot0"] = ctx["AR1"].ap().bitcast(F32)[:, 1024:2048]

    T("AR2", [128, 16384])
    ctx["lg"] = ctx["AR2"].ap().bitcast(F32)[0:16, 0:4096]
    ctx["M"] = ctx["lg"]
    ctx["egf"] = ctx["AR2"].ap().bitcast(F32)[0:16, 4096:8192]
    ctx["lnzbc"] = ctx["egf"]
    ctx["xnf32"] = ctx["AR2"].ap().bitcast(F32)[:, 0:1024]
    ctx["eg"] = ctx["AR2"].ap()[0:16, 8192:12288]
    ctx["jtile"] = ctx["AR2"].ap()[0:16, 12288:16384]
    for c in range(NCH):
        ctx[f"wkv{c}"] = ctx["AR2"].ap()[:, 2048 * c:2048 * (c + 1)]
    for h in range(16):
        ctx[f"w_o{h}"] = ctx["AR2"].ap()[:, 1024 * h:1024 * (h + 1)]

    T("AR3", [128, 16384])
    ctx["eq"] = ctx["AR3"].ap()[0:64, 0:4096]
    ctx["cand"] = ctx["AR3"].ap()[0:64, 4096:8192]
    ctx["qall"] = ctx["cand"]
    ctx["tt"] = ctx["eq"]
    ctx["attG"] = ctx["AR3"].ap().bitcast(F32)[0:16, 2048:4096]
    ctx["kvall"] = ctx["AR3"].ap()[:, 8192:12288]
    ctx["hgt0"] = ctx["AR3"].ap()[:, 12288:13312]
    ctx["hgt1"] = ctx["AR3"].ap()[:, 13312:14336]
    for hh in range(NCH):
        ctx[f"khead{hh}"] = ctx["AR3"].ap()[:, S * hh:S * (hh + 1)]
    ctx["avh"] = ctx["AR3"].ap()[0:64, 0:2048]
    ctx["avh_hi"] = ctx["AR3"].ap()[64:128, 0:2048]
    ctx["eq64"] = ctx["AR3"].ap()[0:64, 2048:4096]
    ctx["eq64_hi"] = ctx["AR3"].ap()[64:128, 2048:4096]
    ctx["idxG"] = ctx["AR3"].ap().bitcast(F32)[0:16, 2048:4096]
    ctx["nsel"] = ctx["AR3"].ap().bitcast(F32)[0:1, 4096:6144]
    ctx["nselb"] = ctx["AR3"].ap()[0:1, 12288:14336]
    ctx["xq_e"] = ctx["AR3"].ap()[:, 0:1024]
    ctx["xt0"] = ctx["AR2"].ap().bitcast(F32)[:, 0:1024]
    ctx["xt1"] = ctx["AR2"].ap().bitcast(F32)[:, 1024:2048]
    ctx["at0"] = ctx["AR2"].ap().bitcast(F32)[:, 2048:3072]
    ctx["at1"] = ctx["AR2"].ap().bitcast(F32)[:, 3072:4096]
    ctx["yt1"] = ctx["AR2"].ap().bitcast(F32)[:, 4096:5120]

    for c in range(NCH):
        T(f"xnT{c}", [128, 2 * S])
    for h2 in range(2):
        ctx[f"cqT{h2}"] = ctx[f"xnT{4 + h2}"].ap()[:, S:2 * S]
        ctx[f"ckvT{h2}"] = ctx[f"xnT{6 + h2}"].ap()[:, S:2 * S]
    for hh in range(NCH):
        ctx[f"qhead{hh}"] = ctx[f"xnT{hh}"].ap()[:, 0:S]
    for c in range(NCH):
        ctx[f"avmP{c}"] = ctx[f"xnT{c}"].ap()[:, 0:S]
        ctx[f"wo_p{c}"] = ctx["AR1"].ap()[:, 1024 * c:1024 * (c + 1)]

    T("attnU_t", [128, 2048])
    ctx["wgf"] = ctx["attnU_t"].ap().bitcast(F32)[:, 0:128]
    ctx["ident128"] = ctx["attnU_t"].ap()[:, 1024:1152]
    ctx["wglo_all"] = ctx["attnU_t"].ap()[:, 512:640]
    for c in range(NCH):
        ctx[f"wglo{c}"] = ctx["wglo_all"][:, 16 * c:16 * (c + 1)]
    for c in range(NCH):
        ctx[f"w_krp{c}"] = ctx["attnU_t"].ap()[:, 128 * c:128 * (c + 1)]
    for hh in range(4):
        ctx[f"attnU{hh}"] = ctx[f"xnT{hh}"].ap()[0:65, S:2 * S]
        ctx[f"attnU128_{hh}"] = ctx[f"xnT{hh}"].ap()[:, S:2 * S]
    ctx["attnU4"] = ctx["AR1"].ap()[0:65, 4096:6144]
    ctx["attnU128_4"] = ctx["AR1"].ap()[:, 4096:6144]
    ctx["attnU5"] = ctx["AR1"].ap()[0:65, 6144:8192]
    ctx["attnU128_5"] = ctx["AR1"].ap()[:, 6144:8192]
    ctx["attnU6"] = ctx["attnU_t"].ap()[0:65, 0:S]
    ctx["attnU128_6"] = ctx["attnU_t"].ap()[:, 0:S]
    ctx["attnU7"] = ctx["AR1"].ap()[0:65, 0:2048]
    ctx["attnU128_7"] = ctx["AR1"].ap()[:, 0:2048]

    T("vh_t", [128, 16 * 520 + 64])
    for kt in range(NKT):
        ctx[f"vh{kt}"] = ctx["vh_t"].ap()[:, 520 * kt:520 * (kt + 1)]
    ctx["g_rep"] = ctx["vh_t"].ap().bitcast(F32)[:, 0:1024]
    ctx["b_rep"] = ctx["vh_t"].ap().bitcast(F32)[:, 1024:2048]
    for c in range(NCH):
        ctx[f"wq{c}"] = ctx["vh_t"].ap()[:, 1024 * c:1024 * (c + 1)]
    ctx["krSt"] = ctx["AR3"].ap()[64:96, 0:2048]
    ctx["krPr"] = ctx["AR3"].ap()[64:96, 2048:4096]
    ctx["pQ0"] = ctx["vh_t"].ap()[64:96, 2048:4096]
    ctx["pQ1"] = ctx["vh_t"].ap()[64:96, 4096:6144]
    ctx["pQ2"] = ctx["vh_t"].ap()[64:96, 6144:8192]
    ctx["pQ3"] = ctx["vh_t"].ap()[64:96, 0:2048]

    T("P0", [128, 4 * SQ]); T("P1", [128, 4 * SQ])
    ctx["trig_c"] = ctx["P0"]; ctx["trig_s"] = ctx["P1"]
    ctx["xq0"] = ctx["P0"].ap()[:, 0:1024]
    ctx["xq1"] = ctx["P1"].ap()[:, 1024:2048]
    ctx["xn0"] = ctx["P0"].ap()[:, 1024:2048]
    ctx["xn1"] = ctx["P1"].ap()[:, 0:1024]
    ctx["xnlo"] = ctx["P1"].ap()[:, 1024:2048]
    ctx["ytile"] = ctx["P0"].ap().bitcast(F32)[:, 0:1024]
    ctx["xtile"] = ctx["P1"].ap().bitcast(F32)[:, 0:1024]
    ctx["ot1"] = ctx["P0"].ap().bitcast(F32)[:, 0:1024]
    ctx["rsumS"] = ctx["P0"].ap()[0:16, 0:2048]
    ctx["rsumS8"] = ctx["P0"].ap()[0:8, 0:2048]
    ctx["rec16n"] = ctx["P1"].ap()[0:16, 0:2048]
    ctx["rec8n"] = ctx["P1"].ap()[0:8, 0:2048]

    ps_big = nc.psum_tensor([128, 2048], F32).__enter__()
    ps_a = nc.psum_tensor([128, 512], F32).__enter__()
    ps_pv = nc.psum_tensor([128, 512], F32).__enter__()
    ps_wo = nc.psum_tensor([128, 1024], F32).__enter__()
    ctx.update(ps_big=ps_big, ps_a=ps_a, ps_pv=ps_pv, ps_wo=ps_wo)
    banks = [ps_a, ps_pv]
    nb = [0]

    for c in range(NCH):
        dma(p, "gpsimd", ctx[f"w_dq{c}"][:, :], wd["W_DQ"][128*c:128*(c+1), :], "dw")
        dma(p, "gpsimd", ctx[f"w_dkv{c}"][:, :], wd["W_DKV"][128*c:128*(c+1), :], "dw")
        dma(p, "gpsimd", ctx["wgf"][:, 16*c:16*(c+1)], wd["Wg"][128*c:128*(c+1), :], "dw")
    for l in range(2):
        dma(p, "gpsimd", ctx[f"w_qp{l}"][:, :], wd["W_UQP"][128*l:128*(l+1), :], "dw")
        dma(p, "gpsimd", ctx[f"w_uk{l}"][:, :], wd["W_UKP"][128*l:128*(l+1), :], "dw")
        dma(p, "gpsimd", ctx[f"w_uv{l}"][:, :], wd["W_UV"][128*l:128*(l+1), :], "dw")
    for c in range(NCH):
        dma(p, "gpsimd", ctx[f"wq{c}"], wd["Wq_r"][128*c:128*(c+1), :], "dw")
    for c in range(2, NCH):
        dma(p, "gpsimd", ctx[f"wkv{c}"][16:128, :],
            wd["Wkv_r"][128*c+16:128*(c+1), :], "dw")
    for k in ("dmask0", "dmask1", "dmask2", "dmask3", "onehot16"):
        dma(p, "gpsimd", ctx[k][:, :], cd[k].ap(), "dw")
    dma(p, "gpsimd", ctx["base"][:, :], base_d.ap(), "dw")
    dma(p, "gpsimd", ctx["jcol"][:, :], jcol_d.ap(), "dw")
    dma(p, "gpsimd", ctx["bq_t"][:, :], wd["bq_td"].ap(), "dw")
    dma(p, "gpsimd", ctx["bkv_t"][:, :], wd["bkv_td"].ap(), "dw")
    dma(p, "gpsimd", ctx["bo_sb"][:, :], wd["bo_row"].ap(), "dw")

    def init0(v):
        v.memset(ctx["epsc"][:, :], EPS)
        v.memset(ctx["onesc"][:, :], 1.0)
        v.memset(ctx["onesr"][:, :], 1.0)
        v.memset(ctx["onesf"][:, :], 1.0)
        return v.memset(ctx["vh_t"][:, 16 * 520:16 * 520 + 64], 0.0)
    step(p, "vector", "ve", init0)
    p.wait_all("vector", "dw")

    def wgsplit(v):
        v.tensor_copy(ctx["wg_all"][:, :], ctx["wgf"][:, :])
        return v.tensor_sub(ctx["wglo_all"][:, :], ctx["wgf"][:, :],
                            ctx["wg_all"][:, :])
    step(p, "vector", "ve", wgsplit)

    def ln_tile(xa, xn_out, m_ap=None, r_ap=None, par=0, pfx="", xq=None):
        sm = ctx[f"{pfx}ssum{par}"]
        sq = ctx[f"{pfx}ssq{par}"]
        tv = ctx[f"{pfx}tvar{par}"]
        tl = ctx[f"{pfx}tln{par}"]
        tm = ctx[f"{pfx}tmsq{par}"]
        if xq is None:
            xq = ctx[f"xq{par}"]
        step(p, "vector", "ve", lambda v, xa=xa, sm=sm: v.reduce_sum(sm, xa, AX.X))
        v_sq = step(p, "scalar", "se", lambda s, xa=xa, sq=sq, xq=xq: s.activation(
            xq, xa, AF.Square, accum_out=sq))
        p.wait("vector", "se", v_sq)
        if m_ap is None:
            m_ap = tv

        def stats(v, m_ap=m_ap, sm=sm, sq=sq, tm=tm):
            v.tensor_scalar_mul(m_ap, sm, 1.0 / 1024)
            v.tensor_mul(tm, m_ap, m_ap)
            return v.scalar_tensor_tensor(tm, sq, 1.0 / 1024, tm,
                                          A.mult, A.subtract)
        v_st = step(p, "vector", "ve", stats)
        p.wait("scalar", "ve", v_st)
        if r_ap is None:
            r_ap = tl

        def rstd(s, r_ap=r_ap, tm=tm, tl=tl):
            s.activation(tl, tm, AF.Ln, bias=ctx["epsc"][:, :])
            return s.activation(r_ap, tl, AF.Exp, scale=-0.5)
        v_rs = step(p, "scalar", "se", rstd)
        p.wait("vector", "se", v_rs)
        return step(p, "vector", "ve", lambda v, xa=xa, xn=xn_out, m=m_ap, r=r_ap:
                    v.tensor_scalar(xn, xa, m, r, A.subtract, A.mult))

    st1_toks, st2_toks = {}, {}
    for t in range(32):
        blk, j = t // 16, t % 16
        par = t % 2
        xa = ctx["xa0"] if par == 0 else ctx["xa1"]
        xn = ctx["xn0"] if par == 0 else ctx["xn1"]
        xnf = (ctx["AR2"].ap().bitcast(F32)[:, 0:1024] if par == 0
               else ctx["AR2"].ap().bitcast(F32)[:, 1024:2048])
        xnlo = (ctx["AR3"].ap()[:, 0:1024] if par == 0
                else ctx["AR3"].ap()[:, 1024:2048])
        if t >= 2:
            p.waitt("sync", st1_toks[t - 2])
            p.waitt("sync", st2_toks[t - 2])
        v_load = dma(p, "sync", xa, x_pos[j, blk, :, :], "dx")
        p.waitt("vector", v_load)
        p.waitt("scalar", v_load)
        v_no = ln_tile(xa, xnf, par=par)

        def split(v, xn=xn, xnf=xnf, xnlo=xnlo):
            v.tensor_copy(xn, xnf)
            return v.tensor_sub(xnlo, xnf, xn)
        v_sp = step(p, "vector", "ve", split)
        p.wait("sync", "ve", v_sp)
        st1_toks[t] = dma(p, "sync", xnF_dram[128*t:128*(t+1), :], xn, "dx")
        st2_toks[t] = dma(p, "sync", xnlo_dram[128*t:128*(t+1), :], xnlo, "dx")
    p.wait_all("sync", "dx")
    for c in range(NCH):
        dma_t(p, "sync", ctx[f"xnT{c}"][:, 0:NTOK], xnF_dram[:, 128*c:128*(c+1)], "dx")
    p.wait("gpsimd", "ve", p.cur("ve"))
    p.wait("gpsimd", "se", p.cur("se"))
    for c in range(2):
        dma(p, "gpsimd", ctx[f"wkv{c}"][16:128, :],
            wd["Wkv_r"][128*c+16:128*(c+1), :], "dw")
    p.wait_all("tensor", "dx")
    p.wait_all("tensor", "dw")
    p.wait("tensor", "ve", p.cur("ve"))

    f1_tok = {0: p.cur("ve"), 1: p.cur("ve")}
    for sl in range(NTOK // 512):
        bi = nb[0] % 2; psum = banks[bi]; nb[0] += 1
        p.wait("tensor", "ve", f1_tok[bi])

        def mm(te, sl=sl, psum=psum):
            r = None
            for c in range(NCH):
                te.matmul(psum[0:16, :], ctx[f"wg{c}"],
                          ctx[f"xnT{c}"][:, 512*sl:512*(sl+1)],
                          start=(c == 0), stop=False)
                r = te.matmul(psum[0:16, :], ctx[f"wglo{c}"],
                              ctx[f"xnT{c}"][:, 512*sl:512*(sl+1)],
                              start=False, stop=(c == NCH - 1))
            return r
        v_mm = step(p, "tensor", "pe", mm)
        p.wait("vector", "pe", v_mm)
        f1_tok[bi] = step(p, "vector", "ve", lambda v, sl=sl, psum=psum: v.tensor_copy(
            ctx["lg"][:, 512*sl:512*(sl+1)], psum[0:16, :]))
    p.wait("sync", "pe", p.cur("pe"))
    for c in range(NCH):
        dma_t(p, "sync", ctx[f"xnT{c}"][:, 0:NTOK], xnlo_dram[:, 128*c:128*(c+1)], "dx")
    p.wait_all("tensor", "dx")
    for sl in range(NTOK // 512):
        bi = nb[0] % 2; psum = banks[bi]; nb[0] += 1
        p.wait("tensor", "ve", f1_tok[bi])

        def mm(te, sl=sl, psum=psum):
            r = None
            for c in range(NCH):
                r = te.matmul(psum[0:16, :], ctx[f"wg{c}"],
                              ctx[f"xnT{c}"][:, 512*sl:512*(sl+1)],
                              start=(c == 0), stop=(c == NCH - 1))
            return r
        v_mm = step(p, "tensor", "pe", mm)
        p.wait("vector", "pe", v_mm)
        f1_tok[bi] = step(p, "vector", "ve", lambda v, sl=sl, psum=psum: v.tensor_add(
            ctx["lg"][:, 512*sl:512*(sl+1)], ctx["lg"][:, 512*sl:512*(sl+1)],
            psum[0:16, :]))
    p.wait("sync", "pe", p.cur("pe"))
    v_lg = p.cur("ve")
    p.wait("scalar", "ve", v_lg)
    v_eg = step(p, "scalar", "se", lambda s: s.activation(ctx["egf"], ctx["lg"], AF.Exp))
    p.wait("tensor", "se", v_eg)
    for sl in range(NTOK // 512):
        bi = nb[0] % 2; psum = banks[bi]; nb[0] += 1
        p.wait("tensor", "ve", f1_tok[bi])

        def mm(te, sl=sl, psum=psum):
            return te.matmul(psum[0:1, :], ctx["onesf"][:, :],
                             ctx["egf"][:, 512*sl:512*(sl+1)], start=True, stop=True)
        v_mm = step(p, "tensor", "pe", mm)
        p.wait("vector", "pe", v_mm)
        f1_tok[bi] = step(p, "vector", "ve", lambda v, sl=sl, psum=psum: v.tensor_copy(
            ctx["zrow"][:, 512*sl:512*(sl+1)], psum[0:1, :]))
    v_z = p.cur("ve")
    p.wait("scalar", "ve", v_z)
    v_lz = step(p, "scalar", "se", lambda s: s.activation(
        ctx["zrow"], ctx["zrow"], AF.Ln))
    p.wait("sync", "se", v_lz)
    dma(p, "sync", lnz_dram.ap(), ctx["zrow"], "dx")
    p.wait_all("sync", "dx")
    v_zb = dma(p, "sync", ctx["lnzbc"], lnz_dram.ap().partition_broadcast(16), "dx")
    p.waitt("vector", v_zb)
    v_M = step(p, "vector", "ve", lambda v: v.tensor_sub(ctx["M"], ctx["lg"],
                                                         ctx["lnzbc"]))
    p.wait("gpsimd", "ve", v_M)
    tok_jt = dma(p, "gpsimd", ctx["jtile"], cd["jtile"].ap(), "dw")
    p.waitt("vector", tok_jt)

    def red(v):
        ap = ctx["M"].rearrange("l (b j q) -> l b q j", b=NBLK, j=16)
        return v.tensor_reduce(ctx["mx"].ap().rearrange("l (b q) -> l b q", b=NBLK),
                               ap, AX.X, A.max)
    step(p, "vector", "ve", red)

    def eqf(v):
        apM = ctx["M"].rearrange("l (b j q) -> l b j q", b=NBLK, j=16)
        apmx = ctx["mx"].ap().rearrange("l (b q) -> l b () q", b=NBLK)
        apmx = apmx.broadcast_to([16, NBLK, 16, NPOS // NBLK])
        return v.tensor_tensor(
            ctx["eq"][0:16, :].rearrange("l (b j q) -> l b j q", b=NBLK, j=16),
            apM, apmx, A.is_ge)
    step(p, "vector", "ve", eqf)

    def candf(v):
        v.tensor_scalar(ctx["cand"][0:16, :], ctx["jtile"], JBIG, None, A.subtract)
        v.tensor_mul(ctx["cand"][0:16, :], ctx["cand"][0:16, :], ctx["eq"][0:16, :])
        return v.tensor_scalar(ctx["cand"][0:16, :], ctx["cand"][0:16, :],
                               JBIG, None, A.add)
    step(p, "vector", "ve", candf)

    def redmin(v):
        ap = ctx["cand"][0:16, :].rearrange("l (b j q) -> l b q j", b=NBLK, j=16)
        return v.tensor_reduce(ctx["idxf"].ap().rearrange("l (b q) -> l b q", b=NBLK),
                               ap, AX.X, A.min)
    step(p, "vector", "ve", redmin)
    step(p, "vector", "ve", lambda v: v.tensor_copy(ctx["eq"][0:16, 0:NPOS],
                                                    ctx["idxf"][:, :]))
    p.wait("sync", "ve", p.cur("ve"))
    dma(p, "sync", idx_sc.ap(), ctx["eq"][0:16, 0:NPOS], "dx")
    p.wait_all("sync", "dx")
    for b in range(NBLK):
        dma_t(p, "sync", ctx[f"idxT{b}"][:, 0:16], idx_sc[:, 128*b:128*(b+1)], "dx")
    p.wait_all("vector", "dx")
    for b in range(NBLK):
        def rtf(v, b=b):
            v.tensor_scalar(ctx[f"rt{b}"][:, :], ctx[f"idxT{b}"][:, 0:16],
                            128.0, None, A.mult)
            v.tensor_scalar_add(ctx[f"rt{b}"][:, :], ctx[f"rt{b}"][:, :],
                                ctx["base"][:, b:b+1])
            return v.tensor_copy(ctx[f"rtu{b}"][:, :], ctx[f"rt{b}"][:, :])
        step(p, "vector", "ve", rtf)
    v_rt = p.cur("ve")
    st = {}

    def emit_f1_tail(feed):
        p.wait("gpsimd", "ve", v_rt)
        t_id = dma(p, "gpsimd", ctx["ident128"], cd["ident128"].ap(), "dw")
        p.waitt("tensor", t_id)
        gbufs = ([ctx["AR3"].ap()[:, 12288 + 1024*i:12288 + 1024*(i+1)]
                  for i in range(4)] +
                 [ctx["AR3"].ap()[:, 4096 + 1024*i:4096 + 1024*(i+1)]
                  for i in range(4)])
        psb_slots = ([(ctx["ps_big"].ap().bitcast(BF16), 1024 * i)
                      for i in range(4)] +
                     [(ctx["ps_wo"].ap().bitcast(BF16), 1024 * i)
                      for i in range(2)])
        tp_toks = {}
        ev_toks = {}
        gi = 0
        for l in range(16):
            for b in range(NBLK):
                tok0 = l * NPOS + b * 128
                buf = gbufs[gi % 8]
                if gi >= 8:
                    p.wait("gpsimd", "pe", tp_toks[gi - 8])
                tg = dma_ind(p, "gpsimd", buf, xnF_dram.ap(),
                             ctx[f"rtu{b}"][:, l:l+1], f"dg{gi % 2}")
                p.waitt("tensor", tg)
                sb_, soff = psb_slots[gi % 6]
                if gi >= 6:
                    p.wait("tensor", "ve", ev_toks[gi - 6])

                def tps(te, buf=buf, sb_=sb_, soff=soff):
                    r = None
                    for f in range(8):
                        r = te.transpose(sb_[:, soff + 128*f:soff + 128*(f+1)],
                                         buf[:, 128*f:128*(f+1)],
                                         ctx["ident128"])
                    return r
                tp_toks[gi] = step(p, "tensor", "pe", tps)
                p.wait("vector", "pe", tp_toks[gi])

                def ev(v, tok0=tok0, sb_=sb_, soff=soff):
                    r = None
                    for c in range(NCH):
                        r = v.tensor_copy(ctx[f"xnT{c}"][:, tok0:tok0+128],
                                          sb_[:, soff + 128*c:soff + 128*(c+1)])
                    return r
                ev_toks[gi] = step(p, "vector", "ve", ev)
                if gi % 2 == 1:
                    feed()
                gi += 1
        while feed():
            pass
        for c in range(NCH):
            dma(p, "gpsimd", ctx[f"wkv{c}"][0:16, :],
                wd["Wkv_r"][128*c:128*c+16, :], "dw")
        p.wait_all("tensor", "dw")
        p.wait("tensor", "ve", p.cur("ve"))
        for l in range(16):
            bi = nb[0] % 2; psum = banks[bi]; nb[0] += 1
            p.wait("tensor", "ve", f1_tok[bi])

            def mmq(te, l=l, psum=psum):
                r = None
                for c in range(NCH):
                    r = te.matmul(psum[0:64, 0:NPOS],
                                  ctx[f"wq{c}"][:, 64*l:64*(l+1)],
                                  ctx[f"xnT{c}"][:, NPOS*l:NPOS*(l+1)],
                                  start=(c == 0), stop=(c == NCH - 1))
                return r
            v_mm = step(p, "tensor", "pe", mmq)
            p.wait("vector", "pe", v_mm)
            f1_tok[bi] = step(p, "vector", "ve",
                              lambda v, l=l, psum=psum: v.tensor_scalar_add(
                ctx["qall"][0:64, NPOS*l:NPOS*(l+1)], psum[0:64, 0:NPOS],
                ctx["bq_t"][:, l:l+1]))
            bi = nb[0] % 2; psum = banks[bi]; nb[0] += 1
            p.wait("tensor", "ve", f1_tok[bi])

            def mmkv(te, l=l, psum=psum):
                r = None
                for c in range(NCH):
                    r = te.matmul(psum[:, 0:NPOS],
                                  ctx[f"wkv{c}"][:, 128*l:128*(l+1)],
                                  ctx[f"xnT{c}"][:, NPOS*l:NPOS*(l+1)],
                                  start=(c == 0), stop=(c == NCH - 1))
                return r
            v_mm = step(p, "tensor", "pe", mmkv)
            p.wait("vector", "pe", v_mm)
            f1_tok[bi] = step(p, "vector", "ve",
                              lambda v, l=l, psum=psum: v.tensor_scalar_add(
                ctx["kvall"][:, NPOS*l:NPOS*(l+1)], psum[:, 0:NPOS],
                ctx["bkv_t"][:, l:l+1]))
        step(p, "vector", "ve", lambda v: v.tensor_mul(
            ctx["tt"][0:64, :], ctx["qall"][0:64, :], ctx["kvall"][0:64, :]))
        p.wait("tensor", "ve", p.cur("ve"))
        for sl in range(8):
            bi = nb[0] % 2; psum = banks[bi]; nb[0] += 1
            p.wait("tensor", "ve", f1_tok[bi])

            def mm(te, sl=sl, psum=psum):
                return te.matmul(psum[0:1, :], ctx["onesc"][0:64, :],
                                 ctx["tt"][0:64, 512*sl:512*(sl+1)],
                                 start=True, stop=True)
            v_mm = step(p, "tensor", "pe", mm)
            p.wait("vector", "pe", v_mm)
            f1_tok[bi] = step(p, "vector", "ve",
                              lambda v, sl=sl, psum=psum: v.tensor_copy(
                ctx["zrow"][:, 512*sl:512*(sl+1)], psum[0:1, :]))
        p.wait("sync", "ve", p.cur("ve"))
        dma(p, "sync", ag1_in.ap().rearrange("a q -> () (a q)")[:, 0:4096],
            ctx["zrow"], "dx")
        dma(p, "sync", ag1_in[16:32, :], ctx["idxf"][:, :], "dx")
        p.wait_all("gpsimd", "dx")

        h_cc, v_cc1 = p.inc("cc", 1)
        st["v_cc1"] = v_cc1
        p.emit("gpsimd", lambda e, h=h_cc: e.collective_compute(
            "AllGather", A.bypass, replica_groups=[list(range(NC))],
            ins=[ag1_in.ap().opt()], outs=[ag1_out.ap().opt()]).then_inc(h, 1))

    def emit_p1b():
        p.wait("gpsimd", "cc", st["v_cc1"])
        v_w16l = dma(p, "gpsimd", ctx["w16"][:, :], ag1_in[0:16, :], "dw")
        v_g1 = dma(p, "gpsimd", ctx["attG"].rearrange("l (s q) -> l s q", s=8),
                   ag1_out.ap().rearrange("(s r) q -> s r q", r=32)[:, 0:16, :]
                   .rearrange("s l q -> l s q"), "dw")
        p.waitt("vector", v_g1)
        step(p, "vector", "ve", lambda v: v.reduce_max(ctx["zf"][0:16, :],
                                                       ctx["attG"], AX.X))
        v_m = step(p, "vector", "ve", lambda v: v.tensor_scalar_mul(
            ctx["zf"][0:16, :], ctx["zf"][0:16, :], -0.125))
        p.wait("scalar", "ve", v_m)
        v_e = step(p, "scalar", "se", lambda s: s.activation(
            ctx["attG"], ctx["attG"], AF.Exp, bias=ctx["zf"][0:16, :], scale=0.125))
        p.wait("vector", "se", v_e)
        step(p, "vector", "ve", lambda v: v.reduce_sum(ctx["tvar"][0:16, :],
                                                       ctx["attG"], AX.X))
        v_rz = step(p, "vector", "ve", lambda v: v.reciprocal(
            ctx["tln"][0:16, :], ctx["tvar"][0:16, :]))
        p.wait("scalar", "ve", v_rz)
        p.waitt("scalar", v_w16l)
        v_e2 = step(p, "scalar", "se", lambda s: s.activation(
            ctx["w16"][:, :], ctx["w16"][:, :], AF.Exp, bias=ctx["zf"][0:16, :],
            scale=0.125))
        p.wait("vector", "se", v_e2)

        def normw(v):
            v.tensor_scalar_mul(ctx["w16"][:, :], ctx["w16"][:, :],
                                ctx["tln"][0:16, :])
            return v.tensor_copy(ctx["w16b"][:, :], ctx["w16"][:, :])
        v_w16 = step(p, "vector", "ve", normw)
        p.wait("tensor", "ve", v_w16)
        for l in range(16):
            psum = banks[nb[0] % 2]; nb[0] += 1
            p.wait("tensor", "ve", p.cur("ve"))

            def mmb(te, l=l, psum=psum):
                return te.matmul(psum[0:64, 0:NPOS],
                                 ctx["onehot16"][:, 64*l:64*(l+1)],
                                 ctx["w16b"][:, :], start=True, stop=True)
            v_mm = step(p, "tensor", "pe", mmb)
            p.wait("vector", "pe", v_mm)
            step(p, "vector", "ve", lambda v, l=l, psum=psum: v.tensor_mul(
                ctx["tt"][0:64, NPOS*l:NPOS*(l+1)],
                ctx["kvall"][64:128, NPOS*l:NPOS*(l+1)], psum[0:64, 0:NPOS]))
        v_av = p.cur("ve")
        p.wait("sync", "ve", v_av)
        dma(p, "sync", ag2_in.ap(), ctx["tt"][0:64, :], "dx")
        p.wait_all("gpsimd", "dx")
        h_cc, v_cc2 = p.inc("cc", 1)
        st["v_cc2"] = v_cc2
        p.emit("gpsimd", lambda e, h=h_cc: e.collective_compute(
            "AllGather", A.bypass, replica_groups=[list(range(NC))],
            ins=[ag2_in.ap().opt()], outs=[ag2_out.ap().opt()]).then_inc(h, 1))

    def make_ln_feeder(bt, alt=False):
        stat_m = ctx[f"stat_m{bt}"]
        stat_r = ctx[f"stat_r{bt}"]
        if alt:
            xas = [ctx["AR3"].ap().bitcast(F32)[:, 2048:3072],
                   ctx["AR3"].ap().bitcast(F32)[:, 3072:4096]]
            xns = [ctx["AR3"].ap()[:, 8192:9216], ctx["AR3"].ap()[:, 9216:10240]]
            xqs = [ctx["AR3"].ap()[:, 10240:11264], ctx["AR3"].ap()[:, 11264:12288]]
        else:
            xas = [ctx["xa0"], ctx["xa1"]]
            xns = [ctx["xn0"], ctx["xn1"]]
            xqs = [None, None]
        state = {"t": 0}
        lnst = {}
        st["lnst"] = lnst

        def feed():
            t = state["t"]
            if t >= 16:
                return False
            if t == 0:
                if not alt:
                    p.wait("scalar", "pe", p.cur("pe"))
                p.wait("scalar", "ve", p.cur("ve"))
                p.wait("sync", "ve", p.cur("ve"))
                p.wait_all("sync", "dx")
            xa = xas[t % 2]
            xn = xns[t % 2]
            if alt:
                eng, fam = "scalar", "ds"
            else:
                eng, fam = ("scalar", "ds") if t % 2 == 0 else ("sync", "dx")
            if t >= 2:
                p.waitt(eng, lnst[t - 2])
            v_load = dma(p, eng, xa, x_own[bt, 128*t:128*(t+1), :], fam)
            p.waitt("vector", v_load)
            p.waitt("scalar", v_load)
            v_no = ln_tile(xa, xn, m_ap=stat_m[:, t:t+1], r_ap=stat_r[:, t:t+1],
                           par=t % 2, pfx="l", xq=xqs[t % 2])
            p.wait("sync", "ve", v_no)
            lnst[t] = dma(p, "sync", xn_dram[128*t:128*(t+1), :], xn, "dx")
            state["t"] = t + 1
            return True
        return feed

    def emit_a_mid(bt, wo_hi_preloaded=False):
        p.wait("sync", "pe", p.cur("pe"))
        for tok in st.get("lnst", {}).values():
            p.waitt("sync", tok)
        for c in range(NCH):
            dma_t(p, "sync", ctx[f"xnT{c}"][:, 0:S], xn_dram[:, 128*c:128*(c+1)], "dx")
        p.wait_all("tensor", "dx")

        p.wait("gpsimd", "pe", p.cur("pe"))
        p.wait("gpsimd", "ve", p.cur("ve"))
        p.wait("gpsimd", "se", p.cur("se"))
        p.wait_all("gpsimd", "dx")
        for h in range(8):
            dma(p, "gpsimd", ctx[f"w_o{h}"][0:64, :], wd["W_O"][64*h:64*(h+1), :], "dw")
        p.wait_all("vector", "dw")

        def wo_zero(v):
            r = None
            for h in range(8):
                r = v.memset(ctx[f"w_o{h}"][64:128, :], 0.0)
            return r
        v_woz = step(p, "vector", "ve", wo_zero)

        def au_zero(v):
            r = None
            for hh in range(8):
                r = v.memset(ctx[f"attnU128_{hh}"][64:128, :], 0.0)
            return r
        v_auz = step(p, "vector", "ve", au_zero)

        p.wait("gpsimd", "ve", v_auz)
        krp_toks = [dma(p, "gpsimd", ctx[f"w_krp{c}"][:, :],
                        wd["W_KRP"][128*c:128*(c+1), :], "dw")
                    for c in range(NCH)]

        nbv = [0]
        bank_tok = {0: ("ve", p.cur("ve")), 1: ("ve", p.cur("ve"))}
        st["nbv"] = nbv; st["bank_tok"] = bank_tok

        def proj(dst_fn, w_fn, nchunks, rows=128, eng="vector"):
            bi = nbv[0] % 2; psum = banks[bi]; nbv[0] += 1
            sem, val = bank_tok[bi]
            p.wait("tensor", sem, val)

            def mms(te, psum=psum):
                r = None
                for i in range(nchunks):
                    wap, xap = w_fn(i)
                    r = te.matmul(psum[0:rows, :], wap, xap,
                                  start=(i == 0), stop=(i == nchunks - 1))
                return r
            v_mm = step(p, "tensor", "pe", mms)
            p.wait(eng, "pe", v_mm)
            sname = "ve" if eng == "vector" else "se"
            tok = step(p, eng, sname, lambda e, psum=psum: dst_fn(e, psum))
            bank_tok[bi] = (sname, tok)
        st["proj"] = proj

        for name, wpfx in (("cqT", "w_dq"), ("ckvT", "w_dkv")):
            for h2 in range(2):
                for qc in range(NQT):
                    def ev(v, psum, name=name, h2=h2, qc=qc):
                        return v.tensor_copy(ctx[name + str(h2)][:, SQ*qc:SQ*(qc+1)],
                                             psum[:, :])
                    proj(ev, lambda i, h2=h2, qc=qc, wpfx=wpfx: (
                        ctx[f"{wpfx}{i}"][:, 128*h2:128*(h2+1)],
                        ctx[f"xnT{i}"][:, SQ*qc:SQ*(qc+1)]), NCH)
        for t in krp_toks:
            p.waitt("tensor", t)
        for qc in range(NQT):
            def ev(v, psum, qc=qc):
                return v.tensor_copy(ctx["krSt"][:, SQ*qc:SQ*(qc+1)], psum[64:96, :])
            proj(ev, lambda i, qc=qc: (ctx[f"w_krp{i}"][:, :],
                                       ctx[f"xnT{i}"][:, SQ*qc:SQ*(qc+1)]), NCH)
        v_kr_done = p.cur("ve")
        p.wait_all("gpsimd", "dx")
        p.wait("gpsimd", "ve", v_kr_done)
        p.wait("gpsimd", "se", p.cur("se"))
        p.wait("gpsimd", "pe", p.cur("pe"))
        t_tc = dma(p, "gpsimd", ctx["trig_c"][:, :], cd["trig_c"].ap(), "dw")
        t_ts = dma(p, "gpsimd", ctx["trig_s"][:, :], cd["trig_s"].ap(), "dw")
        st["trig_toks"] = (t_tc, t_ts)
        t_k1 = dma(p, "gpsimd", ctx["krPr"][0:16, :], ctx["krSt"][16:32, :], "dw")
        t_k2 = dma(p, "gpsimd", ctx["krPr"][16:32, :], ctx["krSt"][0:16, :], "dw")
        for t in (t_tc, t_ts, t_k1, t_k2):
            p.waitt("vector", t)

        def rope_kr(v):
            v.tensor_mul(ctx["krSt"][:, :], ctx["krSt"][:, :],
                         ctx["trig_c"][64:96, :])
            v.tensor_mul(ctx["krPr"][:, :], ctx["krPr"][:, :],
                         ctx["trig_s"][64:96, :])
            return v.tensor_add(ctx["krSt"][:, :], ctx["krSt"][:, :],
                                ctx["krPr"][:, :])
        step(p, "vector", "ve", rope_kr)

        hi_end = 10 if wo_hi_preloaded else 16
        st["wo_hi_toks"] = [dma(p, "gpsimd", ctx[f"w_o{h}"][0:64, :],
                                wd["W_O"][64*h:64*(h+1), :], "dw")
                            for h in range(8, hi_end)]

    def emit_a_back(bt, tail_hook=None):
        proj = st["proj"]
        t_tc, t_ts = st["trig_toks"]
        bank_tok = st["bank_tok"]
        bank_tok[0] = bank_tok[1] = ("ve", p.cur("ve"))
        p.wait_all("vector", "dx")
        p.wait_all("scalar", "dx")

        def kr_fill(v):
            r = None
            for hh in range(NCH):
                if hh > 0:
                    v.tensor_copy(ctx[f"khead{hh}"][64:96, :], ctx["krSt"][:, :])
                r = v.memset(ctx[f"khead{hh}"][96:128, :], 0.0)
            return r
        step(p, "vector", "ve", kr_fill)

        for g in range(2):
            se_g0 = p.cur("se")
            if g == 1:
                for t in st.pop("wo_hi_toks", []) + st.pop("wo_pre_toks", []):
                    p.waitt("vector", t)

                def wo_zero_hi(v):
                    r = None
                    for h in range(8, 16):
                        r = v.memset(ctx[f"w_o{h}"][64:128, :], 0.0)
                    return r
                step(p, "vector", "ve", wo_zero_hi)
                p.wait_all("gpsimd", "dx")
                p.wait("gpsimd", "pe", p.cur("pe"))
                p.wait("gpsimd", "se", p.cur("se"))
                p.wait("gpsimd", "ve", p.cur("ve"))
                t_tc = dma(p, "gpsimd", ctx["trig_c"][:, :], cd["trig_c"].ap(), "dw")
                t_ts = dma(p, "gpsimd", ctx["trig_s"][:, :], cd["trig_s"].ap(), "dw")
            for hh in range(NCH):
                h = 8 * g + hh
                for qc in range(NQT):
                    def ev(s, psum, hh=hh, qc=qc):
                        return s.activation(
                            ctx[f"khead{hh}"][0:64, SQ*qc:SQ*(qc+1)],
                            psum[0:64, :], AF.Copy)
                    proj(ev, lambda l, h=h, qc=qc: (
                        ctx[f"w_uk{l}"][:, 64*h:64*h+128],
                        ctx[f"ckvT{l}"][:, SQ*qc:SQ*(qc+1)]), 2, eng="scalar")
            for hh in range(NCH):
                h = 8 * g + hh
                for qc in range(NQT):
                    def ev(v, psum, hh=hh, qc=qc):
                        return v.tensor_copy(
                            ctx[f"qhead{hh}"][0:96, SQ*qc:SQ*(qc+1)], psum[0:96, :])
                    proj(ev, lambda l, h=h, qc=qc: (
                        ctx[f"w_qp{l}"][:, 96*h:96*h+128],
                        ctx[f"cqT{l}"][:, SQ*qc:SQ*(qc+1)]), 2)
                v_ev4 = p.cur("ve")
                pq = ctx[f"pQ{hh % 4}"]
                p.wait("gpsimd", "ve", v_ev4)
                p.wait("gpsimd", "se", se_g0)
                t1 = dma(p, "gpsimd", pq[0:16, :],
                         ctx[f"qhead{hh}"][80:96, :], "dw")
                t2 = dma(p, "gpsimd", pq[16:32, :],
                         ctx[f"qhead{hh}"][64:80, :], "dw")
                for t in (t_tc, t_ts, t1, t2):
                    p.waitt("vector", t)

                def rope_q(v, hh=hh, pq=pq):
                    v.tensor_mul(ctx[f"qhead{hh}"][64:96, :],
                                 ctx[f"qhead{hh}"][64:96, :],
                                 ctx["trig_c"][64:96, :])
                    v.tensor_mul(pq[:, :], pq[:, :], ctx["trig_s"][64:96, :])
                    return v.tensor_add(ctx[f"qhead{hh}"][64:96, :],
                                        ctx[f"qhead{hh}"][64:96, :], pq[:, :])
                step(p, "vector", "ve", rope_q)
            p.wait("scalar", "ve", p.cur("ve"))
            for kt in range(NKT):
                def ev(s, psum, kt=kt):
                    dst = ctx[f"vh{kt}"].rearrange("k (h e) -> k h e", h=8)
                    src = psum.ap().rearrange("k (h e) -> k h e", h=8)
                    return s.activation(dst[:, :, 0:64], src[:, :, :], AF.Copy)
                proj(ev, lambda l, kt=kt, g=g: (
                    ctx[f"ckvT{l}"][:, 128*kt:128*(kt+1)],
                    ctx[f"w_uv{l}"][:, 512*g:512*(g+1)]), 2, eng="scalar")

            def vh_ones(v):
                r = None
                for kt in range(NKT):
                    dst = ctx[f"vh{kt}"].rearrange("k (h e) -> k h e", h=8)
                    r = v.memset(dst[:, :, 64:65], 1.0)
                return r
            step(p, "vector", "ve", vh_ones)
            v_rope_done = p.cur("ve")
            p.wait("tensor", "ve", v_rope_done)
            p.wait("tensor", "se", p.cur("se"))

            slots = [(ctx["ps_big"], 0), (ctx["ps_big"], 1024), (ctx["ps_wo"], 0)]
            psl = [(ctx["P0"], 0), (ctx["P0"], 1024), (ctx["P1"], 0), (ctx["P1"], 1024)]
            items = []
            for hh in range(8):
                for qt in range(NQT):
                    for k2 in range(2 * (qt + 1)):
                        items.append((hh, qt, k2))
            NI = len(items)
            v_sc = [0] * NI; v_ex = [0] * NI; v_cm = [0] * NI; v_pv = [0] * NI
            v_eu_prev = [p.cur("ve")]
            for i in range(NI + 1):
                if i < NI:
                    hh, qt, k2 = items[i]
                    kts = [2 * k2, 2 * k2 + 1]
                    sb, soff = slots[i % 3]
                    if i >= 3:
                        p.wait("tensor", "se", v_ex[i - 3])

                    def smms(te, kts=kts, hh=hh, qt=qt, sb=sb, soff=soff):
                        r = None
                        for t, kt in enumerate(kts):
                            r = te.matmul(sb[:, soff + SQ*t:soff + SQ*(t+1)],
                                          ctx[f"khead{hh}"][:, 128*kt:128*(kt+1)],
                                          ctx[f"qhead{hh}"][:, SQ*qt:SQ*(qt+1)],
                                          start=True, stop=True)
                        return r
                    v_sc[i] = step(p, "tensor", "pe", smms)
                    pb, poff = psl[i % 4]
                    p.wait("scalar", "pe", v_sc[i])
                    if i >= 4:
                        p.wait("scalar", "pe", v_pv[i - 4])

                    def expf(s, sb=sb, soff=soff, pb=pb, poff=poff):
                        return s.activation(pb[:, poff:poff + 1024],
                                            sb[:, soff:soff + 1024], AF.Exp,
                                            scale=INV_SQRT_DQK)
                    v_ex[i] = step(p, "scalar", "se", expf)
                    p.wait("vector", "se", v_ex[i])

                    def clipmask(v, pb=pb, poff=poff, kts=kts, qt=qt):
                        r = v.tensor_scalar(pb[:, poff:poff + 1024],
                                            pb[:, poff:poff + 1024],
                                            CLIP_HI, CLIP_LO, A.min, A.max)
                        for t, kt in enumerate(kts):
                            d = kt - 4 * qt
                            if 0 <= d < 4:
                                r = v.tensor_mul(pb[:, poff + SQ*t:poff + SQ*(t+1)],
                                                 pb[:, poff + SQ*t:poff + SQ*(t+1)],
                                                 ctx[f"dmask{d}"][:, :])
                        return r
                    v_cm[i] = step(p, "vector", "ve", clipmask)
                if i >= 1:
                    hh1, qt1, k21 = items[i - 1]
                    kts1 = [2 * k21, 2 * k21 + 1]
                    pb1, poff1 = psl[(i - 1) % 4]
                    p.wait("tensor", "ve", v_cm[i - 1])
                    if k21 == 0:
                        p.wait("tensor", "ve", v_eu_prev[0])

                    def pvmms(te, kts1=kts1, k21=k21, qt1=qt1, hh1=hh1,
                              pb1=pb1, poff1=poff1):
                        r = None
                        for t, kt in enumerate(kts1):
                            vsl = ctx["vh_t"].ap()[:, 520*kt + 65*hh1:
                                                   520*kt + 65*hh1 + 128]
                            r = te.matmul(ctx["ps_pv"][:, :], vsl,
                                          pb1[:, poff1 + SQ*t:poff1 + SQ*(t+1)],
                                          start=(k21 == 0 and t == 0),
                                          stop=(k21 == 2 * (qt1 + 1) - 1 and t == 1))
                        return r
                    v_pv[i - 1] = step(p, "tensor", "pe", pvmms)
                    if k21 == 2 * (qt1 + 1) - 1:
                        p.wait("vector", "pe", v_pv[i - 1])
                        v_eu = step(p, "vector", "ve",
                                    lambda v, hh1=hh1, qt1=qt1: v.tensor_copy(
                                        ctx[f"attnU{hh1}"][:, SQ*qt1:SQ*(qt1+1)],
                                        ctx["ps_pv"][0:65, :]))
                        v_eu_prev[0] = v_eu
                        if qt1 == NQT - 1:
                            h = 8 * g + hh1
                            p.wait("gpsimd", "ve", v_eu)
                            dma(p, "gpsimd", rsum_dram[h:h+1, :],
                                ctx[f"attnU{hh1}"][64:65, :], "dw")

            p.wait("vector", "pe", p.cur("pe"))
            v_ms = step(p, "vector", "ve", lambda v: v.memset(ctx["rsumS"], 1.0))
            p.wait("sync", "ve", v_ms)
            p.wait_all("sync", "dw")
            t_rs = dma(p, "sync", ctx["rsumS8"], rsum_dram[8*g:8*g+8, :], "dx")
            p.waitt("scalar", t_rs)
            p.wait("scalar", "ve", p.cur("ve"))

            def recf(s):
                lnf = ctx["ot0"][0:16, :]
                with nc.allow_low_precision("bf16 softmax normalizer"):
                    s.activation(lnf, ctx["rsumS"][:, 0:1024], AF.Ln)
                    s.activation(ctx["rec16n"][:, 0:1024], lnf, AF.Exp,
                                 scale=-1.0)
                    s.activation(lnf, ctx["rsumS"][:, 1024:2048], AF.Ln)
                    return s.activation(ctx["rec16n"][:, 1024:2048], lnf,
                                        AF.Exp, scale=-1.0)
            v_rec = step(p, "scalar", "se", recf)
            p.wait("sync", "se", v_rec)
            dma(p, "sync", rec_dram[8*g:8*g+8, :], ctx["rec8n"], "dx")
            p.wait_all("sync", "dx")
            nb_bufs = [ctx["AR3"].ap()[0:64, 0:2048], ctx["AR3"].ap()[0:64, 2048:4096]]
            nb_tok = [p.cur("ve"), p.cur("ve")]
            v_nm_last = p.cur("ve")
            for hh in range(8):
                h = 8 * g + hh
                buf = nb_bufs[hh % 2]
                p.wait("sync", "ve", nb_tok[hh % 2])
                t_b = dma(p, "sync", buf,
                          rec_dram[h:h+1, :].partition_broadcast(64), "dx")
                p.waitt("vector", t_b)
                v_nm = step(p, "vector", "ve", lambda v, hh=hh, buf=buf: v.tensor_mul(
                    ctx[f"attnU{hh}"][0:64, :], ctx[f"attnU{hh}"][0:64, :], buf))
                nb_tok[hh % 2] = v_nm
                v_nm_last = v_nm

            p.wait("tensor", "ve", v_nm_last)
            p.wait("tensor", "se", v_ex[NI - 1])
            accs = [ctx["ps_wo"].ap()[:, 0:1024], ctx["ps_big"].ap()[:, 0:1024]]
            wo_ev = [p.cur("ve"), p.cur("ve")]
            st_tok = {}
            for q128 in range(16):
                acc = accs[q128 % 2]
                p.wait("tensor", "ve", wo_ev[q128 % 2])

                def womms(te, q128=q128, g=g, acc=acc):
                    r = None
                    for half in range(2):
                        for hh in range(8):
                            h = 8 * g + hh
                            r = te.matmul(acc[:, 512*half:512*(half+1)],
                                          ctx[f"attnU128_{hh}"][:,
                                                                128*q128:128*(q128+1)],
                                          ctx[f"w_o{h}"][:, 512*half:512*(half+1)],
                                          start=(hh == 0), stop=(hh == 7))
                    return r
                v_wo = step(p, "tensor", "pe", womms)
                ot = ctx["ot0"] if q128 % 2 == 0 else ctx["ot1"]
                if g == 0:
                    p.wait("vector", "pe", v_wo)
                    if q128 >= 2:
                        p.waitt("vector", st_tok[q128 - 2])
                    v_ev = step(p, "vector", "ve",
                                lambda v, ot=ot, acc=acc: v.tensor_copy(ot, acc))
                else:
                    if q128 >= 2:
                        p.waitt("sync", st_tok[q128 - 2])
                    v_ld = dma(p, "sync", ot, attn_dram[128*q128:128*(q128+1), :], "dx")
                    p.waitt("vector", v_ld)
                    p.wait("vector", "pe", v_wo)
                    v_ev = step(p, "vector", "ve", lambda v, ot=ot, acc=acc: v.tensor_add(
                        ot, ot, acc))
                wo_ev[q128 % 2] = v_ev
                p.wait("sync", "ve", v_ev)
                st_tok[q128] = dma(p, "sync", attn_dram[128*q128:128*(q128+1), :], ot, "dx")
                if g == 1 and tail_hook is not None:
                    tail_hook()
            if g == 1 and tail_hook is not None:
                while tail_hook():
                    pass
            p.wait_all("vector", "dx")
            p.wait_all("tensor", "dx")

    def make_e_pro_feeder(bt):
        avls = [ctx["avh"], ctx["AR3"].ap()[0:64, 4096:6144]]
        avhs = [ctx["avh_hi"], ctx["AR3"].ap()[64:128, 4096:6144]]
        eqls = [ctx["eq64"], ctx["AR3"].ap()[0:64, 6144:8192]]
        eqhs = [ctx["eq64_hi"], ctx["AR3"].ap()[64:128, 6144:8192]]
        state = {"k": 0, "toks": {}, "mul": {}}

        def issue_pair(c):
            s_ = c % 2
            if c >= 2:
                p.wait("gpsimd", "ve", state["mul"][c - 2])
                p.wait("scalar", "ve", state["mul"][c - 2])
            v_lo = dma(p, "gpsimd", avls[s_].rearrange("d (s q) -> d s q", s=8),
                       ag2_out.ap().rearrange("(s d) (ll q) -> s d ll q", d=64, ll=16)
                       [:, :, 2*c, :].rearrange("s d q -> d s q"), "dw")
            v_hi = dma(p, "scalar", avhs[s_].rearrange("d (s q) -> d s q", s=8),
                       ag2_out.ap().rearrange("(s d) (ll q) -> s d ll q", d=64, ll=16)
                       [:, :, 2*c+1, :].rearrange("s d q -> d s q"), "ds")
            v_el = dma(p, "gpsimd", eqls[s_],
                       eq_dram[2*c:2*c+1, :].partition_broadcast(64), "dw")
            v_eh = dma(p, "scalar", eqhs[s_],
                       eq_dram[2*c+1:2*c+2, :].partition_broadcast(64), "ds")
            state["toks"][c] = (v_lo, v_hi, v_el, v_eh)

        def feed():
            k = state["k"]
            if k > 9:
                return False
            if k == 0:
                p.wait("gpsimd", "cc", st["v_cc2"])
                p.wait_all("gpsimd", "dx")
                p.wait("gpsimd", "ve", p.cur("ve"))
                p.wait("gpsimd", "se", p.cur("se"))
                p.wait("gpsimd", "pe", p.cur("pe"))
                p.wait("scalar", "pe", p.cur("pe"))
                v_ig = dma(p, "gpsimd",
                           ctx["idxG"].rearrange("l (s q) -> l s q", s=8),
                           ag1_out.ap().rearrange("(s r) q -> s r q", r=32)
                           [:, 16:32, :].rearrange("s l q -> l s q"), "dw")
                p.waitt("vector", v_ig)
                v_eqb = step(p, "vector", "ve",
                             lambda v, bt=bt: v.tensor_single_scalar(
                    ctx["eq64"][0:16, :], ctx["idxG"], ctx["jcol"][:, bt:bt+1],
                    A.is_equal))
                state["v_eqb"] = v_eqb
                p.wait("tensor", "ve", v_eqb)
                p.wait("tensor", "se", p.cur("se"))
                for sl in range(4):
                    psum = banks[nb[0] % 2]; nb[0] += 1
                    p.wait("tensor", "ve", p.cur("ve"))

                    def mm(te, sl=sl, psum=psum):
                        return te.matmul(psum[0:1, :], ctx["onesc"][0:16, 0:1],
                                         ctx["eq64"][0:16, 512*sl:512*(sl+1)],
                                         start=True, stop=True)
                    v_mm = step(p, "tensor", "pe", mm)
                    p.wait("vector", "pe", v_mm)
                    step(p, "vector", "ve",
                         lambda v, sl=sl, psum=psum: v.tensor_copy(
                             ctx["nsel"][:, 512*sl:512*(sl+1)], psum[0:1, :]))
                v_ns = step(p, "vector", "ve", lambda v: v.tensor_copy(
                    ctx["nselb"], ctx["nsel"]))
                p.wait("scalar", "ve", v_ns)
                dma(p, "scalar", eq_dram.ap(), ctx["eq64"][0:16, :], "ds")
                p.wait_all("gpsimd", "ds")
                p.wait_all("scalar", "ds")
                dma(p, "scalar", ctx["g_rep"], wd["g_rep"].ap(), "ds")
                dma(p, "scalar", ctx["b_rep"], wd["b_rep"].ap(), "ds")
            elif k == 1:
                p.wait("gpsimd", "ve", state["v_eqb"])
                p.wait("scalar", "ve", state["v_eqb"])
                issue_pair(0)
                issue_pair(1)
            else:
                c = k - 2
                s_ = c % 2
                for t in state["toks"][c]:
                    p.waitt("vector", t)

                def avmf(v, c=c, s_=s_):
                    v.tensor_mul(ctx[f"avmP{c}"][0:64, :], avls[s_], eqls[s_])
                    return v.tensor_mul(ctx[f"avmP{c}"][64:128, :], avhs[s_],
                                        eqhs[s_])
                state["mul"][c] = step(p, "vector", "ve", avmf)
                if c + 2 < NCH:
                    issue_pair(c + 2)
            state["k"] = k + 1
            return True
        return feed

    def emit_e(bt, feed=None, preload_wo_hi=False):
        stat_m = ctx[f"stat_m{bt}"]
        stat_r = ctx[f"stat_r{bt}"]
        p.wait("gpsimd", "pe", p.cur("pe"))
        p.wait("gpsimd", "ve", p.cur("ve"))
        p.wait_all("gpsimd", "dx")
        for c in range(NCH):
            dma(p, "gpsimd", ctx[f"wo_p{c}"][:, :],
                wd["Wo_f"][128*c:128*(c+1), :], "dw")
        if preload_wo_hi:
            st["wo_pre_toks"] = [dma(p, "gpsimd", ctx[f"w_o{h}"][0:64, :],
                                     wd["W_O"][64*h:64*(h+1), :], "dw")
                                 for h in range(10, 16)]
        p.wait_all("tensor", "dw")
        p.wait_all("tensor", "ds")
        p.wait("tensor", "ve", p.cur("ve"))
        def bmrf(v):
            v.tensor_mul(ctx["bmr"], stat_m[:, :], stat_r[:, :])
            return v.tensor_scalar_mul(ctx["bmr"], ctx["bmr"], -1.0)
        v_bmr = step(p, "vector", "ve", bmrf)
        p.wait("scalar", "ve", v_bmr)

        pse = [ctx["ps_wo"].ap()[:, :], ctx["ps_big"].ap()[:, 0:1024]]
        xts = [ctx["xt0"], ctx["xt1"]]
        ats = [ctx["at0"], ctx["at1"]]
        yts = [ctx["ytile"], ctx["yt1"]]
        v_bank = {0: p.cur("ve"), 1: p.cur("ve")}
        v_free = {0: p.cur("ve"), 1: p.cur("ve")}
        v_stats = {0: p.cur("ve"), 1: p.cur("ve")}
        st_out = {}
        e_toks = {}

        def e_first(tb):
            b = tb % 2
            slot = tb if tb < 8 else 15 - tb
            off = 0 if tb < 8 else 128
            scol = slot * 256 + off
            p.wait("gpsimd", "ve", v_free[b])
            v_x = dma(p, "gpsimd", xts[b], x_own[bt, 128*tb:128*(tb+1), :], "dw")
            v_a = dma(p, "gpsimd", ats[b], attn_dram[128*tb:128*(tb+1), :], "dw")
            p.wait("tensor", "ve", v_bank[b])

            def omms(te, scol=scol, acc=pse[b]):
                r = None
                for half in range(2):
                    for c in range(NCH):
                        te.matmul(acc[:, 512*half:512*(half+1)],
                                  ctx[f"avmP{c}"][:, scol:scol+128],
                                  ctx[f"wo_p{c}"][:, 512*half:512*(half+1)],
                                  start=(c == 0), stop=False)
                    r = te.matmul(acc[:, 512*half:512*(half+1)],
                                  ctx["nselb"][:, scol:scol+128],
                                  ctx["bo_sb"][:, 512*half:512*(half+1)],
                                  start=False, stop=True)
                return r
            v_om = step(p, "tensor", "pe", omms)
            p.waitt("scalar", v_x)
            if tb >= 2:
                p.waitt("scalar", st_out[tb - 2])

            def y0f(s, b=b, tb=tb):
                return s.activation(yts[b], xts[b], AF.Identity,
                                    scale=stat_r[:, tb:tb+1],
                                    bias=ctx["bmr"][:, tb:tb+1])
            v_y0 = step(p, "scalar", "se", y0f)
            p.wait("vector", "pe", v_om)
            p.wait("vector", "se", v_y0)

            def yaf(v, b=b):
                v.tensor_add(yts[b], yts[b], pse[b])
                return v.reduce_sum(ctx[f"ssum{b}"], yts[b], AX.X)
            v_y = step(p, "vector", "ve", yaf)
            v_bank[b] = v_y
            e_toks[tb] = (v_a, v_y)

        def e_second(tb):
            b = tb % 2
            v_a, v_y = e_toks[tb]
            p.wait("scalar", "ve", v_y)
            p.wait("scalar", "ve", v_stats[b])
            v_sq = step(p, "scalar", "se", lambda s, b=b: s.activation(
                ctx["xq_e"], yts[b], AF.Square, accum_out=ctx[f"ssq{b}"]))
            p.wait("vector", "se", v_sq)

            def statf(v, b=b):
                v.tensor_scalar_mul(ctx[f"tvar{b}"], ctx[f"ssum{b}"], 1.0/1024)
                v.tensor_mul(ctx[f"tmsq{b}"], ctx[f"tvar{b}"], ctx[f"tvar{b}"])
                return v.scalar_tensor_tensor(ctx[f"tmsq{b}"], ctx[f"ssq{b}"],
                                              1.0/1024, ctx[f"tmsq{b}"],
                                              A.mult, A.subtract)
            v_st2 = step(p, "vector", "ve", statf)
            v_stats[b] = v_st2
            p.wait("scalar", "ve", v_st2)

            def rstd(s, b=b):
                s.activation(ctx[f"tln{b}"], ctx[f"tmsq{b}"], AF.Ln,
                             bias=ctx["epsc"][:, :])
                return s.activation(ctx[f"tln{b}"], ctx[f"tln{b}"], AF.Exp,
                                    scale=-0.5)
            v_rs = step(p, "scalar", "se", rstd)
            p.wait("vector", "se", v_rs)

            def b2f(v, b=b):
                v.tensor_mul(ctx[f"bias2{b}"], ctx[f"tvar{b}"], ctx[f"tln{b}"])
                return v.tensor_scalar_mul(ctx[f"bias2{b}"], ctx[f"bias2{b}"],
                                           -1.0)
            v_b2 = step(p, "vector", "ve", b2f)
            p.wait("scalar", "ve", v_b2)
            v_f1 = step(p, "scalar", "se", lambda s, b=b: s.activation(
                yts[b], yts[b], AF.Identity, scale=ctx[f"tln{b}"],
                bias=ctx[f"bias2{b}"]))
            p.wait("vector", "se", v_f1)
            p.waitt("vector", v_a)

            def fin2(v, b=b):
                v.tensor_mul(yts[b], yts[b], ctx["g_rep"])
                v.tensor_add(yts[b], yts[b], ctx["b_rep"])
                v.tensor_add(yts[b], yts[b], xts[b])
                return v.tensor_add(yts[b], yts[b], ats[b])
            v_f2 = step(p, "vector", "ve", fin2)
            v_free[b] = v_f2
            p.wait("sync", "ve", v_f2)
            st_out[tb] = dma(p, "sync", out_own[bt, 128*tb:128*(tb+1), :],
                             yts[b], "dx")

        for it in range(17):
            if it < 16:
                e_first(it)
            if it >= 1:
                e_second(it - 1)
            if feed is not None:
                feed()
        if feed is not None:
            while feed():
                pass
        p.waitt("scalar", st_out[14])
        p.waitt("scalar", st_out[15])


    feed0 = make_ln_feeder(0)
    emit_f1_tail(feed0)
    emit_a_mid(0)
    emit_p1b()
    emit_a_back(0, tail_hook=make_e_pro_feeder(0))
    feed1 = make_ln_feeder(1, alt=True)
    emit_e(0, feed1, preload_wo_hi=True)
    emit_a_mid(1, wo_hi_preloaded=True)
    emit_a_back(1, tail_hook=make_e_pro_feeder(1))
    emit_e(1)

    p.wait_all("gpsimd", "dx")
    p.wait_all("gpsimd", "dw")
    p.wait_all("gpsimd", "dg")
    return p


def kernel(L0_attn_x, L0_attn_W_DQ, L0_attn_W_UQ, L0_attn_W_QR, L0_attn_W_DKV,
           L0_attn_W_UK, L0_attn_W_UV, L0_attn_W_KR, L0_attn_W_O,
           L0_ffn_attn_mask, L0_ffn_Wg, L0_ffn_Wq, L0_ffn_bq, L0_ffn_Wkv,
           L0_ffn_bkv, L0_ffn_Wo, L0_ffn_bo, L0_ffn_ln_g, L0_ffn_ln_b):
    x = np.ascontiguousarray(np.asarray(L0_attn_x, dtype=np.float32))
    cst = host_constants()

    nc = bass.Bass("TRN2", target_bir_lowering=False)
    p = build(nc)
    with nc.Block() as blk:
        p.build(blk)

    W_UQ = np.asarray(L0_attn_W_UQ, dtype=np.float32)
    W_QRp = permute_qr_cols(np.asarray(L0_attn_W_QR, dtype=np.float32))
    uqp = np.zeros((D_LAT, 1568), np.float32)
    for h in range(16):
        uqp[:, 96*h:96*h+64] = W_UQ[:, 64*h:64*h+64]
        uqp[:, 96*h+64:96*h+96] = W_QRp[:, 32*h:32*h+32]
    uqp[:, 1536:1568] = W_UQ[:, 0:32]
    W_UK = np.asarray(L0_attn_W_UK, dtype=np.float32)
    ukp = np.concatenate([W_UK, W_UK[:, 0:64]], axis=1)
    krp = np.zeros((D_MODEL, 128), np.float32)
    krp[:, 64:96] = permute_qr_cols(np.asarray(L0_attn_W_KR, dtype=np.float32))
    shared = {
        "W_DQ": np.ascontiguousarray(L0_attn_W_DQ, dtype=np.float32),
        "W_UQP": np.ascontiguousarray(uqp),
        "W_DKV": np.ascontiguousarray(L0_attn_W_DKV, dtype=np.float32),
        "W_UKP": np.ascontiguousarray(ukp),
        "W_UV": np.ascontiguousarray(L0_attn_W_UV, dtype=np.float32),
        "W_KRP": np.ascontiguousarray(krp),
        "W_O": np.ascontiguousarray(L0_attn_W_O, dtype=np.float32),
        "Wg": np.ascontiguousarray(L0_ffn_Wg, dtype=np.float32),
        "Wq_r": np.ascontiguousarray(np.asarray(L0_ffn_Wq, dtype=np.float32)
                                     .transpose(1, 0, 2).reshape(D_MODEL, 16 * 64)),
        "Wkv_r": np.ascontiguousarray(np.asarray(L0_ffn_Wkv, dtype=np.float32)
                                      .transpose(1, 0, 2).reshape(D_MODEL, 16 * 128)),
        "Wo_f": np.ascontiguousarray(np.asarray(L0_ffn_Wo, dtype=np.float32)
                                     .reshape(16 * 64, 1024)),
        "bq_td": np.ascontiguousarray(np.asarray(L0_ffn_bq, dtype=np.float32).T),
        "bkv_td": np.ascontiguousarray(np.asarray(L0_ffn_bkv, dtype=np.float32).T),
        "g_rep": np.ascontiguousarray(np.broadcast_to(
            np.asarray(L0_ffn_ln_g, dtype=np.float32)[None, :], (128, 1024))),
        "b_rep": np.ascontiguousarray(np.broadcast_to(
            np.asarray(L0_ffn_ln_b, dtype=np.float32)[None, :], (128, 1024))),
        "bo_row": np.ascontiguousarray(
            np.asarray(L0_ffn_bo, dtype=np.float32)[None, :]),
    }
    for k, v in cst.items():
        shared["c_" + k] = np.ascontiguousarray(v)

    in_maps = []
    for c in range(NC):
        blocks = [c, 15 - c]
        m = dict(shared)
        m["x_own"] = np.ascontiguousarray(x[2 * c:2 * c + 2])
        xp = np.empty((16, NBLK, 128, D_MODEL), np.float32)
        for bi, tb in enumerate(blocks):
            xp[:, bi] = x[:, 128 * tb:128 * (tb + 1), :]
        m["x_pos"] = xp
        basearr = np.zeros((128, NBLK), np.float32)
        for bi in range(NBLK):
            basearr[:, bi] = bi * 2048 + np.arange(128)
        m["c_base"] = basearr
        jc = np.zeros((16, 2), np.float32)
        jc[:, 0] = 2 * c
        jc[:, 1] = 2 * c + 1
        m["c_jcol"] = jc
        in_maps.append(m)

    res = run_bass_kernel_spmd(nc, in_maps, core_ids=list(range(NC)))
    out = np.concatenate([res.results[c]["out_own"] for c in range(NC)], axis=0)
    return out


# version 26
# speedup vs baseline: 1.0057x; 1.0057x over previous
import numpy as np
import ml_dtypes
import concourse.bass as bass
import concourse.mybir as mybir
from concourse.bass_core_rust import IndirectOffsetOnAxis
from concourse.bass_utils import run_bass_kernel_spmd

BF16 = mybir.dt.bfloat16
F32 = mybir.dt.float32
U32 = mybir.dt.uint32
AF = mybir.ActivationFunctionType
A = mybir.AluOpType
AX = mybir.AxisListType
bf = ml_dtypes.bfloat16

D_MODEL, N_HEADS, D_H, D_HR, D_LAT = 1024, 16, 64, 32, 256
S, B, NC = 2048, 16, 8
SQ = 512
NKT, NQT, NCH = 16, 4, 8
NBLK, NPOS = 2, 256
NTOK = 16 * NPOS
INV_SQRT_DQK = 1.0 / np.sqrt(96.0)
CLIP_HI = float(np.exp(np.float32(80.0) * INV_SQRT_DQK))
CLIP_LO = float(np.exp(np.float32(-80.0) * INV_SQRT_DQK))
EPS = 1e-5
JBIG = 16.0


class Prog:
    ENGINES = ("sync", "scalar", "vector", "tensor", "gpsimd")

    def __init__(self, nc):
        self.nc = nc
        self.ops = {e: [] for e in self.ENGINES}
        self.sems = {}
        self.counts = {}
        self._rot = {}

    def sem(self, name):
        if name not in self.sems:
            self.sems[name] = self.nc.semaphore(name).__enter__()
            self.counts[name] = 0
        return self.sems[name]

    def emit(self, engine, fn):
        self.ops[engine].append(fn)

    def inc(self, name, by):
        h = self.sem(name)
        self.counts[name] += by
        return h, self.counts[name]

    def cur(self, name):
        self.sem(name)
        return self.counts[name]

    def wait(self, engine, name, val):
        h = self.sem(name)
        self.emit(engine, lambda e, h=h, v=val: e.wait_ge(h, v))

    def waitt(self, engine, token):
        name, v = token
        h = self.sem(name)
        self.emit(engine, lambda e, h=h, v=v: e.wait_ge(h, v))

    def wait_all(self, engine, family):
        for i in range(NROT):
            name = f"{family}.{i}"
            if name in self.sems and self.counts[name] > 0:
                h = self.sems[name]
                self.emit(engine, lambda e, h=h, v=self.counts[name]: e.wait_ge(h, v))

    def build(self, blk):
        for name in self.ENGINES:
            stream = self.ops[name]
            ser_sem = self.sem("_ser_" + name) if name in ("vector", "scalar") else None

            def make(stream, ser_sem):
                def run(eng):
                    e = SerEng(eng, ser_sem) if ser_sem is not None else eng
                    for fn in stream:
                        fn(e)
                return run
            getattr(blk, name)(make(stream, ser_sem))


_EMITTING = {
    "tensor_copy", "tensor_tensor", "tensor_add", "tensor_sub", "tensor_mul",
    "tensor_max", "tensor_scalar", "tensor_single_scalar", "tensor_scalar_add",
    "tensor_scalar_sub", "tensor_scalar_mul", "tensor_scalar_max",
    "tensor_scalar_min", "tensor_relu", "tensor_reduce", "reduce_max",
    "reduce_sum", "scalar_tensor_tensor", "tensor_tensor_scan", "memset",
    "memzero", "select", "copy_predicated", "transpose", "bn_stats", "bn_aggr",
    "max", "max_index", "pool", "pool_avg", "pool_max", "reciprocal",
    "stream_shuffle", "match_replace", "activation", "add", "mul", "sqrt",
    "square", "sign", "copy", "iota", "affine_select",
}


class SerEng:

    def __init__(self, eng, sem):
        self._eng = eng
        self._sem = sem
        self._n = 0
        self._waited = 0
        self._pend = []

    @staticmethod
    def _span(a):
        try:
            lst = a.ap
            if callable(lst):
                return None
            off = a.offset
            esz = mybir.dt.size(a.dtype)
            lo = off
            hi = off
            for st, cnt in lst[1:]:
                d = st * (cnt - 1)
                if d < 0:
                    lo += d
                else:
                    hi += d
            return (a.tensor.name, lo * esz, (hi + 1) * esz)
        except Exception:
            return None

    def _need(self, rds, wrs):
        need = 0
        for (nm, lo, hi) in wrs:
            for (pn, pl, ph, pw, seq) in self._pend:
                if pn == nm and lo < ph and pl < hi and seq > need:
                    need = seq
        for (nm, lo, hi) in rds:
            for (pn, pl, ph, pw, seq) in self._pend:
                if pw and pn == nm and lo < ph and pl < hi and seq > need:
                    need = seq
        return need

    def _wait_to(self, need):
        if need > self._waited:
            self._eng.wait_ge(self._sem, need)
            self._waited = need
            self._pend = [e for e in self._pend if e[4] > need]

    def flush(self):
        self._wait_to(self._n)

    def __getattr__(self, name):
        f = getattr(self._eng, name)
        if name not in _EMITTING:
            return f

        def wrapped(*a, **k):
            aps = []
            bad = False
            for x in a:
                if hasattr(x, "tensor"):
                    s = self._span(x)
                    if s is None:
                        bad = True
                    aps.append(s)
                else:
                    aps.append(None) if False else None
            kw_aps = []
            for kk, x in k.items():
                if hasattr(x, "tensor"):
                    s = self._span(x)
                    if s is None:
                        bad = True
                    kw_aps.append((kk, s))
            ap_pos = [s for s in aps if s is not None]
            if bad:
                self.flush()
                wrs, rds = [], []
            else:
                wrs = ap_pos[:1] + [s for kk, s in kw_aps if kk == "accum_out" and s]
                rds = ap_pos[1:] + [s for kk, s in kw_aps if kk != "accum_out" and s]
                self._wait_to(self._need(rds, wrs))
            inst = f(*a, **k)
            self._n += 1
            inst.then_inc(self._sem, 1)
            for s in wrs:
                self._pend.append((s[0], s[1], s[2], True, self._n))
            for s in rds:
                self._pend.append((s[0], s[1], s[2], False, self._n))
            if bad:
                self.flush()
            return inst
        return wrapped


NROT = 4


def _dma_common(p, engine, family, emit_fn):
    i = p._rot.get(family, 0)
    p._rot[family] = i + 1
    name = f"{family}.{i % NROT}"
    h, v = p.inc(name, 16)
    if v > 16:
        p.emit(engine, lambda e, h=h, w=v - 16: e.wait_ge(h, w))
    p.emit(engine, lambda e, h=h: emit_fn(e).then_inc(h, 16))
    return (name, v)


def dma(p, engine, out_ap, in_ap, family):
    return _dma_common(p, engine, family,
                       lambda e, o=out_ap, i=in_ap: e.dma_start(out=o, in_=i))


def dma_t(p, engine, out_ap, in_ap, family):
    return _dma_common(p, engine, family,
                       lambda e, o=out_ap, i=in_ap: e.dma_start_transpose(o, i))


def dma_ind(p, engine, out_ap, src_ap, idx_ap, family):
    return _dma_common(p, engine, family,
                       lambda e, o=out_ap, s=src_ap, x=idx_ap: e.indirect_dma_start(
                           o, None, s, IndirectOffsetOnAxis(x, 0)))


def step(p, engine, sem_name, fn, by=1):
    h, v = p.inc(sem_name, by)
    if engine in ("vector", "scalar"):
        def g(e, h=h, by=by, fn=fn):
            fn(e)
            e.flush()
            e.sem_inc(h, by)
        p.emit(engine, g)
    else:
        p.emit(engine, lambda e, h=h, fn=fn, by=by: fn(e).then_inc(h, by))
    return v


def host_constants():
    c = {}
    inv = 10000.0 ** (-np.arange(0, D_HR, 2, dtype=np.float64) / D_HR)
    ang = np.arange(S, dtype=np.float64)[:, None] * inv[None, :]
    cos = np.cos(ang).T.astype(np.float32)
    sin = np.sin(ang).T.astype(np.float32)
    cos32 = np.concatenate([cos, cos], 0)
    sin32 = np.concatenate([-sin, sin], 0)
    c["trig_c"] = np.tile(cos32, (4, 1)).astype(bf)
    c["trig_s"] = np.tile(sin32, (4, 1)).astype(bf)
    q = np.arange(SQ)[None, :]
    k = np.arange(128)[:, None]
    for j in range(4):
        c[f"dmask{j}"] = (q >= (k + 128 * j)).astype(bf)
    oh = np.zeros((16, 1024), bf)
    for h in range(16):
        oh[h, 64 * h:64 * (h + 1)] = 1
    c["onehot16"] = oh
    c["ident128"] = np.eye(128, dtype=bf)
    jt = np.zeros((16, NTOK), bf)
    for t in range(NTOK):
        jt[:, t] = (t // 128) % 16
    c["jtile"] = jt
    return c


def permute_qr_cols(w):
    nh = w.shape[1] // D_HR
    w = w.reshape(w.shape[0], nh, D_HR)
    perm = np.concatenate([np.arange(0, D_HR, 2), np.arange(1, D_HR, 2)])
    return np.ascontiguousarray(w[:, :, perm].reshape(w.shape[0], nh * D_HR))


def build(nc):
    p = Prog(nc)

    x_own = nc.dram_tensor("x_own", [2, S, D_MODEL], F32, kind="ExternalInput")
    x_pos = nc.dram_tensor("x_pos", [16, NBLK, 128, D_MODEL], F32, kind="ExternalInput")
    out_own = nc.dram_tensor("out_own", [2, S, D_MODEL], F32, kind="ExternalOutput")

    w_shapes = {"W_DQ": (D_MODEL, D_LAT), "W_UQP": (D_LAT, 1568),
                "W_DKV": (D_MODEL, D_LAT), "W_UKP": (D_LAT, 1088),
                "W_UV": (D_LAT, 1024),
                "W_KRP": (D_MODEL, 128), "W_O": (1024, D_MODEL),
                "Wg": (D_MODEL, 16), "Wq_r": (D_MODEL, 1024), "Wkv_r": (D_MODEL, 2048),
                "Wo_f": (16 * 64, 1024),
                "bq_td": (64, 16), "bkv_td": (128, 16),
                "g_rep": (128, 1024), "b_rep": (128, 1024), "bo_row": (1, 1024)}
    wd = {n: nc.dram_tensor(n, list(sh), F32, kind="ExternalInput")
          for n, sh in w_shapes.items()}
    cst = host_constants()
    cd = {k: nc.dram_tensor("c_" + k, list(v.shape), BF16, kind="ExternalInput")
          for k, v in cst.items()}
    base_d = nc.dram_tensor("c_base", [128, NBLK], F32, kind="ExternalInput")
    jcol_d = nc.dram_tensor("c_jcol", [16, 2], F32, kind="ExternalInput")

    xn_dram = nc.dram_tensor("xn_dram", [S, D_MODEL], BF16)
    xnF_dram = nc.dram_tensor("xnF_dram", [NTOK, D_MODEL], BF16)
    xnlo_dram = nc.dram_tensor("xnlo_dram", [NTOK, D_MODEL], BF16)
    lnz_dram = nc.dram_tensor("lnz_dram", [1, NTOK], F32)
    hg_dram = nc.dram_tensor("hg_dram", [NTOK, D_MODEL], BF16)
    idx_sc = nc.dram_tensor("idx_sc", [16, NPOS], BF16)
    attn_dram = nc.dram_tensor("attn_dram", [S, D_MODEL], F32)
    eq_dram = nc.dram_tensor("eq_dram", [16, S], BF16)
    rec_dram = nc.dram_tensor("rec_dram", [16, S], BF16)
    rsum_dram = nc.dram_tensor("rsum_dram", [16, S], BF16)
    ag1_in = nc.dram_tensor("ag1_in", [32, NPOS], F32)
    ag1_out = nc.dram_tensor("ag1_out", [NC * 32, NPOS], F32)
    ag2_in = nc.dram_tensor("ag2_in", [64, NTOK], BF16)
    ag2_out = nc.dram_tensor("ag2_out", [NC * 64, NTOK], BF16)

    ctx = {}

    def T(name, shape, dtype=BF16):
        h = nc.sbuf_tensor(name, shape, dtype).__enter__()
        ctx[name] = h
        return h

    for c in range(NCH):
        T(f"w_dq{c}", [128, D_LAT]); T(f"w_dkv{c}", [128, D_LAT])
    for l in range(2):
        T(f"w_qp{l}", [128, 1568])
        T(f"w_uk{l}", [128, 1088]); T(f"w_uv{l}", [128, 1024])
    T("wg_all", [128, 8 * 16])
    T("onesf", [16, 1], F32)
    for c in range(NCH):
        ctx[f"wg{c}"] = ctx["wg_all"].ap()[:, 16 * c:16 * (c + 1)]
    for j in range(4):
        T(f"dmask{j}", [128, SQ])
    T("onehot16", [16, 1024])
    T("onesc", [64, 1]); T("onesr", [1, 64])
    T("epsc", [128, 1], F32)
    T("base", [128, NBLK], F32)
    T("jcol", [16, 2], F32)
    T("bq_t", [64, 16], F32); T("bkv_t", [128, 16], F32)
    T("bo_sb", [1, 1024])
    T("ssum", [128, 1], F32); T("ssq", [128, 1], F32)
    T("tvar", [128, 1], F32); T("tln", [128, 1], F32); T("tmsq", [128, 1], F32)

    for b in range(2):
        T(f"stat_m{b}", [128, 16], F32); T(f"stat_r{b}", [128, 16], F32)
    T("mx", [16, NPOS], F32)
    T("idxf", [16, NPOS], F32)
    for b in range(NBLK):
        T(f"idxT{b}", [128, 16])
        T(f"rt{b}", [128, 16], F32)
        T(f"rtu{b}", [128, 16], U32)
    for par in range(2):
        rt = ctx[f"rt{par}"]
        ctx[f"ssum{par}"] = rt.ap()[:, 5:6]
        ctx[f"ssq{par}"] = rt.ap()[:, 6:7]
        ctx[f"tvar{par}"] = rt.ap()[:, 7:8]
        ctx[f"tln{par}"] = rt.ap()[:, 8:9]
        ctx[f"tmsq{par}"] = rt.ap()[:, 9:10]
        ctx[f"bias2{par}"] = rt.ap()[:, 10:11]
        ctx[f"lssum{par}"] = rt.ap()[:, 11:12]
        ctx[f"lssq{par}"] = rt.ap()[:, 12:13]
        ctx[f"ltvar{par}"] = rt.ap()[:, 13:14]
        ctx[f"ltln{par}"] = rt.ap()[:, 14:15]
        ctx[f"ltmsq{par}"] = rt.ap()[:, 15:16]
    ctx["bmr"] = ctx["rtu0"].ap().bitcast(F32)

    T("w16", [16, NPOS], F32)
    T("w16b", [16, NPOS])
    T("zf", [16, 1], F32)
    T("bias2", [128, 1], F32)

    T("AR1", [128, 8192])
    ctx["xa0"] = ctx["AR1"].ap().bitcast(F32)[:, 0:1024]
    ctx["xa1"] = ctx["AR1"].ap().bitcast(F32)[:, 1024:2048]
    ctx["zrow"] = ctx["AR1"].ap().bitcast(F32)[0:1, 0:4096]
    ctx["ot0"] = ctx["AR1"].ap().bitcast(F32)[:, 1024:2048]

    T("AR2", [128, 16384])
    ctx["lg"] = ctx["AR2"].ap().bitcast(F32)[0:16, 0:4096]
    ctx["M"] = ctx["lg"]
    ctx["egf"] = ctx["AR2"].ap().bitcast(F32)[0:16, 4096:8192]
    ctx["lnzbc"] = ctx["egf"]
    ctx["xnf32"] = ctx["AR2"].ap().bitcast(F32)[:, 0:1024]
    ctx["eg"] = ctx["AR2"].ap()[0:16, 8192:12288]
    ctx["jtile"] = ctx["AR2"].ap()[0:16, 12288:16384]
    for c in range(NCH):
        ctx[f"wkv{c}"] = ctx["AR2"].ap()[:, 2048 * c:2048 * (c + 1)]
    for h in range(16):
        ctx[f"w_o{h}"] = ctx["AR2"].ap()[:, 1024 * h:1024 * (h + 1)]

    T("AR3", [128, 16384])
    ctx["eq"] = ctx["AR3"].ap()[0:64, 0:4096]
    ctx["cand"] = ctx["AR3"].ap()[0:64, 4096:8192]
    ctx["qall"] = ctx["cand"]
    ctx["tt"] = ctx["eq"]
    ctx["attG"] = ctx["AR3"].ap().bitcast(F32)[0:16, 2048:4096]
    ctx["kvall"] = ctx["AR3"].ap()[:, 8192:12288]
    ctx["hgt0"] = ctx["AR3"].ap()[:, 12288:13312]
    ctx["hgt1"] = ctx["AR3"].ap()[:, 13312:14336]
    for hh in range(NCH):
        ctx[f"khead{hh}"] = ctx["AR3"].ap()[:, S * hh:S * (hh + 1)]
    ctx["avh"] = ctx["AR3"].ap()[0:64, 0:2048]
    ctx["avh_hi"] = ctx["AR3"].ap()[64:128, 0:2048]
    ctx["eq64"] = ctx["AR3"].ap()[0:64, 2048:4096]
    ctx["eq64_hi"] = ctx["AR3"].ap()[64:128, 2048:4096]
    ctx["idxG"] = ctx["AR3"].ap().bitcast(F32)[0:16, 2048:4096]
    ctx["nsel"] = ctx["AR3"].ap().bitcast(F32)[0:1, 4096:6144]
    ctx["nselb"] = ctx["AR3"].ap()[0:1, 12288:14336]
    ctx["xq_e"] = ctx["AR3"].ap()[:, 0:1024]
    ctx["xt0"] = ctx["AR2"].ap().bitcast(F32)[:, 0:1024]
    ctx["xt1"] = ctx["AR2"].ap().bitcast(F32)[:, 1024:2048]
    ctx["at0"] = ctx["AR2"].ap().bitcast(F32)[:, 2048:3072]
    ctx["at1"] = ctx["AR2"].ap().bitcast(F32)[:, 3072:4096]
    ctx["yt1"] = ctx["AR2"].ap().bitcast(F32)[:, 4096:5120]

    for c in range(NCH):
        T(f"xnT{c}", [128, 2 * S])
    for h2 in range(2):
        ctx[f"cqT{h2}"] = ctx[f"xnT{4 + h2}"].ap()[:, S:2 * S]
        ctx[f"ckvT{h2}"] = ctx[f"xnT{6 + h2}"].ap()[:, S:2 * S]
    for hh in range(NCH):
        ctx[f"qhead{hh}"] = ctx[f"xnT{hh}"].ap()[:, 0:S]
    for c in range(NCH):
        ctx[f"avmP{c}"] = ctx[f"xnT{c}"].ap()[:, 0:S]
        ctx[f"wo_p{c}"] = ctx["AR1"].ap()[:, 1024 * c:1024 * (c + 1)]

    T("attnU_t", [128, 2048])
    ctx["wgf"] = ctx["attnU_t"].ap().bitcast(F32)[:, 0:128]
    ctx["ident128"] = ctx["attnU_t"].ap()[:, 1024:1152]
    ctx["wglo_all"] = ctx["attnU_t"].ap()[:, 512:640]
    for c in range(NCH):
        ctx[f"wglo{c}"] = ctx["wglo_all"][:, 16 * c:16 * (c + 1)]
    for c in range(NCH):
        ctx[f"w_krp{c}"] = ctx["attnU_t"].ap()[:, 128 * c:128 * (c + 1)]
    for hh in range(4):
        ctx[f"attnU{hh}"] = ctx[f"xnT{hh}"].ap()[0:65, S:2 * S]
        ctx[f"attnU128_{hh}"] = ctx[f"xnT{hh}"].ap()[:, S:2 * S]
    ctx["attnU4"] = ctx["AR1"].ap()[0:65, 4096:6144]
    ctx["attnU128_4"] = ctx["AR1"].ap()[:, 4096:6144]
    ctx["attnU5"] = ctx["AR1"].ap()[0:65, 6144:8192]
    ctx["attnU128_5"] = ctx["AR1"].ap()[:, 6144:8192]
    ctx["attnU6"] = ctx["attnU_t"].ap()[0:65, 0:S]
    ctx["attnU128_6"] = ctx["attnU_t"].ap()[:, 0:S]
    ctx["attnU7"] = ctx["AR1"].ap()[0:65, 0:2048]
    ctx["attnU128_7"] = ctx["AR1"].ap()[:, 0:2048]

    T("vh_t", [128, 16 * 520 + 64])
    for kt in range(NKT):
        ctx[f"vh{kt}"] = ctx["vh_t"].ap()[:, 520 * kt:520 * (kt + 1)]
    ctx["g_rep"] = ctx["vh_t"].ap().bitcast(F32)[:, 0:1024]
    ctx["b_rep"] = ctx["vh_t"].ap().bitcast(F32)[:, 1024:2048]
    for c in range(NCH):
        ctx[f"wq{c}"] = ctx["vh_t"].ap()[:, 1024 * c:1024 * (c + 1)]
    ctx["krSt"] = ctx["AR3"].ap()[64:96, 0:2048]
    ctx["krPr"] = ctx["AR3"].ap()[64:96, 2048:4096]
    ctx["pQ0"] = ctx["vh_t"].ap()[64:96, 2048:4096]
    ctx["pQ1"] = ctx["vh_t"].ap()[64:96, 4096:6144]
    ctx["pQ2"] = ctx["vh_t"].ap()[64:96, 6144:8192]
    ctx["pQ3"] = ctx["vh_t"].ap()[64:96, 0:2048]

    T("P0", [128, 4 * SQ]); T("P1", [128, 4 * SQ])
    ctx["trig_c"] = ctx["P0"]; ctx["trig_s"] = ctx["P1"]
    ctx["xq0"] = ctx["P0"].ap()[:, 0:1024]
    ctx["xq1"] = ctx["P1"].ap()[:, 1024:2048]
    ctx["xn0"] = ctx["P0"].ap()[:, 1024:2048]
    ctx["xn1"] = ctx["P1"].ap()[:, 0:1024]
    ctx["xnlo"] = ctx["P1"].ap()[:, 1024:2048]
    ctx["ytile"] = ctx["P0"].ap().bitcast(F32)[:, 0:1024]
    ctx["xtile"] = ctx["P1"].ap().bitcast(F32)[:, 0:1024]
    ctx["ot1"] = ctx["P0"].ap().bitcast(F32)[:, 0:1024]
    ctx["rsumS"] = ctx["P0"].ap()[0:16, 0:2048]
    ctx["rsumS8"] = ctx["P0"].ap()[0:8, 0:2048]
    ctx["rec16n"] = ctx["P1"].ap()[0:16, 0:2048]
    ctx["rec8n"] = ctx["P1"].ap()[0:8, 0:2048]

    ps_big = nc.psum_tensor([128, 2048], F32).__enter__()
    ps_a = nc.psum_tensor([128, 512], F32).__enter__()
    ps_pv = nc.psum_tensor([128, 512], F32).__enter__()
    ps_wo = nc.psum_tensor([128, 1024], F32).__enter__()
    ctx.update(ps_big=ps_big, ps_a=ps_a, ps_pv=ps_pv, ps_wo=ps_wo)
    banks = [ps_a, ps_pv]
    nb = [0]

    for c in range(NCH):
        dma(p, "gpsimd", ctx[f"w_dq{c}"][:, :], wd["W_DQ"][128*c:128*(c+1), :], "dw")
        dma(p, "gpsimd", ctx[f"w_dkv{c}"][:, :], wd["W_DKV"][128*c:128*(c+1), :], "dw")
        dma(p, "gpsimd", ctx["wgf"][:, 16*c:16*(c+1)], wd["Wg"][128*c:128*(c+1), :], "dw")
    for l in range(2):
        dma(p, "gpsimd", ctx[f"w_qp{l}"][:, :], wd["W_UQP"][128*l:128*(l+1), :], "dw")
        dma(p, "gpsimd", ctx[f"w_uk{l}"][:, :], wd["W_UKP"][128*l:128*(l+1), :], "dw")
        dma(p, "gpsimd", ctx[f"w_uv{l}"][:, :], wd["W_UV"][128*l:128*(l+1), :], "dw")
    for c in range(NCH):
        dma(p, "gpsimd", ctx[f"wq{c}"], wd["Wq_r"][128*c:128*(c+1), :], "dw")
    for c in range(2, NCH):
        dma(p, "gpsimd", ctx[f"wkv{c}"][16:128, :],
            wd["Wkv_r"][128*c+16:128*(c+1), :], "dw")
    for k in ("dmask0", "dmask1", "dmask2", "dmask3", "onehot16"):
        dma(p, "gpsimd", ctx[k][:, :], cd[k].ap(), "dw")
    dma(p, "gpsimd", ctx["base"][:, :], base_d.ap(), "dw")
    dma(p, "gpsimd", ctx["jcol"][:, :], jcol_d.ap(), "dw")
    dma(p, "gpsimd", ctx["bq_t"][:, :], wd["bq_td"].ap(), "dw")
    dma(p, "gpsimd", ctx["bkv_t"][:, :], wd["bkv_td"].ap(), "dw")
    dma(p, "gpsimd", ctx["bo_sb"][:, :], wd["bo_row"].ap(), "dw")

    t_idn = dma(p, "gpsimd", ctx["ident128"], cd["ident128"].ap(), "dw")
    p.waitt("tensor", t_idn)

    def init0(v):
        v.memset(ctx["epsc"][:, :], EPS)
        v.memset(ctx["onesc"][:, :], 1.0)
        v.memset(ctx["onesr"][:, :], 1.0)
        v.memset(ctx["onesf"][:, :], 1.0)
        return v.memset(ctx["vh_t"][:, 16 * 520:16 * 520 + 64], 0.0)
    step(p, "vector", "ve", init0)
    p.wait_all("vector", "dw")

    def wgsplit(v):
        v.tensor_copy(ctx["wg_all"][:, :], ctx["wgf"][:, :])
        return v.tensor_sub(ctx["wglo_all"][:, :], ctx["wgf"][:, :],
                            ctx["wg_all"][:, :])
    step(p, "vector", "ve", wgsplit)

    def ln_tile(xa, xn_out, m_ap=None, r_ap=None, par=0, pfx="", xq=None):
        sm = ctx[f"{pfx}ssum{par}"]
        sq = ctx[f"{pfx}ssq{par}"]
        tv = ctx[f"{pfx}tvar{par}"]
        tl = ctx[f"{pfx}tln{par}"]
        tm = ctx[f"{pfx}tmsq{par}"]
        if xq is None:
            xq = ctx[f"xq{par}"]
        step(p, "vector", "ve", lambda v, xa=xa, sm=sm: v.reduce_sum(sm, xa, AX.X))
        v_sq = step(p, "scalar", "se", lambda s, xa=xa, sq=sq, xq=xq: s.activation(
            xq, xa, AF.Square, accum_out=sq))
        p.wait("vector", "se", v_sq)
        if m_ap is None:
            m_ap = tv

        def stats(v, m_ap=m_ap, sm=sm, sq=sq, tm=tm):
            v.tensor_scalar_mul(m_ap, sm, 1.0 / 1024)
            v.tensor_mul(tm, m_ap, m_ap)
            return v.scalar_tensor_tensor(tm, sq, 1.0 / 1024, tm,
                                          A.mult, A.subtract)
        v_st = step(p, "vector", "ve", stats)
        p.wait("scalar", "ve", v_st)
        if r_ap is None:
            r_ap = tl

        def rstd(s, r_ap=r_ap, tm=tm, tl=tl):
            s.activation(tl, tm, AF.Ln, bias=ctx["epsc"][:, :])
            return s.activation(r_ap, tl, AF.Exp, scale=-0.5)
        v_rs = step(p, "scalar", "se", rstd)
        p.wait("vector", "se", v_rs)
        return step(p, "vector", "ve", lambda v, xa=xa, xn=xn_out, m=m_ap, r=r_ap:
                    v.tensor_scalar(xn, xa, m, r, A.subtract, A.mult))

    st1_toks, st2_toks = {}, {}
    f1_tp, f1_ev = {}, {}
    f1_psb = ([(ctx["ps_big"].ap().bitcast(BF16), 1024 * i) for i in range(4)] +
              [(ctx["ps_wo"].ap().bitcast(BF16), 1024 * i) for i in range(2)])
    for t in range(32):
        blk, j = t // 16, t % 16
        par = t % 2
        xa = ctx["xa0"] if par == 0 else ctx["xa1"]
        xn = ctx["xn0"] if par == 0 else ctx["xn1"]
        xnf = (ctx["AR2"].ap().bitcast(F32)[:, 0:1024] if par == 0
               else ctx["AR2"].ap().bitcast(F32)[:, 1024:2048])
        xnlo = (ctx["AR3"].ap()[:, 0:1024] if par == 0
                else ctx["AR3"].ap()[:, 1024:2048])
        if t >= 2:
            p.waitt("sync", st1_toks[t - 2])
            p.waitt("sync", st2_toks[t - 2])
        v_load = dma(p, "sync", xa, x_pos[j, blk, :, :], "dx")
        p.waitt("vector", v_load)
        p.waitt("scalar", v_load)
        v_no = ln_tile(xa, xnf, par=par)

        def split(v, xn=xn, xnf=xnf, xnlo=xnlo, t=t):
            v.tensor_copy(xn, xnf)
            return v.tensor_sub(xnlo, xnf, xn)
        if t >= 2:
            p.wait("vector", "pe", f1_tp[t - 2])
        v_sp = step(p, "vector", "ve", split)
        p.wait("sync", "ve", v_sp)
        st1_toks[t] = dma(p, "sync", xnF_dram[128*t:128*(t+1), :], xn, "dx")
        st2_toks[t] = dma(p, "sync", xnlo_dram[128*t:128*(t+1), :], xnlo, "dx")
        sb_, soff = f1_psb[t % 6]
        if t >= 6:
            p.wait("tensor", "ve", f1_ev[t - 6])
        p.wait("tensor", "ve", v_sp)

        def tps(te, xn=xn, sb_=sb_, soff=soff):
            r = None
            for f in range(NCH):
                r = te.transpose(sb_[:, soff + 128*f:soff + 128*(f+1)],
                                 xn[:, 128*f:128*(f+1)], ctx["ident128"])
            return r
        f1_tp[t] = step(p, "tensor", "pe", tps)
        p.wait("vector", "pe", f1_tp[t])

        def tev(v, t=t, sb_=sb_, soff=soff):
            r = None
            for c in range(NCH):
                r = v.tensor_copy(ctx[f"xnT{c}"][:, 128*t:128*(t+1)],
                                  sb_[:, soff + 128*c:soff + 128*(c+1)])
            return r
        f1_ev[t] = step(p, "vector", "ve", tev)
    p.wait_all("sync", "dx")
    p.wait("gpsimd", "ve", p.cur("ve"))
    p.wait("gpsimd", "se", p.cur("se"))
    for c in range(2):
        dma(p, "gpsimd", ctx[f"wkv{c}"][16:128, :],
            wd["Wkv_r"][128*c+16:128*(c+1), :], "dw")
    p.wait_all("tensor", "dx")
    p.wait_all("tensor", "dw")
    p.wait("tensor", "ve", p.cur("ve"))

    f1_tok = {0: p.cur("ve"), 1: p.cur("ve")}
    for sl in range(NTOK // 512):
        bi = nb[0] % 2; psum = banks[bi]; nb[0] += 1
        p.wait("tensor", "ve", f1_tok[bi])

        def mm(te, sl=sl, psum=psum):
            r = None
            for c in range(NCH):
                te.matmul(psum[0:16, :], ctx[f"wg{c}"],
                          ctx[f"xnT{c}"][:, 512*sl:512*(sl+1)],
                          start=(c == 0), stop=False)
                r = te.matmul(psum[0:16, :], ctx[f"wglo{c}"],
                              ctx[f"xnT{c}"][:, 512*sl:512*(sl+1)],
                              start=False, stop=(c == NCH - 1))
            return r
        v_mm = step(p, "tensor", "pe", mm)
        p.wait("vector", "pe", v_mm)
        f1_tok[bi] = step(p, "vector", "ve", lambda v, sl=sl, psum=psum: v.tensor_copy(
            ctx["lg"][:, 512*sl:512*(sl+1)], psum[0:16, :]))
    p.wait("sync", "pe", p.cur("pe"))
    for c in range(NCH):
        dma_t(p, "sync", ctx[f"xnT{c}"][:, 0:NTOK], xnlo_dram[:, 128*c:128*(c+1)], "dx")
    p.wait_all("tensor", "dx")
    for sl in range(NTOK // 512):
        bi = nb[0] % 2; psum = banks[bi]; nb[0] += 1
        p.wait("tensor", "ve", f1_tok[bi])

        def mm(te, sl=sl, psum=psum):
            r = None
            for c in range(NCH):
                r = te.matmul(psum[0:16, :], ctx[f"wg{c}"],
                              ctx[f"xnT{c}"][:, 512*sl:512*(sl+1)],
                              start=(c == 0), stop=(c == NCH - 1))
            return r
        v_mm = step(p, "tensor", "pe", mm)
        p.wait("vector", "pe", v_mm)
        f1_tok[bi] = step(p, "vector", "ve", lambda v, sl=sl, psum=psum: v.tensor_add(
            ctx["lg"][:, 512*sl:512*(sl+1)], ctx["lg"][:, 512*sl:512*(sl+1)],
            psum[0:16, :]))
    p.wait("sync", "pe", p.cur("pe"))
    v_lg = p.cur("ve")
    p.wait("scalar", "ve", v_lg)
    v_eg = step(p, "scalar", "se", lambda s: s.activation(ctx["egf"], ctx["lg"], AF.Exp))
    p.wait("tensor", "se", v_eg)
    for sl in range(NTOK // 512):
        bi = nb[0] % 2; psum = banks[bi]; nb[0] += 1
        p.wait("tensor", "ve", f1_tok[bi])

        def mm(te, sl=sl, psum=psum):
            return te.matmul(psum[0:1, :], ctx["onesf"][:, :],
                             ctx["egf"][:, 512*sl:512*(sl+1)], start=True, stop=True)
        v_mm = step(p, "tensor", "pe", mm)
        p.wait("vector", "pe", v_mm)
        f1_tok[bi] = step(p, "vector", "ve", lambda v, sl=sl, psum=psum: v.tensor_copy(
            ctx["zrow"][:, 512*sl:512*(sl+1)], psum[0:1, :]))
    v_z = p.cur("ve")
    p.wait("scalar", "ve", v_z)
    v_lz = step(p, "scalar", "se", lambda s: s.activation(
        ctx["zrow"], ctx["zrow"], AF.Ln))
    p.wait("sync", "se", v_lz)
    dma(p, "sync", lnz_dram.ap(), ctx["zrow"], "dx")
    p.wait_all("sync", "dx")
    v_zb = dma(p, "sync", ctx["lnzbc"], lnz_dram.ap().partition_broadcast(16), "dx")
    p.waitt("vector", v_zb)
    v_M = step(p, "vector", "ve", lambda v: v.tensor_sub(ctx["M"], ctx["lg"],
                                                         ctx["lnzbc"]))
    p.wait("gpsimd", "ve", v_M)
    tok_jt = dma(p, "gpsimd", ctx["jtile"], cd["jtile"].ap(), "dw")
    p.waitt("vector", tok_jt)

    def red(v):
        ap = ctx["M"].rearrange("l (b j q) -> l b q j", b=NBLK, j=16)
        return v.tensor_reduce(ctx["mx"].ap().rearrange("l (b q) -> l b q", b=NBLK),
                               ap, AX.X, A.max)
    step(p, "vector", "ve", red)

    def eqf(v):
        apM = ctx["M"].rearrange("l (b j q) -> l b j q", b=NBLK, j=16)
        apmx = ctx["mx"].ap().rearrange("l (b q) -> l b () q", b=NBLK)
        apmx = apmx.broadcast_to([16, NBLK, 16, NPOS // NBLK])
        return v.tensor_tensor(
            ctx["eq"][0:16, :].rearrange("l (b j q) -> l b j q", b=NBLK, j=16),
            apM, apmx, A.is_ge)
    step(p, "vector", "ve", eqf)

    def candf(v):
        v.tensor_scalar(ctx["cand"][0:16, :], ctx["jtile"], JBIG, None, A.subtract)
        v.tensor_mul(ctx["cand"][0:16, :], ctx["cand"][0:16, :], ctx["eq"][0:16, :])
        return v.tensor_scalar(ctx["cand"][0:16, :], ctx["cand"][0:16, :],
                               JBIG, None, A.add)
    step(p, "vector", "ve", candf)

    def redmin(v):
        ap = ctx["cand"][0:16, :].rearrange("l (b j q) -> l b q j", b=NBLK, j=16)
        return v.tensor_reduce(ctx["idxf"].ap().rearrange("l (b q) -> l b q", b=NBLK),
                               ap, AX.X, A.min)
    step(p, "vector", "ve", redmin)
    step(p, "vector", "ve", lambda v: v.tensor_copy(ctx["eq"][0:16, 0:NPOS],
                                                    ctx["idxf"][:, :]))
    p.wait("sync", "ve", p.cur("ve"))
    dma(p, "sync", idx_sc.ap(), ctx["eq"][0:16, 0:NPOS], "dx")
    p.wait_all("sync", "dx")
    for b in range(NBLK):
        dma_t(p, "sync", ctx[f"idxT{b}"][:, 0:16], idx_sc[:, 128*b:128*(b+1)], "dx")
    p.wait_all("vector", "dx")
    for b in range(NBLK):
        def rtf(v, b=b):
            v.tensor_scalar(ctx[f"rt{b}"][:, :], ctx[f"idxT{b}"][:, 0:16],
                            128.0, None, A.mult)
            v.tensor_scalar_add(ctx[f"rt{b}"][:, :], ctx[f"rt{b}"][:, :],
                                ctx["base"][:, b:b+1])
            return v.tensor_copy(ctx[f"rtu{b}"][:, :], ctx[f"rt{b}"][:, :])
        step(p, "vector", "ve", rtf)
    v_rt = p.cur("ve")
    st = {}

    def emit_f1_tail(feed):
        p.wait("gpsimd", "ve", v_rt)
        gbufs = ([ctx["AR3"].ap()[:, 12288 + 1024*i:12288 + 1024*(i+1)]
                  for i in range(4)] +
                 [ctx["AR3"].ap()[:, 4096 + 1024*i:4096 + 1024*(i+1)]
                  for i in range(4)])
        psb_slots = ([(ctx["ps_big"].ap().bitcast(BF16), 1024 * i)
                      for i in range(4)] +
                     [(ctx["ps_wo"].ap().bitcast(BF16), 1024 * i)
                      for i in range(2)])
        tp_toks = {}
        ev_toks = {}
        gi = 0
        for l in range(16):
            for b in range(NBLK):
                tok0 = l * NPOS + b * 128
                buf = gbufs[gi % 8]
                if gi >= 8:
                    p.wait("gpsimd", "pe", tp_toks[gi - 8])
                tg = dma_ind(p, "gpsimd", buf, xnF_dram.ap(),
                             ctx[f"rtu{b}"][:, l:l+1], f"dg{gi % 2}")
                p.waitt("tensor", tg)
                sb_, soff = psb_slots[gi % 6]
                if gi >= 6:
                    p.wait("tensor", "ve", ev_toks[gi - 6])

                def tps(te, buf=buf, sb_=sb_, soff=soff):
                    r = None
                    for f in range(8):
                        r = te.transpose(sb_[:, soff + 128*f:soff + 128*(f+1)],
                                         buf[:, 128*f:128*(f+1)],
                                         ctx["ident128"])
                    return r
                tp_toks[gi] = step(p, "tensor", "pe", tps)
                p.wait("vector", "pe", tp_toks[gi])

                def ev(v, tok0=tok0, sb_=sb_, soff=soff):
                    r = None
                    for c in range(NCH):
                        r = v.tensor_copy(ctx[f"xnT{c}"][:, tok0:tok0+128],
                                          sb_[:, soff + 128*c:soff + 128*(c+1)])
                    return r
                ev_toks[gi] = step(p, "vector", "ve", ev)
                if gi % 2 == 1:
                    feed()
                gi += 1
        while feed():
            pass
        for c in range(NCH):
            dma(p, "gpsimd", ctx[f"wkv{c}"][0:16, :],
                wd["Wkv_r"][128*c:128*c+16, :], "dw")
        p.wait_all("tensor", "dw")
        p.wait("tensor", "ve", p.cur("ve"))
        for l in range(16):
            bi = nb[0] % 2; psum = banks[bi]; nb[0] += 1
            p.wait("tensor", "ve", f1_tok[bi])

            def mmq(te, l=l, psum=psum):
                r = None
                for c in range(NCH):
                    r = te.matmul(psum[0:64, 0:NPOS],
                                  ctx[f"wq{c}"][:, 64*l:64*(l+1)],
                                  ctx[f"xnT{c}"][:, NPOS*l:NPOS*(l+1)],
                                  start=(c == 0), stop=(c == NCH - 1))
                return r
            v_mm = step(p, "tensor", "pe", mmq)
            p.wait("vector", "pe", v_mm)
            f1_tok[bi] = step(p, "vector", "ve",
                              lambda v, l=l, psum=psum: v.tensor_scalar_add(
                ctx["qall"][0:64, NPOS*l:NPOS*(l+1)], psum[0:64, 0:NPOS],
                ctx["bq_t"][:, l:l+1]))
            bi = nb[0] % 2; psum = banks[bi]; nb[0] += 1
            p.wait("tensor", "ve", f1_tok[bi])

            def mmkv(te, l=l, psum=psum):
                r = None
                for c in range(NCH):
                    r = te.matmul(psum[:, 0:NPOS],
                                  ctx[f"wkv{c}"][:, 128*l:128*(l+1)],
                                  ctx[f"xnT{c}"][:, NPOS*l:NPOS*(l+1)],
                                  start=(c == 0), stop=(c == NCH - 1))
                return r
            v_mm = step(p, "tensor", "pe", mmkv)
            p.wait("vector", "pe", v_mm)
            f1_tok[bi] = step(p, "vector", "ve",
                              lambda v, l=l, psum=psum: v.tensor_scalar_add(
                ctx["kvall"][:, NPOS*l:NPOS*(l+1)], psum[:, 0:NPOS],
                ctx["bkv_t"][:, l:l+1]))
        step(p, "vector", "ve", lambda v: v.tensor_mul(
            ctx["tt"][0:64, :], ctx["qall"][0:64, :], ctx["kvall"][0:64, :]))
        p.wait("tensor", "ve", p.cur("ve"))
        for sl in range(8):
            bi = nb[0] % 2; psum = banks[bi]; nb[0] += 1
            p.wait("tensor", "ve", f1_tok[bi])

            def mm(te, sl=sl, psum=psum):
                return te.matmul(psum[0:1, :], ctx["onesc"][0:64, :],
                                 ctx["tt"][0:64, 512*sl:512*(sl+1)],
                                 start=True, stop=True)
            v_mm = step(p, "tensor", "pe", mm)
            p.wait("vector", "pe", v_mm)
            f1_tok[bi] = step(p, "vector", "ve",
                              lambda v, sl=sl, psum=psum: v.tensor_copy(
                ctx["zrow"][:, 512*sl:512*(sl+1)], psum[0:1, :]))
        p.wait("sync", "ve", p.cur("ve"))
        dma(p, "sync", ag1_in.ap().rearrange("a q -> () (a q)")[:, 0:4096],
            ctx["zrow"], "dx")
        dma(p, "sync", ag1_in[16:32, :], ctx["idxf"][:, :], "dx")
        p.wait_all("gpsimd", "dx")

        h_cc, v_cc1 = p.inc("cc", 1)
        st["v_cc1"] = v_cc1
        p.emit("gpsimd", lambda e, h=h_cc: e.collective_compute(
            "AllGather", A.bypass, replica_groups=[list(range(NC))],
            ins=[ag1_in.ap().opt()], outs=[ag1_out.ap().opt()]).then_inc(h, 1))

    def emit_p1b():
        p.wait("gpsimd", "cc", st["v_cc1"])
        v_w16l = dma(p, "gpsimd", ctx["w16"][:, :], ag1_in[0:16, :], "dw")
        v_g1 = dma(p, "gpsimd", ctx["attG"].rearrange("l (s q) -> l s q", s=8),
                   ag1_out.ap().rearrange("(s r) q -> s r q", r=32)[:, 0:16, :]
                   .rearrange("s l q -> l s q"), "dw")
        p.waitt("vector", v_g1)
        step(p, "vector", "ve", lambda v: v.reduce_max(ctx["zf"][0:16, :],
                                                       ctx["attG"], AX.X))
        v_m = step(p, "vector", "ve", lambda v: v.tensor_scalar_mul(
            ctx["zf"][0:16, :], ctx["zf"][0:16, :], -0.125))
        p.wait("scalar", "ve", v_m)
        v_e = step(p, "scalar", "se", lambda s: s.activation(
            ctx["attG"], ctx["attG"], AF.Exp, bias=ctx["zf"][0:16, :], scale=0.125))
        p.wait("vector", "se", v_e)
        step(p, "vector", "ve", lambda v: v.reduce_sum(ctx["tvar"][0:16, :],
                                                       ctx["attG"], AX.X))
        v_rz = step(p, "vector", "ve", lambda v: v.reciprocal(
            ctx["tln"][0:16, :], ctx["tvar"][0:16, :]))
        p.wait("scalar", "ve", v_rz)
        p.waitt("scalar", v_w16l)
        v_e2 = step(p, "scalar", "se", lambda s: s.activation(
            ctx["w16"][:, :], ctx["w16"][:, :], AF.Exp, bias=ctx["zf"][0:16, :],
            scale=0.125))
        p.wait("vector", "se", v_e2)

        def normw(v):
            v.tensor_scalar_mul(ctx["w16"][:, :], ctx["w16"][:, :],
                                ctx["tln"][0:16, :])
            return v.tensor_copy(ctx["w16b"][:, :], ctx["w16"][:, :])
        v_w16 = step(p, "vector", "ve", normw)
        p.wait("tensor", "ve", v_w16)
        for l in range(16):
            psum = banks[nb[0] % 2]; nb[0] += 1
            p.wait("tensor", "ve", p.cur("ve"))

            def mmb(te, l=l, psum=psum):
                return te.matmul(psum[0:64, 0:NPOS],
                                 ctx["onehot16"][:, 64*l:64*(l+1)],
                                 ctx["w16b"][:, :], start=True, stop=True)
            v_mm = step(p, "tensor", "pe", mmb)
            p.wait("vector", "pe", v_mm)
            step(p, "vector", "ve", lambda v, l=l, psum=psum: v.tensor_mul(
                ctx["tt"][0:64, NPOS*l:NPOS*(l+1)],
                ctx["kvall"][64:128, NPOS*l:NPOS*(l+1)], psum[0:64, 0:NPOS]))
        v_av = p.cur("ve")
        p.wait("sync", "ve", v_av)
        dma(p, "sync", ag2_in.ap(), ctx["tt"][0:64, :], "dx")
        p.wait_all("gpsimd", "dx")
        h_cc, v_cc2 = p.inc("cc", 1)
        st["v_cc2"] = v_cc2
        p.emit("gpsimd", lambda e, h=h_cc: e.collective_compute(
            "AllGather", A.bypass, replica_groups=[list(range(NC))],
            ins=[ag2_in.ap().opt()], outs=[ag2_out.ap().opt()]).then_inc(h, 1))

    def make_ln_feeder(bt, alt=False):
        stat_m = ctx[f"stat_m{bt}"]
        stat_r = ctx[f"stat_r{bt}"]
        if alt:
            xas = [ctx["AR3"].ap().bitcast(F32)[:, 2048:3072],
                   ctx["AR3"].ap().bitcast(F32)[:, 3072:4096]]
            xns = [ctx["AR3"].ap()[:, 8192:9216], ctx["AR3"].ap()[:, 9216:10240]]
            xqs = [ctx["AR3"].ap()[:, 10240:11264], ctx["AR3"].ap()[:, 11264:12288]]
        else:
            xas = [ctx["xa0"], ctx["xa1"]]
            xns = [ctx["xn0"], ctx["xn1"]]
            xqs = [None, None]
        state = {"t": 0}
        lnst = {}
        st["lnst"] = lnst

        def feed():
            t = state["t"]
            if t >= 16:
                return False
            if t == 0:
                if not alt:
                    p.wait("scalar", "pe", p.cur("pe"))
                p.wait("scalar", "ve", p.cur("ve"))
                p.wait("sync", "ve", p.cur("ve"))
                p.wait_all("sync", "dx")
            xa = xas[t % 2]
            xn = xns[t % 2]
            if alt:
                eng, fam = "scalar", "ds"
            else:
                eng, fam = ("scalar", "ds") if t % 2 == 0 else ("sync", "dx")
            if t >= 2:
                p.waitt(eng, lnst[t - 2])
            v_load = dma(p, eng, xa, x_own[bt, 128*t:128*(t+1), :], fam)
            p.waitt("vector", v_load)
            p.waitt("scalar", v_load)
            v_no = ln_tile(xa, xn, m_ap=stat_m[:, t:t+1], r_ap=stat_r[:, t:t+1],
                           par=t % 2, pfx="l", xq=xqs[t % 2])
            p.wait("sync", "ve", v_no)
            lnst[t] = dma(p, "sync", xn_dram[128*t:128*(t+1), :], xn, "dx")
            state["t"] = t + 1
            return True
        return feed

    def emit_a_mid(bt, wo_hi_preloaded=False):
        p.wait("sync", "pe", p.cur("pe"))
        for tok in st.get("lnst", {}).values():
            p.waitt("sync", tok)
        for c in range(NCH):
            dma_t(p, "sync", ctx[f"xnT{c}"][:, 0:S], xn_dram[:, 128*c:128*(c+1)], "dx")
        p.wait_all("tensor", "dx")

        p.wait("gpsimd", "pe", p.cur("pe"))
        p.wait("gpsimd", "ve", p.cur("ve"))
        p.wait("gpsimd", "se", p.cur("se"))
        p.wait_all("gpsimd", "dx")
        for h in range(8):
            dma(p, "gpsimd", ctx[f"w_o{h}"][0:64, :], wd["W_O"][64*h:64*(h+1), :], "dw")
        p.wait_all("vector", "dw")

        def wo_zero(v):
            r = None
            for h in range(8):
                r = v.memset(ctx[f"w_o{h}"][64:128, :], 0.0)
            return r
        v_woz = step(p, "vector", "ve", wo_zero)

        def au_zero(v):
            r = None
            for hh in range(8):
                r = v.memset(ctx[f"attnU128_{hh}"][64:128, :], 0.0)
            return r
        v_auz = step(p, "vector", "ve", au_zero)

        p.wait("gpsimd", "ve", v_auz)
        krp_toks = [dma(p, "gpsimd", ctx[f"w_krp{c}"][:, :],
                        wd["W_KRP"][128*c:128*(c+1), :], "dw")
                    for c in range(NCH)]

        nbv = [0]
        bank_tok = {0: ("ve", p.cur("ve")), 1: ("ve", p.cur("ve"))}
        st["nbv"] = nbv; st["bank_tok"] = bank_tok

        def proj(dst_fn, w_fn, nchunks, rows=128, eng="vector"):
            bi = nbv[0] % 2; psum = banks[bi]; nbv[0] += 1
            sem, val = bank_tok[bi]
            p.wait("tensor", sem, val)

            def mms(te, psum=psum):
                r = None
                for i in range(nchunks):
                    wap, xap = w_fn(i)
                    r = te.matmul(psum[0:rows, :], wap, xap,
                                  start=(i == 0), stop=(i == nchunks - 1))
                return r
            v_mm = step(p, "tensor", "pe", mms)
            p.wait(eng, "pe", v_mm)
            sname = "ve" if eng == "vector" else "se"
            tok = step(p, eng, sname, lambda e, psum=psum: dst_fn(e, psum))
            bank_tok[bi] = (sname, tok)
        st["proj"] = proj

        for name, wpfx in (("cqT", "w_dq"), ("ckvT", "w_dkv")):
            for h2 in range(2):
                for qc in range(NQT):
                    def ev(v, psum, name=name, h2=h2, qc=qc):
                        return v.tensor_copy(ctx[name + str(h2)][:, SQ*qc:SQ*(qc+1)],
                                             psum[:, :])
                    proj(ev, lambda i, h2=h2, qc=qc, wpfx=wpfx: (
                        ctx[f"{wpfx}{i}"][:, 128*h2:128*(h2+1)],
                        ctx[f"xnT{i}"][:, SQ*qc:SQ*(qc+1)]), NCH)
        for t in krp_toks:
            p.waitt("tensor", t)
        for qc in range(NQT):
            def ev(v, psum, qc=qc):
                return v.tensor_copy(ctx["krSt"][:, SQ*qc:SQ*(qc+1)], psum[64:96, :])
            proj(ev, lambda i, qc=qc: (ctx[f"w_krp{i}"][:, :],
                                       ctx[f"xnT{i}"][:, SQ*qc:SQ*(qc+1)]), NCH)
        v_kr_done = p.cur("ve")
        p.wait_all("gpsimd", "dx")
        p.wait("gpsimd", "ve", v_kr_done)
        p.wait("gpsimd", "se", p.cur("se"))
        p.wait("gpsimd", "pe", p.cur("pe"))
        t_tc = dma(p, "gpsimd", ctx["trig_c"][:, :], cd["trig_c"].ap(), "dw")
        t_ts = dma(p, "gpsimd", ctx["trig_s"][:, :], cd["trig_s"].ap(), "dw")
        st["trig_toks"] = (t_tc, t_ts)
        t_k1 = dma(p, "gpsimd", ctx["krPr"][0:16, :], ctx["krSt"][16:32, :], "dw")
        t_k2 = dma(p, "gpsimd", ctx["krPr"][16:32, :], ctx["krSt"][0:16, :], "dw")
        for t in (t_tc, t_ts, t_k1, t_k2):
            p.waitt("vector", t)

        def rope_kr(v):
            v.tensor_mul(ctx["krSt"][:, :], ctx["krSt"][:, :],
                         ctx["trig_c"][64:96, :])
            v.tensor_mul(ctx["krPr"][:, :], ctx["krPr"][:, :],
                         ctx["trig_s"][64:96, :])
            return v.tensor_add(ctx["krSt"][:, :], ctx["krSt"][:, :],
                                ctx["krPr"][:, :])
        step(p, "vector", "ve", rope_kr)

        hi_end = 10 if wo_hi_preloaded else 16
        st["wo_hi_toks"] = [dma(p, "gpsimd", ctx[f"w_o{h}"][0:64, :],
                                wd["W_O"][64*h:64*(h+1), :], "dw")
                            for h in range(8, hi_end)]

    def emit_a_back(bt, tail_hook=None):
        proj = st["proj"]
        t_tc, t_ts = st["trig_toks"]
        bank_tok = st["bank_tok"]
        bank_tok[0] = bank_tok[1] = ("ve", p.cur("ve"))
        p.wait_all("vector", "dx")
        p.wait_all("scalar", "dx")

        def kr_fill(v):
            r = None
            for hh in range(NCH):
                if hh > 0:
                    v.tensor_copy(ctx[f"khead{hh}"][64:96, :], ctx["krSt"][:, :])
                r = v.memset(ctx[f"khead{hh}"][96:128, :], 0.0)
            return r
        step(p, "vector", "ve", kr_fill)

        for g in range(2):
            se_g0 = p.cur("se")
            if g == 1:
                for t in st.pop("wo_hi_toks", []) + st.pop("wo_pre_toks", []):
                    p.waitt("vector", t)

                def wo_zero_hi(v):
                    r = None
                    for h in range(8, 16):
                        r = v.memset(ctx[f"w_o{h}"][64:128, :], 0.0)
                    return r
                step(p, "vector", "ve", wo_zero_hi)
                p.wait_all("gpsimd", "dx")
                p.wait("gpsimd", "pe", p.cur("pe"))
                p.wait("gpsimd", "se", p.cur("se"))
                p.wait("gpsimd", "ve", p.cur("ve"))
                t_tc = dma(p, "gpsimd", ctx["trig_c"][:, :], cd["trig_c"].ap(), "dw")
                t_ts = dma(p, "gpsimd", ctx["trig_s"][:, :], cd["trig_s"].ap(), "dw")
            for hh in range(NCH):
                h = 8 * g + hh
                for qc in range(NQT):
                    def ev(s, psum, hh=hh, qc=qc):
                        return s.activation(
                            ctx[f"khead{hh}"][0:64, SQ*qc:SQ*(qc+1)],
                            psum[0:64, :], AF.Copy)
                    proj(ev, lambda l, h=h, qc=qc: (
                        ctx[f"w_uk{l}"][:, 64*h:64*h+128],
                        ctx[f"ckvT{l}"][:, SQ*qc:SQ*(qc+1)]), 2, eng="scalar")
            for hh in range(NCH):
                h = 8 * g + hh
                for qc in range(NQT):
                    def ev(v, psum, hh=hh, qc=qc):
                        return v.tensor_copy(
                            ctx[f"qhead{hh}"][0:96, SQ*qc:SQ*(qc+1)], psum[0:96, :])
                    proj(ev, lambda l, h=h, qc=qc: (
                        ctx[f"w_qp{l}"][:, 96*h:96*h+128],
                        ctx[f"cqT{l}"][:, SQ*qc:SQ*(qc+1)]), 2)
                v_ev4 = p.cur("ve")
                pq = ctx[f"pQ{hh % 4}"]
                p.wait("gpsimd", "ve", v_ev4)
                p.wait("gpsimd", "se", se_g0)
                t1 = dma(p, "gpsimd", pq[0:16, :],
                         ctx[f"qhead{hh}"][80:96, :], "dw")
                t2 = dma(p, "gpsimd", pq[16:32, :],
                         ctx[f"qhead{hh}"][64:80, :], "dw")
                for t in (t_tc, t_ts, t1, t2):
                    p.waitt("vector", t)

                def rope_q(v, hh=hh, pq=pq):
                    v.tensor_mul(ctx[f"qhead{hh}"][64:96, :],
                                 ctx[f"qhead{hh}"][64:96, :],
                                 ctx["trig_c"][64:96, :])
                    v.tensor_mul(pq[:, :], pq[:, :], ctx["trig_s"][64:96, :])
                    return v.tensor_add(ctx[f"qhead{hh}"][64:96, :],
                                        ctx[f"qhead{hh}"][64:96, :], pq[:, :])
                step(p, "vector", "ve", rope_q)
            p.wait("scalar", "ve", p.cur("ve"))
            for kt in range(NKT):
                def ev(s, psum, kt=kt):
                    dst = ctx[f"vh{kt}"].rearrange("k (h e) -> k h e", h=8)
                    src = psum.ap().rearrange("k (h e) -> k h e", h=8)
                    return s.activation(dst[:, :, 0:64], src[:, :, :], AF.Copy)
                proj(ev, lambda l, kt=kt, g=g: (
                    ctx[f"ckvT{l}"][:, 128*kt:128*(kt+1)],
                    ctx[f"w_uv{l}"][:, 512*g:512*(g+1)]), 2, eng="scalar")

            def vh_ones(v):
                r = None
                for kt in range(NKT):
                    dst = ctx[f"vh{kt}"].rearrange("k (h e) -> k h e", h=8)
                    r = v.memset(dst[:, :, 64:65], 1.0)
                return r
            step(p, "vector", "ve", vh_ones)
            v_rope_done = p.cur("ve")
            p.wait("tensor", "ve", v_rope_done)
            p.wait("tensor", "se", p.cur("se"))

            slots = [(ctx["ps_big"], 0), (ctx["ps_big"], 1024), (ctx["ps_wo"], 0)]
            psl = [(ctx["P0"], 0), (ctx["P0"], 1024), (ctx["P1"], 0), (ctx["P1"], 1024)]
            items = []
            for hh in range(8):
                for qt in range(NQT):
                    for k2 in range(2 * (qt + 1)):
                        items.append((hh, qt, k2))
            NI = len(items)
            v_sc = [0] * NI; v_ex = [0] * NI; v_cm = [0] * NI; v_pv = [0] * NI
            v_eu_prev = [p.cur("ve")]
            for i in range(NI + 1):
                if i < NI:
                    hh, qt, k2 = items[i]
                    kts = [2 * k2, 2 * k2 + 1]
                    sb, soff = slots[i % 3]
                    if i >= 3:
                        p.wait("tensor", "se", v_ex[i - 3])

                    def smms(te, kts=kts, hh=hh, qt=qt, sb=sb, soff=soff):
                        r = None
                        for t, kt in enumerate(kts):
                            r = te.matmul(sb[:, soff + SQ*t:soff + SQ*(t+1)],
                                          ctx[f"khead{hh}"][:, 128*kt:128*(kt+1)],
                                          ctx[f"qhead{hh}"][:, SQ*qt:SQ*(qt+1)],
                                          start=True, stop=True)
                        return r
                    v_sc[i] = step(p, "tensor", "pe", smms)
                    pb, poff = psl[i % 4]
                    p.wait("scalar", "pe", v_sc[i])
                    if i >= 4:
                        p.wait("scalar", "pe", v_pv[i - 4])

                    def expf(s, sb=sb, soff=soff, pb=pb, poff=poff):
                        return s.activation(pb[:, poff:poff + 1024],
                                            sb[:, soff:soff + 1024], AF.Exp,
                                            scale=INV_SQRT_DQK)
                    v_ex[i] = step(p, "scalar", "se", expf)
                    p.wait("vector", "se", v_ex[i])

                    def clipmask(v, pb=pb, poff=poff, kts=kts, qt=qt):
                        r = v.tensor_scalar(pb[:, poff:poff + 1024],
                                            pb[:, poff:poff + 1024],
                                            CLIP_HI, CLIP_LO, A.min, A.max)
                        for t, kt in enumerate(kts):
                            d = kt - 4 * qt
                            if 0 <= d < 4:
                                r = v.tensor_mul(pb[:, poff + SQ*t:poff + SQ*(t+1)],
                                                 pb[:, poff + SQ*t:poff + SQ*(t+1)],
                                                 ctx[f"dmask{d}"][:, :])
                        return r
                    v_cm[i] = step(p, "vector", "ve", clipmask)
                if i >= 1:
                    hh1, qt1, k21 = items[i - 1]
                    kts1 = [2 * k21, 2 * k21 + 1]
                    pb1, poff1 = psl[(i - 1) % 4]
                    p.wait("tensor", "ve", v_cm[i - 1])
                    if k21 == 0:
                        p.wait("tensor", "ve", v_eu_prev[0])

                    def pvmms(te, kts1=kts1, k21=k21, qt1=qt1, hh1=hh1,
                              pb1=pb1, poff1=poff1):
                        r = None
                        for t, kt in enumerate(kts1):
                            vsl = ctx["vh_t"].ap()[:, 520*kt + 65*hh1:
                                                   520*kt + 65*hh1 + 128]
                            r = te.matmul(ctx["ps_pv"][:, :], vsl,
                                          pb1[:, poff1 + SQ*t:poff1 + SQ*(t+1)],
                                          start=(k21 == 0 and t == 0),
                                          stop=(k21 == 2 * (qt1 + 1) - 1 and t == 1))
                        return r
                    v_pv[i - 1] = step(p, "tensor", "pe", pvmms)
                    if k21 == 2 * (qt1 + 1) - 1:
                        p.wait("vector", "pe", v_pv[i - 1])
                        v_eu = step(p, "vector", "ve",
                                    lambda v, hh1=hh1, qt1=qt1: v.tensor_copy(
                                        ctx[f"attnU{hh1}"][:, SQ*qt1:SQ*(qt1+1)],
                                        ctx["ps_pv"][0:65, :]))
                        v_eu_prev[0] = v_eu
                        if qt1 == NQT - 1:
                            h = 8 * g + hh1
                            p.wait("gpsimd", "ve", v_eu)
                            dma(p, "gpsimd", rsum_dram[h:h+1, :],
                                ctx[f"attnU{hh1}"][64:65, :], "dw")

            p.wait("vector", "pe", p.cur("pe"))
            v_ms = step(p, "vector", "ve", lambda v: v.memset(ctx["rsumS"], 1.0))
            p.wait("sync", "ve", v_ms)
            p.wait_all("sync", "dw")
            t_rs = dma(p, "sync", ctx["rsumS8"], rsum_dram[8*g:8*g+8, :], "dx")
            p.waitt("scalar", t_rs)
            p.wait("scalar", "ve", p.cur("ve"))

            def recf(s):
                lnf = ctx["ot0"][0:16, :]
                with nc.allow_low_precision("bf16 softmax normalizer"):
                    s.activation(lnf, ctx["rsumS"][:, 0:1024], AF.Ln)
                    s.activation(ctx["rec16n"][:, 0:1024], lnf, AF.Exp,
                                 scale=-1.0)
                    s.activation(lnf, ctx["rsumS"][:, 1024:2048], AF.Ln)
                    return s.activation(ctx["rec16n"][:, 1024:2048], lnf,
                                        AF.Exp, scale=-1.0)
            v_rec = step(p, "scalar", "se", recf)
            p.wait("sync", "se", v_rec)
            dma(p, "sync", rec_dram[8*g:8*g+8, :], ctx["rec8n"], "dx")
            p.wait_all("sync", "dx")
            nb_bufs = [ctx["AR3"].ap()[0:64, 0:2048], ctx["AR3"].ap()[0:64, 2048:4096]]
            nb_tok = [p.cur("ve"), p.cur("ve")]
            v_nm_last = p.cur("ve")
            for hh in range(8):
                h = 8 * g + hh
                buf = nb_bufs[hh % 2]
                p.wait("sync", "ve", nb_tok[hh % 2])
                t_b = dma(p, "sync", buf,
                          rec_dram[h:h+1, :].partition_broadcast(64), "dx")
                p.waitt("vector", t_b)
                v_nm = step(p, "vector", "ve", lambda v, hh=hh, buf=buf: v.tensor_mul(
                    ctx[f"attnU{hh}"][0:64, :], ctx[f"attnU{hh}"][0:64, :], buf))
                nb_tok[hh % 2] = v_nm
                v_nm_last = v_nm

            p.wait("tensor", "ve", v_nm_last)
            p.wait("tensor", "se", v_ex[NI - 1])
            accs = [ctx["ps_wo"].ap()[:, 0:1024], ctx["ps_big"].ap()[:, 0:1024]]
            wo_ev = [p.cur("ve"), p.cur("ve")]
            st_tok = {}
            for q128 in range(16):
                acc = accs[q128 % 2]
                p.wait("tensor", "ve", wo_ev[q128 % 2])

                def womms(te, q128=q128, g=g, acc=acc):
                    r = None
                    for half in range(2):
                        for hh in range(8):
                            h = 8 * g + hh
                            r = te.matmul(acc[:, 512*half:512*(half+1)],
                                          ctx[f"attnU128_{hh}"][:,
                                                                128*q128:128*(q128+1)],
                                          ctx[f"w_o{h}"][:, 512*half:512*(half+1)],
                                          start=(hh == 0), stop=(hh == 7))
                    return r
                v_wo = step(p, "tensor", "pe", womms)
                ot = ctx["ot0"] if q128 % 2 == 0 else ctx["ot1"]
                if g == 0:
                    p.wait("vector", "pe", v_wo)
                    if q128 >= 2:
                        p.waitt("vector", st_tok[q128 - 2])
                    v_ev = step(p, "vector", "ve",
                                lambda v, ot=ot, acc=acc: v.tensor_copy(ot, acc))
                else:
                    if q128 >= 2:
                        p.waitt("sync", st_tok[q128 - 2])
                    v_ld = dma(p, "sync", ot, attn_dram[128*q128:128*(q128+1), :], "dx")
                    p.waitt("vector", v_ld)
                    p.wait("vector", "pe", v_wo)
                    v_ev = step(p, "vector", "ve", lambda v, ot=ot, acc=acc: v.tensor_add(
                        ot, ot, acc))
                wo_ev[q128 % 2] = v_ev
                p.wait("sync", "ve", v_ev)
                st_tok[q128] = dma(p, "sync", attn_dram[128*q128:128*(q128+1), :], ot, "dx")
                if g == 1 and tail_hook is not None:
                    tail_hook()
            if g == 1 and tail_hook is not None:
                while tail_hook():
                    pass
            p.wait_all("vector", "dx")
            p.wait_all("tensor", "dx")

    def make_e_pro_feeder(bt):
        avls = [ctx["avh"], ctx["AR3"].ap()[0:64, 4096:6144]]
        avhs = [ctx["avh_hi"], ctx["AR3"].ap()[64:128, 4096:6144]]
        eqls = [ctx["eq64"], ctx["AR3"].ap()[0:64, 6144:8192]]
        eqhs = [ctx["eq64_hi"], ctx["AR3"].ap()[64:128, 6144:8192]]
        state = {"k": 0, "toks": {}, "mul": {}}

        def issue_pair(c):
            s_ = c % 2
            if c >= 2:
                p.wait("gpsimd", "ve", state["mul"][c - 2])
                p.wait("scalar", "ve", state["mul"][c - 2])
            v_lo = dma(p, "gpsimd", avls[s_].rearrange("d (s q) -> d s q", s=8),
                       ag2_out.ap().rearrange("(s d) (ll q) -> s d ll q", d=64, ll=16)
                       [:, :, 2*c, :].rearrange("s d q -> d s q"), "dw")
            v_hi = dma(p, "scalar", avhs[s_].rearrange("d (s q) -> d s q", s=8),
                       ag2_out.ap().rearrange("(s d) (ll q) -> s d ll q", d=64, ll=16)
                       [:, :, 2*c+1, :].rearrange("s d q -> d s q"), "ds")
            v_el = dma(p, "gpsimd", eqls[s_],
                       eq_dram[2*c:2*c+1, :].partition_broadcast(64), "dw")
            v_eh = dma(p, "scalar", eqhs[s_],
                       eq_dram[2*c+1:2*c+2, :].partition_broadcast(64), "ds")
            state["toks"][c] = (v_lo, v_hi, v_el, v_eh)

        def feed():
            k = state["k"]
            if k > 9:
                return False
            if k == 0:
                p.wait("gpsimd", "cc", st["v_cc2"])
                p.wait_all("gpsimd", "dx")
                p.wait("gpsimd", "ve", p.cur("ve"))
                p.wait("gpsimd", "se", p.cur("se"))
                p.wait("gpsimd", "pe", p.cur("pe"))
                p.wait("scalar", "pe", p.cur("pe"))
                v_ig = dma(p, "gpsimd",
                           ctx["idxG"].rearrange("l (s q) -> l s q", s=8),
                           ag1_out.ap().rearrange("(s r) q -> s r q", r=32)
                           [:, 16:32, :].rearrange("s l q -> l s q"), "dw")
                p.waitt("vector", v_ig)
                v_eqb = step(p, "vector", "ve",
                             lambda v, bt=bt: v.tensor_single_scalar(
                    ctx["eq64"][0:16, :], ctx["idxG"], ctx["jcol"][:, bt:bt+1],
                    A.is_equal))
                state["v_eqb"] = v_eqb
                p.wait("tensor", "ve", v_eqb)
                p.wait("tensor", "se", p.cur("se"))
                for sl in range(4):
                    psum = banks[nb[0] % 2]; nb[0] += 1
                    p.wait("tensor", "ve", p.cur("ve"))

                    def mm(te, sl=sl, psum=psum):
                        return te.matmul(psum[0:1, :], ctx["onesc"][0:16, 0:1],
                                         ctx["eq64"][0:16, 512*sl:512*(sl+1)],
                                         start=True, stop=True)
                    v_mm = step(p, "tensor", "pe", mm)
                    p.wait("vector", "pe", v_mm)
                    step(p, "vector", "ve",
                         lambda v, sl=sl, psum=psum: v.tensor_copy(
                             ctx["nsel"][:, 512*sl:512*(sl+1)], psum[0:1, :]))
                v_ns = step(p, "vector", "ve", lambda v: v.tensor_copy(
                    ctx["nselb"], ctx["nsel"]))
                p.wait("scalar", "ve", v_ns)
                dma(p, "scalar", eq_dram.ap(), ctx["eq64"][0:16, :], "ds")
                p.wait_all("gpsimd", "ds")
                p.wait_all("scalar", "ds")
                dma(p, "scalar", ctx["g_rep"], wd["g_rep"].ap(), "ds")
                dma(p, "scalar", ctx["b_rep"], wd["b_rep"].ap(), "ds")
            elif k == 1:
                p.wait("gpsimd", "ve", state["v_eqb"])
                p.wait("scalar", "ve", state["v_eqb"])
                issue_pair(0)
                issue_pair(1)
            else:
                c = k - 2
                s_ = c % 2
                for t in state["toks"][c]:
                    p.waitt("vector", t)

                def avmf(v, c=c, s_=s_):
                    v.tensor_mul(ctx[f"avmP{c}"][0:64, :], avls[s_], eqls[s_])
                    return v.tensor_mul(ctx[f"avmP{c}"][64:128, :], avhs[s_],
                                        eqhs[s_])
                state["mul"][c] = step(p, "vector", "ve", avmf)
                if c + 2 < NCH:
                    issue_pair(c + 2)
            state["k"] = k + 1
            return True
        return feed

    def emit_e(bt, feed=None, preload_wo_hi=False):
        stat_m = ctx[f"stat_m{bt}"]
        stat_r = ctx[f"stat_r{bt}"]
        p.wait("gpsimd", "pe", p.cur("pe"))
        p.wait("gpsimd", "ve", p.cur("ve"))
        p.wait_all("gpsimd", "dx")
        for c in range(NCH):
            dma(p, "gpsimd", ctx[f"wo_p{c}"][:, :],
                wd["Wo_f"][128*c:128*(c+1), :], "dw")
        if preload_wo_hi:
            st["wo_pre_toks"] = [dma(p, "gpsimd", ctx[f"w_o{h}"][0:64, :],
                                     wd["W_O"][64*h:64*(h+1), :], "dw")
                                 for h in range(10, 16)]
        p.wait_all("tensor", "dw")
        p.wait_all("tensor", "ds")
        p.wait("tensor", "ve", p.cur("ve"))
        def bmrf(v):
            v.tensor_mul(ctx["bmr"], stat_m[:, :], stat_r[:, :])
            return v.tensor_scalar_mul(ctx["bmr"], ctx["bmr"], -1.0)
        v_bmr = step(p, "vector", "ve", bmrf)
        p.wait("scalar", "ve", v_bmr)

        pse = [ctx["ps_wo"].ap()[:, :], ctx["ps_big"].ap()[:, 0:1024]]
        xts = [ctx["xt0"], ctx["xt1"]]
        ats = [ctx["at0"], ctx["at1"]]
        yts = [ctx["ytile"], ctx["yt1"]]
        v_bank = {0: p.cur("ve"), 1: p.cur("ve")}
        v_free = {0: p.cur("ve"), 1: p.cur("ve")}
        v_stats = {0: p.cur("ve"), 1: p.cur("ve")}
        st_out = {}
        e_toks = {}

        def e_first(tb):
            b = tb % 2
            slot = tb if tb < 8 else 15 - tb
            off = 0 if tb < 8 else 128
            scol = slot * 256 + off
            p.wait("gpsimd", "ve", v_free[b])
            v_x = dma(p, "gpsimd", xts[b], x_own[bt, 128*tb:128*(tb+1), :], "dw")
            v_a = dma(p, "gpsimd", ats[b], attn_dram[128*tb:128*(tb+1), :], "dw")
            p.wait("tensor", "ve", v_bank[b])

            def omms(te, scol=scol, acc=pse[b]):
                r = None
                for half in range(2):
                    for c in range(NCH):
                        te.matmul(acc[:, 512*half:512*(half+1)],
                                  ctx[f"avmP{c}"][:, scol:scol+128],
                                  ctx[f"wo_p{c}"][:, 512*half:512*(half+1)],
                                  start=(c == 0), stop=False)
                    r = te.matmul(acc[:, 512*half:512*(half+1)],
                                  ctx["nselb"][:, scol:scol+128],
                                  ctx["bo_sb"][:, 512*half:512*(half+1)],
                                  start=False, stop=True)
                return r
            v_om = step(p, "tensor", "pe", omms)
            p.waitt("scalar", v_x)
            if tb >= 2:
                p.waitt("scalar", st_out[tb - 2])

            def y0f(s, b=b, tb=tb):
                return s.activation(yts[b], xts[b], AF.Identity,
                                    scale=stat_r[:, tb:tb+1],
                                    bias=ctx["bmr"][:, tb:tb+1])
            v_y0 = step(p, "scalar", "se", y0f)
            p.wait("vector", "pe", v_om)
            p.wait("vector", "se", v_y0)

            def yaf(v, b=b):
                v.tensor_add(yts[b], yts[b], pse[b])
                return v.reduce_sum(ctx[f"ssum{b}"], yts[b], AX.X)
            v_y = step(p, "vector", "ve", yaf)
            v_bank[b] = v_y
            e_toks[tb] = (v_a, v_y)

        def e_second(tb):
            b = tb % 2
            v_a, v_y = e_toks[tb]
            p.wait("scalar", "ve", v_y)
            p.wait("scalar", "ve", v_stats[b])
            v_sq = step(p, "scalar", "se", lambda s, b=b: s.activation(
                ctx["xq_e"], yts[b], AF.Square, accum_out=ctx[f"ssq{b}"]))
            p.wait("vector", "se", v_sq)

            def statf(v, b=b):
                v.tensor_scalar_mul(ctx[f"tvar{b}"], ctx[f"ssum{b}"], 1.0/1024)
                v.tensor_mul(ctx[f"tmsq{b}"], ctx[f"tvar{b}"], ctx[f"tvar{b}"])
                return v.scalar_tensor_tensor(ctx[f"tmsq{b}"], ctx[f"ssq{b}"],
                                              1.0/1024, ctx[f"tmsq{b}"],
                                              A.mult, A.subtract)
            v_st2 = step(p, "vector", "ve", statf)
            v_stats[b] = v_st2
            p.wait("scalar", "ve", v_st2)

            def rstd(s, b=b):
                s.activation(ctx[f"tln{b}"], ctx[f"tmsq{b}"], AF.Ln,
                             bias=ctx["epsc"][:, :])
                return s.activation(ctx[f"tln{b}"], ctx[f"tln{b}"], AF.Exp,
                                    scale=-0.5)
            v_rs = step(p, "scalar", "se", rstd)
            p.wait("vector", "se", v_rs)

            def b2f(v, b=b):
                v.tensor_mul(ctx[f"bias2{b}"], ctx[f"tvar{b}"], ctx[f"tln{b}"])
                return v.tensor_scalar_mul(ctx[f"bias2{b}"], ctx[f"bias2{b}"],
                                           -1.0)
            v_b2 = step(p, "vector", "ve", b2f)
            p.wait("scalar", "ve", v_b2)
            v_f1 = step(p, "scalar", "se", lambda s, b=b: s.activation(
                yts[b], yts[b], AF.Identity, scale=ctx[f"tln{b}"],
                bias=ctx[f"bias2{b}"]))
            p.wait("vector", "se", v_f1)
            p.waitt("vector", v_a)

            def fin2(v, b=b):
                v.tensor_mul(yts[b], yts[b], ctx["g_rep"])
                v.tensor_add(yts[b], yts[b], ctx["b_rep"])
                v.tensor_add(yts[b], yts[b], xts[b])
                return v.tensor_add(yts[b], yts[b], ats[b])
            v_f2 = step(p, "vector", "ve", fin2)
            v_free[b] = v_f2
            p.wait("sync", "ve", v_f2)
            st_out[tb] = dma(p, "sync", out_own[bt, 128*tb:128*(tb+1), :],
                             yts[b], "dx")

        for it in range(17):
            if it < 16:
                e_first(it)
            if it >= 1:
                e_second(it - 1)
            if feed is not None:
                feed()
        if feed is not None:
            while feed():
                pass
        p.waitt("scalar", st_out[14])
        p.waitt("scalar", st_out[15])


    feed0 = make_ln_feeder(0)
    emit_f1_tail(feed0)
    emit_a_mid(0)
    emit_p1b()
    emit_a_back(0, tail_hook=make_e_pro_feeder(0))
    feed1 = make_ln_feeder(1, alt=True)
    emit_e(0, feed1, preload_wo_hi=True)
    emit_a_mid(1, wo_hi_preloaded=True)
    emit_a_back(1, tail_hook=make_e_pro_feeder(1))
    emit_e(1)

    p.wait_all("gpsimd", "dx")
    p.wait_all("gpsimd", "dw")
    p.wait_all("gpsimd", "dg")
    return p


def kernel(L0_attn_x, L0_attn_W_DQ, L0_attn_W_UQ, L0_attn_W_QR, L0_attn_W_DKV,
           L0_attn_W_UK, L0_attn_W_UV, L0_attn_W_KR, L0_attn_W_O,
           L0_ffn_attn_mask, L0_ffn_Wg, L0_ffn_Wq, L0_ffn_bq, L0_ffn_Wkv,
           L0_ffn_bkv, L0_ffn_Wo, L0_ffn_bo, L0_ffn_ln_g, L0_ffn_ln_b):
    x = np.ascontiguousarray(np.asarray(L0_attn_x, dtype=np.float32))
    cst = host_constants()

    nc = bass.Bass("TRN2", target_bir_lowering=False)
    p = build(nc)
    with nc.Block() as blk:
        p.build(blk)

    W_UQ = np.asarray(L0_attn_W_UQ, dtype=np.float32)
    W_QRp = permute_qr_cols(np.asarray(L0_attn_W_QR, dtype=np.float32))
    uqp = np.zeros((D_LAT, 1568), np.float32)
    for h in range(16):
        uqp[:, 96*h:96*h+64] = W_UQ[:, 64*h:64*h+64]
        uqp[:, 96*h+64:96*h+96] = W_QRp[:, 32*h:32*h+32]
    uqp[:, 1536:1568] = W_UQ[:, 0:32]
    W_UK = np.asarray(L0_attn_W_UK, dtype=np.float32)
    ukp = np.concatenate([W_UK, W_UK[:, 0:64]], axis=1)
    krp = np.zeros((D_MODEL, 128), np.float32)
    krp[:, 64:96] = permute_qr_cols(np.asarray(L0_attn_W_KR, dtype=np.float32))
    shared = {
        "W_DQ": np.ascontiguousarray(L0_attn_W_DQ, dtype=np.float32),
        "W_UQP": np.ascontiguousarray(uqp),
        "W_DKV": np.ascontiguousarray(L0_attn_W_DKV, dtype=np.float32),
        "W_UKP": np.ascontiguousarray(ukp),
        "W_UV": np.ascontiguousarray(L0_attn_W_UV, dtype=np.float32),
        "W_KRP": np.ascontiguousarray(krp),
        "W_O": np.ascontiguousarray(L0_attn_W_O, dtype=np.float32),
        "Wg": np.ascontiguousarray(L0_ffn_Wg, dtype=np.float32),
        "Wq_r": np.ascontiguousarray(np.asarray(L0_ffn_Wq, dtype=np.float32)
                                     .transpose(1, 0, 2).reshape(D_MODEL, 16 * 64)),
        "Wkv_r": np.ascontiguousarray(np.asarray(L0_ffn_Wkv, dtype=np.float32)
                                      .transpose(1, 0, 2).reshape(D_MODEL, 16 * 128)),
        "Wo_f": np.ascontiguousarray(np.asarray(L0_ffn_Wo, dtype=np.float32)
                                     .reshape(16 * 64, 1024)),
        "bq_td": np.ascontiguousarray(np.asarray(L0_ffn_bq, dtype=np.float32).T),
        "bkv_td": np.ascontiguousarray(np.asarray(L0_ffn_bkv, dtype=np.float32).T),
        "g_rep": np.ascontiguousarray(np.broadcast_to(
            np.asarray(L0_ffn_ln_g, dtype=np.float32)[None, :], (128, 1024))),
        "b_rep": np.ascontiguousarray(np.broadcast_to(
            np.asarray(L0_ffn_ln_b, dtype=np.float32)[None, :], (128, 1024))),
        "bo_row": np.ascontiguousarray(
            np.asarray(L0_ffn_bo, dtype=np.float32)[None, :]),
    }
    for k, v in cst.items():
        shared["c_" + k] = np.ascontiguousarray(v)

    in_maps = []
    for c in range(NC):
        blocks = [c, 15 - c]
        m = dict(shared)
        m["x_own"] = np.ascontiguousarray(x[2 * c:2 * c + 2])
        xp = np.empty((16, NBLK, 128, D_MODEL), np.float32)
        for bi, tb in enumerate(blocks):
            xp[:, bi] = x[:, 128 * tb:128 * (tb + 1), :]
        m["x_pos"] = xp
        basearr = np.zeros((128, NBLK), np.float32)
        for bi in range(NBLK):
            basearr[:, bi] = bi * 2048 + np.arange(128)
        m["c_base"] = basearr
        jc = np.zeros((16, 2), np.float32)
        jc[:, 0] = 2 * c
        jc[:, 1] = 2 * c + 1
        m["c_jcol"] = jc
        in_maps.append(m)

    res = run_bass_kernel_spmd(nc, in_maps, core_ids=list(range(NC)))
    out = np.concatenate([res.results[c]["out_own"] for c in range(NC)], axis=0)
    return out


# version 28
# speedup vs baseline: 1.0503x; 1.0444x over previous
import numpy as np
import ml_dtypes
import concourse.bass as bass
import concourse.mybir as mybir
from concourse.bass_core_rust import IndirectOffsetOnAxis
from concourse.bass_utils import run_bass_kernel_spmd

BF16 = mybir.dt.bfloat16
F32 = mybir.dt.float32
U32 = mybir.dt.uint32
AF = mybir.ActivationFunctionType
A = mybir.AluOpType
AX = mybir.AxisListType
bf = ml_dtypes.bfloat16

D_MODEL, N_HEADS, D_H, D_HR, D_LAT = 1024, 16, 64, 32, 256
S, B, NC = 2048, 16, 8
SQ = 512
NKT, NQT, NCH = 16, 4, 8
NBLK, NPOS = 2, 256
NTOK = 16 * NPOS
INV_SQRT_DQK = 1.0 / np.sqrt(96.0)
CLIP_HI = float(np.exp(np.float32(80.0) * INV_SQRT_DQK))
CLIP_LO = float(np.exp(np.float32(-80.0) * INV_SQRT_DQK))
EPS = 1e-5
JBIG = 16.0


class Prog:
    ENGINES = ("sync", "scalar", "vector", "tensor", "gpsimd")

    def __init__(self, nc):
        self.nc = nc
        self.ops = {e: [] for e in self.ENGINES}
        self.sems = {}
        self.counts = {}
        self._rot = {}

    def sem(self, name):
        if name not in self.sems:
            self.sems[name] = self.nc.semaphore(name).__enter__()
            self.counts[name] = 0
        return self.sems[name]

    def emit(self, engine, fn):
        self.ops[engine].append(fn)

    def inc(self, name, by):
        h = self.sem(name)
        self.counts[name] += by
        return h, self.counts[name]

    def cur(self, name):
        self.sem(name)
        return self.counts[name]

    def wait(self, engine, name, val):
        h = self.sem(name)
        self.emit(engine, lambda e, h=h, v=val: e.wait_ge(h, v))

    def waitt(self, engine, token):
        name, v = token
        h = self.sem(name)
        self.emit(engine, lambda e, h=h, v=v: e.wait_ge(h, v))

    def wait_all(self, engine, family):
        for i in range(NROT):
            name = f"{family}.{i}"
            if name in self.sems and self.counts[name] > 0:
                h = self.sems[name]
                self.emit(engine, lambda e, h=h, v=self.counts[name]: e.wait_ge(h, v))

    def build(self, blk):
        for name in self.ENGINES:
            stream = self.ops[name]
            ser_sem = self.sem("_ser_" + name) if name in ("vector", "scalar") else None

            def make(stream, ser_sem):
                def run(eng):
                    e = SerEng(eng, ser_sem) if ser_sem is not None else eng
                    for fn in stream:
                        fn(e)
                return run
            getattr(blk, name)(make(stream, ser_sem))


_EMITTING = {
    "tensor_copy", "tensor_tensor", "tensor_add", "tensor_sub", "tensor_mul",
    "tensor_max", "tensor_scalar", "tensor_single_scalar", "tensor_scalar_add",
    "tensor_scalar_sub", "tensor_scalar_mul", "tensor_scalar_max",
    "tensor_scalar_min", "tensor_relu", "tensor_reduce", "reduce_max",
    "reduce_sum", "scalar_tensor_tensor", "tensor_tensor_scan", "memset",
    "memzero", "select", "copy_predicated", "transpose", "bn_stats", "bn_aggr",
    "max", "max_index", "pool", "pool_avg", "pool_max", "reciprocal",
    "stream_shuffle", "match_replace", "activation", "add", "mul", "sqrt",
    "square", "sign", "copy", "iota", "affine_select",
}


class SerEng:

    def __init__(self, eng, sem):
        self._eng = eng
        self._sem = sem
        self._n = 0
        self._waited = 0
        self._pend = []

    @staticmethod
    def _span(a):
        try:
            lst = a.ap
            if callable(lst):
                return None
            off = a.offset
            esz = mybir.dt.size(a.dtype)
            lo = off
            hi = off
            for st, cnt in lst[1:]:
                d = st * (cnt - 1)
                if d < 0:
                    lo += d
                else:
                    hi += d
            return (a.tensor.name, lo * esz, (hi + 1) * esz)
        except Exception:
            return None

    def _need(self, rds, wrs):
        need = 0
        for (nm, lo, hi) in wrs:
            for (pn, pl, ph, pw, seq) in self._pend:
                if pn == nm and lo < ph and pl < hi and seq > need:
                    need = seq
        for (nm, lo, hi) in rds:
            for (pn, pl, ph, pw, seq) in self._pend:
                if pw and pn == nm and lo < ph and pl < hi and seq > need:
                    need = seq
        return need

    def _wait_to(self, need):
        if need > self._waited:
            self._eng.wait_ge(self._sem, need)
            self._waited = need
            self._pend = [e for e in self._pend if e[4] > need]

    def flush(self):
        self._wait_to(self._n)

    def __getattr__(self, name):
        f = getattr(self._eng, name)
        if name not in _EMITTING:
            return f

        def wrapped(*a, **k):
            aps = []
            bad = False
            for x in a:
                if hasattr(x, "tensor"):
                    s = self._span(x)
                    if s is None:
                        bad = True
                    aps.append(s)
                else:
                    aps.append(None) if False else None
            kw_aps = []
            for kk, x in k.items():
                if hasattr(x, "tensor"):
                    s = self._span(x)
                    if s is None:
                        bad = True
                    kw_aps.append((kk, s))
            ap_pos = [s for s in aps if s is not None]
            if bad:
                self.flush()
                wrs, rds = [], []
            else:
                wrs = ap_pos[:1] + [s for kk, s in kw_aps if kk == "accum_out" and s]
                rds = ap_pos[1:] + [s for kk, s in kw_aps if kk != "accum_out" and s]
                self._wait_to(self._need(rds, wrs))
            inst = f(*a, **k)
            self._n += 1
            inst.then_inc(self._sem, 1)
            for s in wrs:
                self._pend.append((s[0], s[1], s[2], True, self._n))
            for s in rds:
                self._pend.append((s[0], s[1], s[2], False, self._n))
            if bad:
                self.flush()
            return inst
        return wrapped


NROT = 4


def _dma_common(p, engine, family, emit_fn):
    i = p._rot.get(family, 0)
    p._rot[family] = i + 1
    name = f"{family}.{i % NROT}"
    h, v = p.inc(name, 16)
    if v > 16:
        p.emit(engine, lambda e, h=h, w=v - 16: e.wait_ge(h, w))
    p.emit(engine, lambda e, h=h: emit_fn(e).then_inc(h, 16))
    return (name, v)


def dma(p, engine, out_ap, in_ap, family):
    return _dma_common(p, engine, family,
                       lambda e, o=out_ap, i=in_ap: e.dma_start(out=o, in_=i))


def dma_t(p, engine, out_ap, in_ap, family):
    return _dma_common(p, engine, family,
                       lambda e, o=out_ap, i=in_ap: e.dma_start_transpose(o, i))


def dma_ind(p, engine, out_ap, src_ap, idx_ap, family):
    return _dma_common(p, engine, family,
                       lambda e, o=out_ap, s=src_ap, x=idx_ap: e.indirect_dma_start(
                           o, None, s, IndirectOffsetOnAxis(x, 0)))


def step(p, engine, sem_name, fn, by=1):
    h, v = p.inc(sem_name, by)
    if engine in ("vector", "scalar"):
        def g(e, h=h, by=by, fn=fn):
            fn(e)
            e.flush()
            e.sem_inc(h, by)
        p.emit(engine, g)
    else:
        p.emit(engine, lambda e, h=h, fn=fn, by=by: fn(e).then_inc(h, by))
    return v


def host_constants():
    c = {}
    inv = 10000.0 ** (-np.arange(0, D_HR, 2, dtype=np.float64) / D_HR)
    ang = np.arange(S, dtype=np.float64)[:, None] * inv[None, :]
    cos = np.cos(ang).T.astype(np.float32)
    sin = np.sin(ang).T.astype(np.float32)
    cos32 = np.concatenate([cos, cos], 0)
    sin32 = np.concatenate([-sin, sin], 0)
    c["trig_c"] = np.tile(cos32, (4, 1)).astype(bf)
    c["trig_s"] = np.tile(sin32, (4, 1)).astype(bf)
    q = np.arange(SQ)[None, :]
    k = np.arange(128)[:, None]
    for j in range(4):
        c[f"dmask{j}"] = (q >= (k + 128 * j)).astype(bf)
    oh = np.zeros((16, 1024), bf)
    for h in range(16):
        oh[h, 64 * h:64 * (h + 1)] = 1
    c["onehot16"] = oh
    c["ident128"] = np.eye(128, dtype=bf)
    jt = np.zeros((16, NTOK), bf)
    for t in range(NTOK):
        jt[:, t] = (t // 128) % 16
    c["jtile"] = jt
    return c


def permute_qr_cols(w):
    nh = w.shape[1] // D_HR
    w = w.reshape(w.shape[0], nh, D_HR)
    perm = np.concatenate([np.arange(0, D_HR, 2), np.arange(1, D_HR, 2)])
    return np.ascontiguousarray(w[:, :, perm].reshape(w.shape[0], nh * D_HR))


def build(nc):
    p = Prog(nc)

    x_own = nc.dram_tensor("x_own", [2, S, D_MODEL], F32, kind="ExternalInput")
    x_pos = nc.dram_tensor("x_pos", [16, NBLK, 128, D_MODEL], F32, kind="ExternalInput")
    out_own = nc.dram_tensor("out_own", [2, S, D_MODEL], F32, kind="ExternalOutput")

    w_shapes = {"W_DQ": (D_MODEL, D_LAT), "W_UQP": (D_LAT, 1568),
                "W_DKV": (D_MODEL, D_LAT), "W_UKP": (D_LAT, 1088),
                "W_UV": (D_LAT, 1024),
                "W_KRP": (D_MODEL, 128), "W_O": (1024, D_MODEL),
                "Wg": (D_MODEL, 16), "Wq_r": (D_MODEL, 1024), "Wkv_r": (D_MODEL, 2048),
                "Wo_f": (16 * 64, 1024),
                "bq_td": (64, 16), "bkv_td": (128, 16),
                "g_rep": (128, 1024), "b_rep": (128, 1024), "bo_row": (1, 1024)}
    wd = {n: nc.dram_tensor(n, list(sh), F32, kind="ExternalInput")
          for n, sh in w_shapes.items()}
    cst = host_constants()
    cd = {k: nc.dram_tensor("c_" + k, list(v.shape), BF16, kind="ExternalInput")
          for k, v in cst.items()}
    base_d = nc.dram_tensor("c_base", [128, NBLK], F32, kind="ExternalInput")
    jcol_d = nc.dram_tensor("c_jcol", [16, 2], F32, kind="ExternalInput")

    xn_dram = nc.dram_tensor("xn_dram", [S, D_MODEL], BF16)
    xnF_dram = nc.dram_tensor("xnF_dram", [NTOK, D_MODEL], BF16)
    xnlo_dram = nc.dram_tensor("xnlo_dram", [NTOK, D_MODEL], BF16)
    lnz_dram = nc.dram_tensor("lnz_dram", [1, NTOK], F32)
    hg_dram = nc.dram_tensor("hg_dram", [NTOK, D_MODEL], BF16)
    idx_sc = nc.dram_tensor("idx_sc", [16, NPOS], BF16)
    attn_dram = nc.dram_tensor("attn_dram", [S, D_MODEL], F32)
    eq_dram = nc.dram_tensor("eq_dram", [16, S], BF16)
    rec_dram = nc.dram_tensor("rec_dram", [16, S], BF16)
    rsum_dram = nc.dram_tensor("rsum_dram", [16, S], BF16)
    ag1_in = nc.dram_tensor("ag1_in", [32, NPOS], F32)
    ag1_out = nc.dram_tensor("ag1_out", [NC * 32, NPOS], F32)
    ag2_in = nc.dram_tensor("ag2_in", [64, NTOK], BF16)
    ag2_out = nc.dram_tensor("ag2_out", [NC * 64, NTOK], BF16)

    ctx = {}

    def T(name, shape, dtype=BF16):
        h = nc.sbuf_tensor(name, shape, dtype).__enter__()
        ctx[name] = h
        return h

    for c in range(NCH):
        T(f"w_dq{c}", [128, D_LAT]); T(f"w_dkv{c}", [128, D_LAT])
    for l in range(2):
        T(f"w_qp{l}", [128, 1568])
        T(f"w_uk{l}", [128, 1088]); T(f"w_uv{l}", [128, 1024])
    T("wg_all", [128, 8 * 16])
    T("onesf", [16, 1], F32)
    for c in range(NCH):
        ctx[f"wg{c}"] = ctx["wg_all"].ap()[:, 16 * c:16 * (c + 1)]
    for j in range(4):
        T(f"dmask{j}", [128, SQ])
    T("onehot16", [16, 1024])
    T("onesc", [64, 1]); T("onesr", [1, 64])
    T("epsc", [128, 1], F32)
    T("base", [128, NBLK], F32)
    T("jcol", [16, 2], F32)
    T("bq_t", [64, 16], F32); T("bkv_t", [128, 16], F32)
    T("bo_sb", [1, 1024])
    T("ssum", [128, 1], F32); T("ssq", [128, 1], F32)
    T("tvar", [128, 1], F32); T("tln", [128, 1], F32); T("tmsq", [128, 1], F32)

    for b in range(2):
        T(f"stat_m{b}", [128, 16], F32); T(f"stat_r{b}", [128, 16], F32)
    T("mx", [16, NPOS], F32)
    T("idxf", [16, NPOS], F32)
    for b in range(NBLK):
        T(f"idxT{b}", [128, 16])
        T(f"rt{b}", [128, 16], F32)
        T(f"rtu{b}", [128, 16], U32)
    for par in range(2):
        rt = ctx[f"rt{par}"]
        ctx[f"ssum{par}"] = rt.ap()[:, 5:6]
        ctx[f"ssq{par}"] = rt.ap()[:, 6:7]
        ctx[f"tvar{par}"] = rt.ap()[:, 7:8]
        ctx[f"tln{par}"] = rt.ap()[:, 8:9]
        ctx[f"tmsq{par}"] = rt.ap()[:, 9:10]
        ctx[f"bias2{par}"] = rt.ap()[:, 10:11]
        ctx[f"lssum{par}"] = rt.ap()[:, 11:12]
        ctx[f"lssq{par}"] = rt.ap()[:, 12:13]
        ctx[f"ltvar{par}"] = rt.ap()[:, 13:14]
        ctx[f"ltln{par}"] = rt.ap()[:, 14:15]
        ctx[f"ltmsq{par}"] = rt.ap()[:, 15:16]
    ctx["bmr"] = ctx["rtu0"].ap().bitcast(F32)

    T("w16", [16, NPOS], F32)
    T("w16b", [16, NPOS])
    T("zf", [16, 1], F32)
    T("bias2", [128, 1], F32)

    T("AR1", [128, 8192])
    ctx["xa0"] = ctx["AR1"].ap().bitcast(F32)[:, 0:1024]
    ctx["xa1"] = ctx["AR1"].ap().bitcast(F32)[:, 1024:2048]
    ctx["zrow"] = ctx["AR1"].ap().bitcast(F32)[0:1, 0:4096]
    ctx["ot0"] = ctx["AR1"].ap().bitcast(F32)[:, 1024:2048]

    T("AR2", [128, 16384])
    ctx["lg"] = ctx["AR2"].ap().bitcast(F32)[0:16, 0:4096]
    ctx["M"] = ctx["lg"]
    ctx["egf"] = ctx["AR2"].ap().bitcast(F32)[0:16, 4096:8192]
    ctx["lnzbc"] = ctx["egf"]
    ctx["xnf32"] = ctx["AR2"].ap().bitcast(F32)[:, 0:1024]
    ctx["eg"] = ctx["AR2"].ap()[0:16, 8192:12288]
    ctx["jtile"] = ctx["AR2"].ap()[0:16, 12288:16384]
    for c in range(NCH):
        ctx[f"wkv{c}"] = ctx["AR2"].ap()[:, 2048 * c:2048 * (c + 1)]
    for h in range(16):
        ctx[f"w_o{h}"] = ctx["AR2"].ap()[:, 1024 * h:1024 * (h + 1)]

    T("AR3", [128, 16384])
    ctx["eq"] = ctx["AR3"].ap()[0:64, 0:4096]
    ctx["cand"] = ctx["AR3"].ap()[0:64, 4096:8192]
    ctx["qall"] = ctx["cand"]
    ctx["tt"] = ctx["eq"]
    ctx["attG"] = ctx["AR3"].ap().bitcast(F32)[0:16, 2048:4096]
    ctx["kvall"] = ctx["AR3"].ap()[:, 8192:12288]
    ctx["hgt0"] = ctx["AR3"].ap()[:, 12288:13312]
    ctx["hgt1"] = ctx["AR3"].ap()[:, 13312:14336]
    for hh in range(NCH):
        ctx[f"khead{hh}"] = ctx["AR3"].ap()[:, S * hh:S * (hh + 1)]
    ctx["avh"] = ctx["AR3"].ap()[0:64, 0:2048]
    ctx["avh_hi"] = ctx["AR3"].ap()[64:128, 0:2048]
    ctx["eq64"] = ctx["AR3"].ap()[0:64, 2048:4096]
    ctx["eq64_hi"] = ctx["AR3"].ap()[64:128, 2048:4096]
    ctx["idxG"] = ctx["AR3"].ap().bitcast(F32)[0:16, 2048:4096]
    ctx["nsel"] = ctx["AR3"].ap().bitcast(F32)[0:1, 4096:6144]
    ctx["nselb"] = ctx["AR3"].ap()[0:1, 12288:14336]
    ctx["xq_e"] = ctx["AR3"].ap()[:, 0:1024]
    ctx["xt0"] = ctx["AR2"].ap().bitcast(F32)[:, 0:1024]
    ctx["xt1"] = ctx["AR2"].ap().bitcast(F32)[:, 1024:2048]
    ctx["at0"] = ctx["AR2"].ap().bitcast(F32)[:, 2048:3072]
    ctx["at1"] = ctx["AR2"].ap().bitcast(F32)[:, 3072:4096]
    ctx["yt1"] = ctx["AR2"].ap().bitcast(F32)[:, 4096:5120]

    for c in range(NCH):
        T(f"xnT{c}", [128, 2 * S])
    for h2 in range(2):
        ctx[f"cqT{h2}"] = ctx[f"xnT{4 + h2}"].ap()[:, S:2 * S]
        ctx[f"ckvT{h2}"] = ctx[f"xnT{6 + h2}"].ap()[:, S:2 * S]
    for hh in range(NCH):
        ctx[f"qhead{hh}"] = ctx[f"xnT{hh}"].ap()[:, 0:S]
    for c in range(NCH):
        ctx[f"avmP{c}"] = ctx[f"xnT{c}"].ap()[:, 0:S]
        ctx[f"wo_p{c}"] = ctx["AR1"].ap()[:, 1024 * c:1024 * (c + 1)]

    T("attnU_t", [128, 2048])
    ctx["wgf"] = ctx["attnU_t"].ap().bitcast(F32)[:, 0:128]
    ctx["ident128"] = ctx["attnU_t"].ap()[:, 1024:1152]
    ctx["wglo_all"] = ctx["attnU_t"].ap()[:, 512:640]
    for c in range(NCH):
        ctx[f"wglo{c}"] = ctx["wglo_all"][:, 16 * c:16 * (c + 1)]
    for c in range(NCH):
        ctx[f"w_krp{c}"] = ctx["attnU_t"].ap()[:, 128 * c:128 * (c + 1)]
    for hh in range(4):
        ctx[f"attnU{hh}"] = ctx[f"xnT{hh}"].ap()[0:65, S:2 * S]
        ctx[f"attnU128_{hh}"] = ctx[f"xnT{hh}"].ap()[:, S:2 * S]
    ctx["attnU4"] = ctx["AR1"].ap()[0:65, 4096:6144]
    ctx["attnU128_4"] = ctx["AR1"].ap()[:, 4096:6144]
    ctx["attnU5"] = ctx["AR1"].ap()[0:65, 6144:8192]
    ctx["attnU128_5"] = ctx["AR1"].ap()[:, 6144:8192]
    ctx["attnU6"] = ctx["attnU_t"].ap()[0:65, 0:S]
    ctx["attnU128_6"] = ctx["attnU_t"].ap()[:, 0:S]
    ctx["attnU7"] = ctx["AR1"].ap()[0:65, 0:2048]
    ctx["attnU128_7"] = ctx["AR1"].ap()[:, 0:2048]

    T("vh_t", [128, 16 * 520 + 64])
    for kt in range(NKT):
        ctx[f"vh{kt}"] = ctx["vh_t"].ap()[:, 520 * kt:520 * (kt + 1)]
    ctx["g_rep"] = ctx["vh_t"].ap().bitcast(F32)[:, 0:1024]
    ctx["b_rep"] = ctx["vh_t"].ap().bitcast(F32)[:, 1024:2048]
    for c in range(NCH):
        ctx[f"wq{c}"] = ctx["vh_t"].ap()[:, 1024 * c:1024 * (c + 1)]
    ctx["krSt"] = ctx["AR3"].ap()[64:96, 0:2048]
    ctx["krPr"] = ctx["AR3"].ap()[64:96, 2048:4096]
    ctx["pQ0"] = ctx["vh_t"].ap()[64:96, 2048:4096]
    ctx["pQ1"] = ctx["vh_t"].ap()[64:96, 4096:6144]
    ctx["pQ2"] = ctx["vh_t"].ap()[64:96, 6144:8192]
    ctx["pQ3"] = ctx["vh_t"].ap()[64:96, 0:2048]

    T("P0", [128, 4 * SQ]); T("P1", [128, 4 * SQ])
    ctx["trig_c"] = ctx["P0"]; ctx["trig_s"] = ctx["P1"]
    ctx["xq0"] = ctx["P0"].ap()[:, 0:1024]
    ctx["xq1"] = ctx["P1"].ap()[:, 1024:2048]
    ctx["xn0"] = ctx["P0"].ap()[:, 1024:2048]
    ctx["xn1"] = ctx["P1"].ap()[:, 0:1024]
    ctx["xnlo"] = ctx["P1"].ap()[:, 1024:2048]
    ctx["ytile"] = ctx["P0"].ap().bitcast(F32)[:, 0:1024]
    ctx["xtile"] = ctx["P1"].ap().bitcast(F32)[:, 0:1024]
    ctx["ot1"] = ctx["P0"].ap().bitcast(F32)[:, 0:1024]
    ctx["rsumS"] = ctx["P0"].ap()[0:16, 0:2048]
    ctx["rsumS8"] = ctx["P0"].ap()[0:8, 0:2048]
    ctx["rec16n"] = ctx["P1"].ap()[0:16, 0:2048]
    ctx["rec8n"] = ctx["P1"].ap()[0:8, 0:2048]

    ps_big = nc.psum_tensor([128, 2048], F32).__enter__()
    ps_a = nc.psum_tensor([128, 512], F32).__enter__()
    ps_pv = nc.psum_tensor([128, 512], F32).__enter__()
    ps_wo = nc.psum_tensor([128, 1024], F32).__enter__()
    ctx.update(ps_big=ps_big, ps_a=ps_a, ps_pv=ps_pv, ps_wo=ps_wo)
    banks = [ps_a, ps_pv]
    nb = [0]

    for c in range(NCH):
        dma(p, "gpsimd", ctx[f"w_dq{c}"][:, :], wd["W_DQ"][128*c:128*(c+1), :], "dw")
        dma(p, "gpsimd", ctx[f"w_dkv{c}"][:, :], wd["W_DKV"][128*c:128*(c+1), :], "dw")
        dma(p, "gpsimd", ctx["wgf"][:, 16*c:16*(c+1)], wd["Wg"][128*c:128*(c+1), :], "dw")
    for l in range(2):
        dma(p, "gpsimd", ctx[f"w_qp{l}"][:, :], wd["W_UQP"][128*l:128*(l+1), :], "dw")
        dma(p, "gpsimd", ctx[f"w_uk{l}"][:, :], wd["W_UKP"][128*l:128*(l+1), :], "dw")
        dma(p, "gpsimd", ctx[f"w_uv{l}"][:, :], wd["W_UV"][128*l:128*(l+1), :], "dw")
    for c in range(NCH):
        dma(p, "gpsimd", ctx[f"wq{c}"], wd["Wq_r"][128*c:128*(c+1), :], "dw")
    for c in range(2, NCH):
        dma(p, "gpsimd", ctx[f"wkv{c}"][16:128, :],
            wd["Wkv_r"][128*c+16:128*(c+1), :], "dw")
    for k in ("dmask0", "dmask1", "dmask2", "dmask3", "onehot16"):
        dma(p, "gpsimd", ctx[k][:, :], cd[k].ap(), "dw")
    dma(p, "gpsimd", ctx["base"][:, :], base_d.ap(), "dw")
    dma(p, "gpsimd", ctx["jcol"][:, :], jcol_d.ap(), "dw")
    dma(p, "gpsimd", ctx["bq_t"][:, :], wd["bq_td"].ap(), "dw")
    dma(p, "gpsimd", ctx["bkv_t"][:, :], wd["bkv_td"].ap(), "dw")
    dma(p, "gpsimd", ctx["bo_sb"][:, :], wd["bo_row"].ap(), "dw")

    def init0(v):
        v.memset(ctx["epsc"][:, :], EPS)
        v.memset(ctx["onesc"][:, :], 1.0)
        v.memset(ctx["onesr"][:, :], 1.0)
        v.memset(ctx["onesf"][:, :], 1.0)
        return v.memset(ctx["vh_t"][:, 16 * 520:16 * 520 + 64], 0.0)
    step(p, "vector", "ve", init0)
    p.wait_all("vector", "dw")

    def wgsplit(v):
        v.tensor_copy(ctx["wg_all"][:, :], ctx["wgf"][:, :])
        return v.tensor_sub(ctx["wglo_all"][:, :], ctx["wgf"][:, :],
                            ctx["wg_all"][:, :])
    step(p, "vector", "ve", wgsplit)

    def ln_tile(xa, xn_out, m_ap=None, r_ap=None, par=0, pfx="", xq=None):
        sm = ctx[f"{pfx}ssum{par}"]
        sq = ctx[f"{pfx}ssq{par}"]
        tv = ctx[f"{pfx}tvar{par}"]
        tl = ctx[f"{pfx}tln{par}"]
        tm = ctx[f"{pfx}tmsq{par}"]
        if xq is None:
            xq = ctx[f"xq{par}"]
        step(p, "vector", "ve", lambda v, xa=xa, sm=sm: v.reduce_sum(sm, xa, AX.X))
        v_sq = step(p, "scalar", "se", lambda s, xa=xa, sq=sq, xq=xq: s.activation(
            xq, xa, AF.Square, accum_out=sq))
        p.wait("vector", "se", v_sq)
        if m_ap is None:
            m_ap = tv

        def stats(v, m_ap=m_ap, sm=sm, sq=sq, tm=tm):
            v.tensor_scalar_mul(m_ap, sm, 1.0 / 1024)
            v.tensor_mul(tm, m_ap, m_ap)
            return v.scalar_tensor_tensor(tm, sq, 1.0 / 1024, tm,
                                          A.mult, A.subtract)
        v_st = step(p, "vector", "ve", stats)
        p.wait("scalar", "ve", v_st)
        if r_ap is None:
            r_ap = tl

        def rstd(s, r_ap=r_ap, tm=tm, tl=tl):
            s.activation(tl, tm, AF.Ln, bias=ctx["epsc"][:, :])
            return s.activation(r_ap, tl, AF.Exp, scale=-0.5)
        v_rs = step(p, "scalar", "se", rstd)
        p.wait("vector", "se", v_rs)
        return step(p, "vector", "ve", lambda v, xa=xa, xn=xn_out, m=m_ap, r=r_ap:
                    v.tensor_scalar(xn, xa, m, r, A.subtract, A.mult))

    st1_toks, st2_toks = {}, {}
    for t in range(32):
        blk, j = t // 16, t % 16
        par = t % 2
        xa = ctx["xa0"] if par == 0 else ctx["xa1"]
        xn = ctx["xn0"] if par == 0 else ctx["xn1"]
        xnf = (ctx["AR2"].ap().bitcast(F32)[:, 0:1024] if par == 0
               else ctx["AR2"].ap().bitcast(F32)[:, 1024:2048])
        xnlo = (ctx["AR3"].ap()[:, 0:1024] if par == 0
                else ctx["AR3"].ap()[:, 1024:2048])
        if t >= 2:
            p.waitt("sync", st1_toks[t - 2])
            p.waitt("sync", st2_toks[t - 2])
        v_load = dma(p, "sync", xa, x_pos[j, blk, :, :], "dx")
        p.waitt("vector", v_load)
        p.waitt("scalar", v_load)
        v_no = ln_tile(xa, xnf, par=par)

        def split(v, xn=xn, xnf=xnf, xnlo=xnlo):
            v.tensor_copy(xn, xnf)
            return v.tensor_sub(xnlo, xnf, xn)
        v_sp = step(p, "vector", "ve", split)
        p.wait("sync", "ve", v_sp)
        st1_toks[t] = dma(p, "sync", xnF_dram[128*t:128*(t+1), :], xn, "dx")
        st2_toks[t] = dma(p, "sync", xnlo_dram[128*t:128*(t+1), :], xnlo, "dx")
    p.wait_all("sync", "dx")
    for c in range(NCH):
        dma_t(p, "sync", ctx[f"xnT{c}"][:, 0:NTOK], xnF_dram[:, 128*c:128*(c+1)], "dx")
    p.wait("gpsimd", "ve", p.cur("ve"))
    p.wait("gpsimd", "se", p.cur("se"))
    for c in range(2):
        dma(p, "gpsimd", ctx[f"wkv{c}"][16:128, :],
            wd["Wkv_r"][128*c+16:128*(c+1), :], "dw")
    p.wait_all("tensor", "dx")
    p.wait_all("tensor", "dw")
    p.wait("tensor", "ve", p.cur("ve"))

    f1_tok = {0: p.cur("ve"), 1: p.cur("ve")}
    for sl in range(NTOK // 512):
        bi = nb[0] % 2; psum = banks[bi]; nb[0] += 1
        p.wait("tensor", "ve", f1_tok[bi])

        def mm(te, sl=sl, psum=psum):
            r = None
            for c in range(NCH):
                te.matmul(psum[0:16, :], ctx[f"wg{c}"],
                          ctx[f"xnT{c}"][:, 512*sl:512*(sl+1)],
                          start=(c == 0), stop=False)
                r = te.matmul(psum[0:16, :], ctx[f"wglo{c}"],
                              ctx[f"xnT{c}"][:, 512*sl:512*(sl+1)],
                              start=False, stop=(c == NCH - 1))
            return r
        v_mm = step(p, "tensor", "pe", mm)
        p.wait("vector", "pe", v_mm)
        f1_tok[bi] = step(p, "vector", "ve", lambda v, sl=sl, psum=psum: v.tensor_copy(
            ctx["lg"][:, 512*sl:512*(sl+1)], psum[0:16, :]))
    p.wait("sync", "pe", p.cur("pe"))
    for c in range(NCH):
        dma_t(p, "sync", ctx[f"xnT{c}"][:, 0:NTOK], xnlo_dram[:, 128*c:128*(c+1)], "dx")
    p.wait_all("tensor", "dx")
    for sl in range(NTOK // 512):
        bi = nb[0] % 2; psum = banks[bi]; nb[0] += 1
        p.wait("tensor", "ve", f1_tok[bi])

        def mm(te, sl=sl, psum=psum):
            r = None
            for c in range(NCH):
                r = te.matmul(psum[0:16, :], ctx[f"wg{c}"],
                              ctx[f"xnT{c}"][:, 512*sl:512*(sl+1)],
                              start=(c == 0), stop=(c == NCH - 1))
            return r
        v_mm = step(p, "tensor", "pe", mm)
        p.wait("vector", "pe", v_mm)
        f1_tok[bi] = step(p, "vector", "ve", lambda v, sl=sl, psum=psum: v.tensor_add(
            ctx["lg"][:, 512*sl:512*(sl+1)], ctx["lg"][:, 512*sl:512*(sl+1)],
            psum[0:16, :]))
    p.wait("sync", "pe", p.cur("pe"))
    v_lg = p.cur("ve")
    p.wait("scalar", "ve", v_lg)
    v_eg = step(p, "scalar", "se", lambda s: s.activation(ctx["egf"], ctx["lg"], AF.Exp))
    p.wait("tensor", "se", v_eg)
    for sl in range(NTOK // 512):
        bi = nb[0] % 2; psum = banks[bi]; nb[0] += 1
        p.wait("tensor", "ve", f1_tok[bi])

        def mm(te, sl=sl, psum=psum):
            return te.matmul(psum[0:1, :], ctx["onesf"][:, :],
                             ctx["egf"][:, 512*sl:512*(sl+1)], start=True, stop=True)
        v_mm = step(p, "tensor", "pe", mm)
        p.wait("vector", "pe", v_mm)
        f1_tok[bi] = step(p, "vector", "ve", lambda v, sl=sl, psum=psum: v.tensor_copy(
            ctx["zrow"][:, 512*sl:512*(sl+1)], psum[0:1, :]))
    v_z = p.cur("ve")
    p.wait("scalar", "ve", v_z)
    v_lz = step(p, "scalar", "se", lambda s: s.activation(
        ctx["zrow"], ctx["zrow"], AF.Ln))
    p.wait("sync", "se", v_lz)
    dma(p, "sync", lnz_dram.ap(), ctx["zrow"], "dx")
    p.wait_all("sync", "dx")
    v_zb = dma(p, "sync", ctx["lnzbc"], lnz_dram.ap().partition_broadcast(16), "dx")
    p.waitt("vector", v_zb)
    v_M = step(p, "vector", "ve", lambda v: v.tensor_sub(ctx["M"], ctx["lg"],
                                                         ctx["lnzbc"]))
    p.wait("gpsimd", "ve", v_M)
    tok_jt = dma(p, "gpsimd", ctx["jtile"], cd["jtile"].ap(), "dw")
    p.waitt("vector", tok_jt)

    def red(v):
        ap = ctx["M"].rearrange("l (b j q) -> l b q j", b=NBLK, j=16)
        return v.tensor_reduce(ctx["mx"].ap().rearrange("l (b q) -> l b q", b=NBLK),
                               ap, AX.X, A.max)
    step(p, "vector", "ve", red)

    def eqf(v):
        apM = ctx["M"].rearrange("l (b j q) -> l b j q", b=NBLK, j=16)
        apmx = ctx["mx"].ap().rearrange("l (b q) -> l b () q", b=NBLK)
        apmx = apmx.broadcast_to([16, NBLK, 16, NPOS // NBLK])
        return v.tensor_tensor(
            ctx["eq"][0:16, :].rearrange("l (b j q) -> l b j q", b=NBLK, j=16),
            apM, apmx, A.is_ge)
    step(p, "vector", "ve", eqf)

    def candf(v):
        v.tensor_scalar(ctx["cand"][0:16, :], ctx["jtile"], JBIG, None, A.subtract)
        v.tensor_mul(ctx["cand"][0:16, :], ctx["cand"][0:16, :], ctx["eq"][0:16, :])
        return v.tensor_scalar(ctx["cand"][0:16, :], ctx["cand"][0:16, :],
                               JBIG, None, A.add)
    step(p, "vector", "ve", candf)

    def redmin(v):
        ap = ctx["cand"][0:16, :].rearrange("l (b j q) -> l b q j", b=NBLK, j=16)
        return v.tensor_reduce(ctx["idxf"].ap().rearrange("l (b q) -> l b q", b=NBLK),
                               ap, AX.X, A.min)
    step(p, "vector", "ve", redmin)
    step(p, "vector", "ve", lambda v: v.tensor_copy(ctx["eq"][0:16, 0:NPOS],
                                                    ctx["idxf"][:, :]))
    p.wait("sync", "ve", p.cur("ve"))
    dma(p, "sync", idx_sc.ap(), ctx["eq"][0:16, 0:NPOS], "dx")
    p.wait_all("sync", "dx")
    for b in range(NBLK):
        dma_t(p, "sync", ctx[f"idxT{b}"][:, 0:16], idx_sc[:, 128*b:128*(b+1)], "dx")
    p.wait_all("vector", "dx")
    for b in range(NBLK):
        def rtf(v, b=b):
            v.tensor_scalar(ctx[f"rt{b}"][:, :], ctx[f"idxT{b}"][:, 0:16],
                            128.0, None, A.mult)
            v.tensor_scalar_add(ctx[f"rt{b}"][:, :], ctx[f"rt{b}"][:, :],
                                ctx["base"][:, b:b+1])
            return v.tensor_copy(ctx[f"rtu{b}"][:, :], ctx[f"rt{b}"][:, :])
        step(p, "vector", "ve", rtf)
    v_rt = p.cur("ve")
    st = {}

    def emit_f1_tail(feed):
        p.wait("gpsimd", "ve", v_rt)
        t_id = dma(p, "gpsimd", ctx["ident128"], cd["ident128"].ap(), "dw")
        p.waitt("tensor", t_id)
        gbufs = ([ctx["AR3"].ap()[:, 12288 + 1024*i:12288 + 1024*(i+1)]
                  for i in range(4)] +
                 [ctx["AR3"].ap()[:, 4096 + 1024*i:4096 + 1024*(i+1)]
                  for i in range(4)])
        psb_slots = ([(ctx["ps_big"].ap().bitcast(BF16), 1024 * i)
                      for i in range(4)] +
                     [(ctx["ps_wo"].ap().bitcast(BF16), 1024 * i)
                      for i in range(2)])
        tp_toks = {}
        ev_toks = {}
        gi = 0
        for l in range(16):
            for b in range(NBLK):
                tok0 = l * NPOS + b * 128
                buf = gbufs[gi % 8]
                if gi >= 8:
                    p.wait("gpsimd", "pe", tp_toks[gi - 8])
                tg = dma_ind(p, "gpsimd", buf, xnF_dram.ap(),
                             ctx[f"rtu{b}"][:, l:l+1], f"dg{gi % 2}")
                p.waitt("tensor", tg)
                sb_, soff = psb_slots[gi % 6]
                if gi >= 6:
                    p.wait("tensor", "ve", ev_toks[gi - 6])

                def tps(te, buf=buf, sb_=sb_, soff=soff):
                    r = None
                    for f in range(8):
                        r = te.transpose(sb_[:, soff + 128*f:soff + 128*(f+1)],
                                         buf[:, 128*f:128*(f+1)],
                                         ctx["ident128"])
                    return r
                tp_toks[gi] = step(p, "tensor", "pe", tps)
                p.wait("vector", "pe", tp_toks[gi])

                def ev(v, tok0=tok0, sb_=sb_, soff=soff):
                    r = None
                    for c in range(NCH):
                        r = v.tensor_copy(ctx[f"xnT{c}"][:, tok0:tok0+128],
                                          sb_[:, soff + 128*c:soff + 128*(c+1)])
                    return r
                ev_toks[gi] = step(p, "vector", "ve", ev)
                if gi % 2 == 1:
                    feed()
                gi += 1
        while feed():
            pass
        for c in range(NCH):
            dma(p, "gpsimd", ctx[f"wkv{c}"][0:16, :],
                wd["Wkv_r"][128*c:128*c+16, :], "dw")
        p.wait_all("tensor", "dw")
        p.wait("tensor", "ve", p.cur("ve"))
        for l in range(16):
            bi = nb[0] % 2; psum = banks[bi]; nb[0] += 1
            p.wait("tensor", "ve", f1_tok[bi])

            def mmq(te, l=l, psum=psum):
                r = None
                for c in range(NCH):
                    r = te.matmul(psum[0:64, 0:NPOS],
                                  ctx[f"wq{c}"][:, 64*l:64*(l+1)],
                                  ctx[f"xnT{c}"][:, NPOS*l:NPOS*(l+1)],
                                  start=(c == 0), stop=(c == NCH - 1))
                return r
            v_mm = step(p, "tensor", "pe", mmq)
            p.wait("vector", "pe", v_mm)
            f1_tok[bi] = step(p, "vector", "ve",
                              lambda v, l=l, psum=psum: v.tensor_scalar_add(
                ctx["qall"][0:64, NPOS*l:NPOS*(l+1)], psum[0:64, 0:NPOS],
                ctx["bq_t"][:, l:l+1]))
            bi = nb[0] % 2; psum = banks[bi]; nb[0] += 1
            p.wait("tensor", "ve", f1_tok[bi])

            def mmkv(te, l=l, psum=psum):
                r = None
                for c in range(NCH):
                    r = te.matmul(psum[:, 0:NPOS],
                                  ctx[f"wkv{c}"][:, 128*l:128*(l+1)],
                                  ctx[f"xnT{c}"][:, NPOS*l:NPOS*(l+1)],
                                  start=(c == 0), stop=(c == NCH - 1))
                return r
            v_mm = step(p, "tensor", "pe", mmkv)
            p.wait("vector", "pe", v_mm)
            f1_tok[bi] = step(p, "vector", "ve",
                              lambda v, l=l, psum=psum: v.tensor_scalar_add(
                ctx["kvall"][:, NPOS*l:NPOS*(l+1)], psum[:, 0:NPOS],
                ctx["bkv_t"][:, l:l+1]))
        step(p, "vector", "ve", lambda v: v.tensor_mul(
            ctx["tt"][0:64, :], ctx["qall"][0:64, :], ctx["kvall"][0:64, :]))
        p.wait("tensor", "ve", p.cur("ve"))
        for sl in range(8):
            bi = nb[0] % 2; psum = banks[bi]; nb[0] += 1
            p.wait("tensor", "ve", f1_tok[bi])

            def mm(te, sl=sl, psum=psum):
                return te.matmul(psum[0:1, :], ctx["onesc"][0:64, :],
                                 ctx["tt"][0:64, 512*sl:512*(sl+1)],
                                 start=True, stop=True)
            v_mm = step(p, "tensor", "pe", mm)
            p.wait("vector", "pe", v_mm)
            f1_tok[bi] = step(p, "vector", "ve",
                              lambda v, sl=sl, psum=psum: v.tensor_copy(
                ctx["zrow"][:, 512*sl:512*(sl+1)], psum[0:1, :]))
        p.wait("sync", "ve", p.cur("ve"))
        dma(p, "sync", ag1_in.ap().rearrange("a q -> () (a q)")[:, 0:4096],
            ctx["zrow"], "dx")
        dma(p, "sync", ag1_in[16:32, :], ctx["idxf"][:, :], "dx")
        p.wait_all("gpsimd", "dx")

        h_cc, v_cc1 = p.inc("cc", 1)
        st["v_cc1"] = v_cc1
        p.emit("gpsimd", lambda e, h=h_cc: e.collective_compute(
            "AllGather", A.bypass, replica_groups=[list(range(NC))],
            ins=[ag1_in.ap().opt()], outs=[ag1_out.ap().opt()]).then_inc(h, 1))

    def emit_p1b():
        p.wait("gpsimd", "cc", st["v_cc1"])
        v_w16l = dma(p, "gpsimd", ctx["w16"][:, :], ag1_in[0:16, :], "dw")
        v_g1 = dma(p, "gpsimd", ctx["attG"].rearrange("l (s q) -> l s q", s=8),
                   ag1_out.ap().rearrange("(s r) q -> s r q", r=32)[:, 0:16, :]
                   .rearrange("s l q -> l s q"), "dw")
        p.waitt("vector", v_g1)
        step(p, "vector", "ve", lambda v: v.reduce_max(ctx["zf"][0:16, :],
                                                       ctx["attG"], AX.X))
        v_m = step(p, "vector", "ve", lambda v: v.tensor_scalar_mul(
            ctx["zf"][0:16, :], ctx["zf"][0:16, :], -0.125))
        p.wait("scalar", "ve", v_m)
        v_e = step(p, "scalar", "se", lambda s: s.activation(
            ctx["attG"], ctx["attG"], AF.Exp, bias=ctx["zf"][0:16, :], scale=0.125))
        p.wait("vector", "se", v_e)
        step(p, "vector", "ve", lambda v: v.reduce_sum(ctx["tvar"][0:16, :],
                                                       ctx["attG"], AX.X))
        v_rz = step(p, "vector", "ve", lambda v: v.reciprocal(
            ctx["tln"][0:16, :], ctx["tvar"][0:16, :]))
        p.wait("scalar", "ve", v_rz)
        p.waitt("scalar", v_w16l)
        v_e2 = step(p, "scalar", "se", lambda s: s.activation(
            ctx["w16"][:, :], ctx["w16"][:, :], AF.Exp, bias=ctx["zf"][0:16, :],
            scale=0.125))
        p.wait("vector", "se", v_e2)

        def normw(v):
            v.tensor_scalar_mul(ctx["w16"][:, :], ctx["w16"][:, :],
                                ctx["tln"][0:16, :])
            return v.tensor_copy(ctx["w16b"][:, :], ctx["w16"][:, :])
        v_w16 = step(p, "vector", "ve", normw)
        p.wait("tensor", "ve", v_w16)
        for l in range(16):
            psum = banks[nb[0] % 2]; nb[0] += 1
            p.wait("tensor", "ve", p.cur("ve"))

            def mmb(te, l=l, psum=psum):
                return te.matmul(psum[0:64, 0:NPOS],
                                 ctx["onehot16"][:, 64*l:64*(l+1)],
                                 ctx["w16b"][:, :], start=True, stop=True)
            v_mm = step(p, "tensor", "pe", mmb)
            p.wait("vector", "pe", v_mm)
            step(p, "vector", "ve", lambda v, l=l, psum=psum: v.tensor_mul(
                ctx["tt"][0:64, NPOS*l:NPOS*(l+1)],
                ctx["kvall"][64:128, NPOS*l:NPOS*(l+1)], psum[0:64, 0:NPOS]))
        v_av = p.cur("ve")
        p.wait("sync", "ve", v_av)
        dma(p, "sync", ag2_in.ap(), ctx["tt"][0:64, :], "dx")
        p.wait_all("gpsimd", "dx")
        h_cc, v_cc2 = p.inc("cc", 1)
        st["v_cc2"] = v_cc2
        p.emit("gpsimd", lambda e, h=h_cc: e.collective_compute(
            "AllGather", A.bypass, replica_groups=[list(range(NC))],
            ins=[ag2_in.ap().opt()], outs=[ag2_out.ap().opt()]).then_inc(h, 1))

    def make_ln_feeder(bt, alt=False):
        stat_m = ctx[f"stat_m{bt}"]
        stat_r = ctx[f"stat_r{bt}"]
        if alt:
            xas = [ctx["AR3"].ap().bitcast(F32)[:, 2048:3072],
                   ctx["AR3"].ap().bitcast(F32)[:, 3072:4096]]
            xns = [ctx["AR3"].ap()[:, 8192:9216], ctx["AR3"].ap()[:, 9216:10240]]
            xqs = [ctx["AR3"].ap()[:, 10240:11264], ctx["AR3"].ap()[:, 11264:12288]]
        else:
            xas = [ctx["xa0"], ctx["xa1"]]
            xns = [ctx["xn0"], ctx["xn1"]]
            xqs = [None, None]
        state = {"t": 0}
        lnst = {}
        st["lnst"] = lnst

        def feed():
            t = state["t"]
            if t >= 16:
                return False
            if t == 0:
                if not alt:
                    p.wait("scalar", "pe", p.cur("pe"))
                p.wait("scalar", "ve", p.cur("ve"))
                p.wait("sync", "ve", p.cur("ve"))
                p.wait_all("sync", "dx")
            xa = xas[t % 2]
            xn = xns[t % 2]
            if alt:
                eng, fam = "scalar", "ds"
            else:
                eng, fam = ("scalar", "ds") if t % 2 == 0 else ("sync", "dx")
            if t >= 2:
                p.waitt(eng, lnst[t - 2])
            v_load = dma(p, eng, xa, x_own[bt, 128*t:128*(t+1), :], fam)
            p.waitt("vector", v_load)
            p.waitt("scalar", v_load)
            v_no = ln_tile(xa, xn, m_ap=stat_m[:, t:t+1], r_ap=stat_r[:, t:t+1],
                           par=t % 2, pfx="l", xq=xqs[t % 2])
            p.wait("sync", "ve", v_no)
            lnst[t] = dma(p, "sync", xn_dram[128*t:128*(t+1), :], xn, "dx")
            state["t"] = t + 1
            return True
        return feed

    def emit_a_mid(bt, wo_hi_preloaded=False):
        p.wait("sync", "pe", p.cur("pe"))
        for tok in st.get("lnst", {}).values():
            p.waitt("sync", tok)
        for c in range(NCH):
            dma_t(p, "sync", ctx[f"xnT{c}"][:, 0:S], xn_dram[:, 128*c:128*(c+1)], "dx")
        p.wait_all("tensor", "dx")

        p.wait("gpsimd", "pe", p.cur("pe"))
        p.wait("gpsimd", "ve", p.cur("ve"))
        p.wait("gpsimd", "se", p.cur("se"))
        p.wait_all("gpsimd", "dx")
        for h in range(8):
            dma(p, "gpsimd", ctx[f"w_o{h}"][0:64, :], wd["W_O"][64*h:64*(h+1), :], "dw")
        p.wait_all("vector", "dw")

        def wo_zero(v):
            r = None
            for h in range(8):
                r = v.memset(ctx[f"w_o{h}"][64:128, :], 0.0)
            return r
        v_woz = step(p, "vector", "ve", wo_zero)

        def au_zero(v):
            r = None
            for hh in range(8):
                r = v.memset(ctx[f"attnU128_{hh}"][64:128, :], 0.0)
            return r
        v_auz = step(p, "vector", "ve", au_zero)

        p.wait("gpsimd", "ve", v_auz)
        krp_toks = [dma(p, "gpsimd", ctx[f"w_krp{c}"][:, :],
                        wd["W_KRP"][128*c:128*(c+1), :], "dw")
                    for c in range(NCH)]

        nbv = [0]
        bank_tok = {0: ("ve", p.cur("ve")), 1: ("ve", p.cur("ve"))}
        st["nbv"] = nbv; st["bank_tok"] = bank_tok

        def proj(dst_fn, w_fn, nchunks, rows=128, eng="vector"):
            bi = nbv[0] % 2; psum = banks[bi]; nbv[0] += 1
            sem, val = bank_tok[bi]
            p.wait("tensor", sem, val)

            def mms(te, psum=psum):
                r = None
                for i in range(nchunks):
                    wap, xap = w_fn(i)
                    r = te.matmul(psum[0:rows, :], wap, xap,
                                  start=(i == 0), stop=(i == nchunks - 1))
                return r
            v_mm = step(p, "tensor", "pe", mms)
            p.wait(eng, "pe", v_mm)
            sname = "ve" if eng == "vector" else "se"
            tok = step(p, eng, sname, lambda e, psum=psum: dst_fn(e, psum))
            bank_tok[bi] = (sname, tok)
        st["proj"] = proj

        for name, wpfx in (("cqT", "w_dq"), ("ckvT", "w_dkv")):
            for h2 in range(2):
                for qc in range(NQT):
                    def ev(v, psum, name=name, h2=h2, qc=qc):
                        return v.tensor_copy(ctx[name + str(h2)][:, SQ*qc:SQ*(qc+1)],
                                             psum[:, :])
                    proj(ev, lambda i, h2=h2, qc=qc, wpfx=wpfx: (
                        ctx[f"{wpfx}{i}"][:, 128*h2:128*(h2+1)],
                        ctx[f"xnT{i}"][:, SQ*qc:SQ*(qc+1)]), NCH)
        for t in krp_toks:
            p.waitt("tensor", t)
        for qc in range(NQT):
            def ev(v, psum, qc=qc):
                return v.tensor_copy(ctx["krSt"][:, SQ*qc:SQ*(qc+1)], psum[64:96, :])
            proj(ev, lambda i, qc=qc: (ctx[f"w_krp{i}"][:, :],
                                       ctx[f"xnT{i}"][:, SQ*qc:SQ*(qc+1)]), NCH)
        v_kr_done = p.cur("ve")
        p.wait_all("gpsimd", "dx")
        p.wait("gpsimd", "ve", v_kr_done)
        p.wait("gpsimd", "se", p.cur("se"))
        p.wait("gpsimd", "pe", p.cur("pe"))
        t_tc = dma(p, "gpsimd", ctx["trig_c"][:, :], cd["trig_c"].ap(), "dw")
        t_ts = dma(p, "gpsimd", ctx["trig_s"][:, :], cd["trig_s"].ap(), "dw")
        st["trig_toks"] = (t_tc, t_ts)
        t_k1 = dma(p, "gpsimd", ctx["krPr"][0:16, :], ctx["krSt"][16:32, :], "dw")
        t_k2 = dma(p, "gpsimd", ctx["krPr"][16:32, :], ctx["krSt"][0:16, :], "dw")
        for t in (t_tc, t_ts, t_k1, t_k2):
            p.waitt("vector", t)

        def rope_kr(v):
            v.tensor_mul(ctx["krSt"][:, :], ctx["krSt"][:, :],
                         ctx["trig_c"][64:96, :])
            v.tensor_mul(ctx["krPr"][:, :], ctx["krPr"][:, :],
                         ctx["trig_s"][64:96, :])
            return v.tensor_add(ctx["krSt"][:, :], ctx["krSt"][:, :],
                                ctx["krPr"][:, :])
        step(p, "vector", "ve", rope_kr)

        hi_end = 10 if wo_hi_preloaded else 16
        st["wo_hi_toks"] = [dma(p, "gpsimd", ctx[f"w_o{h}"][0:64, :],
                                wd["W_O"][64*h:64*(h+1), :], "dw")
                            for h in range(8, hi_end)]

    def emit_a_back(bt, tail_hook=None):
        proj = st["proj"]
        t_tc, t_ts = st["trig_toks"]
        bank_tok = st["bank_tok"]
        bank_tok[0] = bank_tok[1] = ("ve", p.cur("ve"))
        p.wait_all("vector", "dx")
        p.wait_all("scalar", "dx")

        def kr_fill(v):
            r = None
            for hh in range(NCH):
                if hh > 0:
                    v.tensor_copy(ctx[f"khead{hh}"][64:96, :], ctx["krSt"][:, :])
                r = v.memset(ctx[f"khead{hh}"][96:128, :], 0.0)
            return r
        step(p, "vector", "ve", kr_fill)

        for g in range(2):
            se_g0 = p.cur("se")
            nb_prev_rope = [p.cur("ve")] * 4
            if g == 1:
                for t in st.pop("wo_hi_toks", []) + st.pop("wo_pre_toks", []):
                    p.waitt("vector", t)

                def wo_zero_hi(v):
                    r = None
                    for h in range(8, 16):
                        r = v.memset(ctx[f"w_o{h}"][64:128, :], 0.0)
                    return r
                step(p, "vector", "ve", wo_zero_hi)
                p.wait_all("gpsimd", "dx")
                p.wait("gpsimd", "pe", p.cur("pe"))
                p.wait("gpsimd", "se", p.cur("se"))
                p.wait("gpsimd", "ve", p.cur("ve"))
                t_tc = dma(p, "gpsimd", ctx["trig_c"][:, :], cd["trig_c"].ap(), "dw")
                t_ts = dma(p, "gpsimd", ctx["trig_s"][:, :], cd["trig_s"].ap(), "dw")
            for hh in range(NCH):
                h = 8 * g + hh
                for qc in range(NQT):
                    def ev(s, psum, hh=hh, qc=qc):
                        return s.activation(
                            ctx[f"khead{hh}"][0:64, SQ*qc:SQ*(qc+1)],
                            psum[0:64, :], AF.Copy)
                    proj(ev, lambda l, h=h, qc=qc: (
                        ctx[f"w_uk{l}"][:, 64*h:64*h+128],
                        ctx[f"ckvT{l}"][:, SQ*qc:SQ*(qc+1)]), 2, eng="scalar")
            for hh in range(NCH):
                h = 8 * g + hh
                for qc in range(NQT):
                    def ev(s, psum, hh=hh, qc=qc):
                        return s.activation(
                            ctx[f"qhead{hh}"][0:96, SQ*qc:SQ*(qc+1)],
                            psum[0:96, :], AF.Copy)
                    proj(ev, lambda l, h=h, qc=qc: (
                        ctx[f"w_qp{l}"][:, 96*h:96*h+128],
                        ctx[f"cqT{l}"][:, SQ*qc:SQ*(qc+1)]), 2, eng="scalar")
                v_ev4 = p.cur("se")
                pq = ctx[f"pQ{hh % 4}"]
                p.wait("gpsimd", "se", v_ev4)
                p.wait("gpsimd", "ve", nb_prev_rope[hh % 4])
                p.wait("gpsimd", "se", se_g0)
                t1 = dma(p, "gpsimd", pq[0:16, :],
                         ctx[f"qhead{hh}"][80:96, :], "dw")
                t2 = dma(p, "gpsimd", pq[16:32, :],
                         ctx[f"qhead{hh}"][64:80, :], "dw")
                for t in (t_tc, t_ts, t1, t2):
                    p.waitt("vector", t)
                p.wait("vector", "se", v_ev4)

                def rope_q(v, hh=hh, pq=pq):
                    v.tensor_mul(ctx[f"qhead{hh}"][64:96, :],
                                 ctx[f"qhead{hh}"][64:96, :],
                                 ctx["trig_c"][64:96, :])
                    v.tensor_mul(pq[:, :], pq[:, :], ctx["trig_s"][64:96, :])
                    return v.tensor_add(ctx[f"qhead{hh}"][64:96, :],
                                        ctx[f"qhead{hh}"][64:96, :], pq[:, :])
                nb_prev_rope[hh % 4] = step(p, "vector", "ve", rope_q)
            p.wait("scalar", "ve", p.cur("ve"))
            for kt in range(NKT):
                def ev(s, psum, kt=kt):
                    dst = ctx[f"vh{kt}"].rearrange("k (h e) -> k h e", h=8)
                    src = psum.ap().rearrange("k (h e) -> k h e", h=8)
                    return s.activation(dst[:, :, 0:64], src[:, :, :], AF.Copy)
                proj(ev, lambda l, kt=kt, g=g: (
                    ctx[f"ckvT{l}"][:, 128*kt:128*(kt+1)],
                    ctx[f"w_uv{l}"][:, 512*g:512*(g+1)]), 2, eng="scalar")

            def vh_ones(v):
                r = None
                for kt in range(NKT):
                    dst = ctx[f"vh{kt}"].rearrange("k (h e) -> k h e", h=8)
                    r = v.memset(dst[:, :, 64:65], 1.0)
                return r
            step(p, "vector", "ve", vh_ones)
            v_rope_done = p.cur("ve")
            p.wait("tensor", "ve", v_rope_done)
            p.wait("tensor", "se", p.cur("se"))

            slots = [(ctx["ps_big"], 0), (ctx["ps_big"], 1024), (ctx["ps_wo"], 0)]
            psl = [(ctx["P0"], 0), (ctx["P0"], 1024), (ctx["P1"], 0), (ctx["P1"], 1024)]
            items = []
            for hh in range(8):
                for qt in range(NQT):
                    for k2 in range(2 * (qt + 1)):
                        items.append((hh, qt, k2))
            NI = len(items)
            v_sc = [0] * NI; v_ex = [0] * NI; v_cm = [0] * NI; v_pv = [0] * NI
            v_eu_prev = [p.cur("ve")]
            for i in range(NI + 1):
                if i < NI:
                    hh, qt, k2 = items[i]
                    kts = [2 * k2, 2 * k2 + 1]
                    sb, soff = slots[i % 3]
                    if i >= 3:
                        p.wait("tensor", "se", v_ex[i - 3])

                    def smms(te, kts=kts, hh=hh, qt=qt, sb=sb, soff=soff):
                        r = None
                        for t, kt in enumerate(kts):
                            r = te.matmul(sb[:, soff + SQ*t:soff + SQ*(t+1)],
                                          ctx[f"khead{hh}"][:, 128*kt:128*(kt+1)],
                                          ctx[f"qhead{hh}"][:, SQ*qt:SQ*(qt+1)],
                                          start=True, stop=True)
                        return r
                    v_sc[i] = step(p, "tensor", "pe", smms)
                    pb, poff = psl[i % 4]
                    p.wait("scalar", "pe", v_sc[i])
                    if i >= 4:
                        p.wait("scalar", "pe", v_pv[i - 4])

                    def expf(s, sb=sb, soff=soff, pb=pb, poff=poff):
                        return s.activation(pb[:, poff:poff + 1024],
                                            sb[:, soff:soff + 1024], AF.Exp,
                                            scale=INV_SQRT_DQK)
                    v_ex[i] = step(p, "scalar", "se", expf)
                    p.wait("vector", "se", v_ex[i])

                    def clipmask(v, pb=pb, poff=poff, kts=kts, qt=qt):
                        r = v.tensor_scalar(pb[:, poff:poff + 1024],
                                            pb[:, poff:poff + 1024],
                                            CLIP_HI, CLIP_LO, A.min, A.max)
                        for t, kt in enumerate(kts):
                            d = kt - 4 * qt
                            if 0 <= d < 4:
                                r = v.tensor_mul(pb[:, poff + SQ*t:poff + SQ*(t+1)],
                                                 pb[:, poff + SQ*t:poff + SQ*(t+1)],
                                                 ctx[f"dmask{d}"][:, :])
                        return r
                    v_cm[i] = step(p, "vector", "ve", clipmask)
                if i >= 1:
                    hh1, qt1, k21 = items[i - 1]
                    kts1 = [2 * k21, 2 * k21 + 1]
                    pb1, poff1 = psl[(i - 1) % 4]
                    p.wait("tensor", "ve", v_cm[i - 1])
                    if k21 == 0:
                        p.wait("tensor", "ve", v_eu_prev[0])

                    def pvmms(te, kts1=kts1, k21=k21, qt1=qt1, hh1=hh1,
                              pb1=pb1, poff1=poff1):
                        r = None
                        for t, kt in enumerate(kts1):
                            vsl = ctx["vh_t"].ap()[:, 520*kt + 65*hh1:
                                                   520*kt + 65*hh1 + 128]
                            r = te.matmul(ctx["ps_pv"][:, :], vsl,
                                          pb1[:, poff1 + SQ*t:poff1 + SQ*(t+1)],
                                          start=(k21 == 0 and t == 0),
                                          stop=(k21 == 2 * (qt1 + 1) - 1 and t == 1))
                        return r
                    v_pv[i - 1] = step(p, "tensor", "pe", pvmms)
                    if k21 == 2 * (qt1 + 1) - 1:
                        p.wait("vector", "pe", v_pv[i - 1])
                        v_eu = step(p, "vector", "ve",
                                    lambda v, hh1=hh1, qt1=qt1: v.tensor_copy(
                                        ctx[f"attnU{hh1}"][:, SQ*qt1:SQ*(qt1+1)],
                                        ctx["ps_pv"][0:65, :]))
                        v_eu_prev[0] = v_eu
                        if qt1 == NQT - 1:
                            h = 8 * g + hh1
                            p.wait("gpsimd", "ve", v_eu)
                            dma(p, "gpsimd", rsum_dram[h:h+1, :],
                                ctx[f"attnU{hh1}"][64:65, :], "dw")

            p.wait("vector", "pe", p.cur("pe"))
            v_ms = step(p, "vector", "ve", lambda v: v.memset(ctx["rsumS"], 1.0))
            p.wait("sync", "ve", v_ms)
            p.wait_all("sync", "dw")
            t_rs = dma(p, "sync", ctx["rsumS8"], rsum_dram[8*g:8*g+8, :], "dx")
            p.waitt("scalar", t_rs)
            p.wait("scalar", "ve", p.cur("ve"))

            def recf(s):
                lnf = ctx["ot0"][0:16, :]
                with nc.allow_low_precision("bf16 softmax normalizer"):
                    s.activation(lnf, ctx["rsumS"][:, 0:1024], AF.Ln)
                    s.activation(ctx["rec16n"][:, 0:1024], lnf, AF.Exp,
                                 scale=-1.0)
                    s.activation(lnf, ctx["rsumS"][:, 1024:2048], AF.Ln)
                    return s.activation(ctx["rec16n"][:, 1024:2048], lnf,
                                        AF.Exp, scale=-1.0)
            v_rec = step(p, "scalar", "se", recf)
            p.wait("sync", "se", v_rec)
            dma(p, "sync", rec_dram[8*g:8*g+8, :], ctx["rec8n"], "dx")
            p.wait_all("sync", "dx")
            nb_bufs = [ctx["AR3"].ap()[0:64, 0:2048], ctx["AR3"].ap()[0:64, 2048:4096]]
            nb_tok = [p.cur("ve"), p.cur("ve")]
            v_nm_last = p.cur("ve")
            for hh in range(8):
                h = 8 * g + hh
                buf = nb_bufs[hh % 2]
                p.wait("sync", "ve", nb_tok[hh % 2])
                t_b = dma(p, "sync", buf,
                          rec_dram[h:h+1, :].partition_broadcast(64), "dx")
                p.waitt("vector", t_b)
                v_nm = step(p, "vector", "ve", lambda v, hh=hh, buf=buf: v.tensor_mul(
                    ctx[f"attnU{hh}"][0:64, :], ctx[f"attnU{hh}"][0:64, :], buf))
                nb_tok[hh % 2] = v_nm
                v_nm_last = v_nm

            p.wait("tensor", "ve", v_nm_last)
            p.wait("tensor", "se", v_ex[NI - 1])
            accs = [ctx["ps_wo"].ap()[:, 0:1024], ctx["ps_big"].ap()[:, 0:1024]]
            wo_ev = [p.cur("ve"), p.cur("ve")]
            st_tok = {}
            for q128 in range(16):
                acc = accs[q128 % 2]
                p.wait("tensor", "ve", wo_ev[q128 % 2])

                def womms(te, q128=q128, g=g, acc=acc):
                    r = None
                    for half in range(2):
                        for hh in range(8):
                            h = 8 * g + hh
                            r = te.matmul(acc[:, 512*half:512*(half+1)],
                                          ctx[f"attnU128_{hh}"][:,
                                                                128*q128:128*(q128+1)],
                                          ctx[f"w_o{h}"][:, 512*half:512*(half+1)],
                                          start=(hh == 0), stop=(hh == 7))
                    return r
                v_wo = step(p, "tensor", "pe", womms)
                ot = ctx["ot0"] if q128 % 2 == 0 else ctx["ot1"]
                if g == 0:
                    p.wait("vector", "pe", v_wo)
                    if q128 >= 2:
                        p.waitt("vector", st_tok[q128 - 2])
                    v_ev = step(p, "vector", "ve",
                                lambda v, ot=ot, acc=acc: v.tensor_copy(ot, acc))
                else:
                    if q128 >= 2:
                        p.waitt("sync", st_tok[q128 - 2])
                    v_ld = dma(p, "sync", ot, attn_dram[128*q128:128*(q128+1), :], "dx")
                    p.waitt("vector", v_ld)
                    p.wait("vector", "pe", v_wo)
                    v_ev = step(p, "vector", "ve", lambda v, ot=ot, acc=acc: v.tensor_add(
                        ot, ot, acc))
                wo_ev[q128 % 2] = v_ev
                p.wait("sync", "ve", v_ev)
                st_tok[q128] = dma(p, "sync", attn_dram[128*q128:128*(q128+1), :], ot, "dx")
                if g == 1 and tail_hook is not None:
                    tail_hook()
            if g == 1 and tail_hook is not None:
                while tail_hook():
                    pass
            p.wait_all("vector", "dx")
            p.wait_all("tensor", "dx")

    def make_e_pro_feeder(bt):
        avls = [ctx["avh"], ctx["AR3"].ap()[0:64, 4096:6144]]
        avhs = [ctx["avh_hi"], ctx["AR3"].ap()[64:128, 4096:6144]]
        eqls = [ctx["eq64"], ctx["AR3"].ap()[0:64, 6144:8192]]
        eqhs = [ctx["eq64_hi"], ctx["AR3"].ap()[64:128, 6144:8192]]
        state = {"k": 0, "toks": {}, "mul": {}}

        def issue_pair(c):
            s_ = c % 2
            if c >= 2:
                p.wait("gpsimd", "ve", state["mul"][c - 2])
                p.wait("scalar", "ve", state["mul"][c - 2])
            v_lo = dma(p, "gpsimd", avls[s_].rearrange("d (s q) -> d s q", s=8),
                       ag2_out.ap().rearrange("(s d) (ll q) -> s d ll q", d=64, ll=16)
                       [:, :, 2*c, :].rearrange("s d q -> d s q"), "dw")
            v_hi = dma(p, "scalar", avhs[s_].rearrange("d (s q) -> d s q", s=8),
                       ag2_out.ap().rearrange("(s d) (ll q) -> s d ll q", d=64, ll=16)
                       [:, :, 2*c+1, :].rearrange("s d q -> d s q"), "ds")
            v_el = dma(p, "gpsimd", eqls[s_],
                       eq_dram[2*c:2*c+1, :].partition_broadcast(64), "dw")
            v_eh = dma(p, "scalar", eqhs[s_],
                       eq_dram[2*c+1:2*c+2, :].partition_broadcast(64), "ds")
            state["toks"][c] = (v_lo, v_hi, v_el, v_eh)

        def feed():
            k = state["k"]
            if k > 9:
                return False
            if k == 0:
                p.wait("gpsimd", "cc", st["v_cc2"])
                p.wait_all("gpsimd", "dx")
                p.wait("gpsimd", "ve", p.cur("ve"))
                p.wait("gpsimd", "se", p.cur("se"))
                p.wait("gpsimd", "pe", p.cur("pe"))
                p.wait("scalar", "pe", p.cur("pe"))
                v_ig = dma(p, "gpsimd",
                           ctx["idxG"].rearrange("l (s q) -> l s q", s=8),
                           ag1_out.ap().rearrange("(s r) q -> s r q", r=32)
                           [:, 16:32, :].rearrange("s l q -> l s q"), "dw")
                p.waitt("vector", v_ig)
                v_eqb = step(p, "vector", "ve",
                             lambda v, bt=bt: v.tensor_single_scalar(
                    ctx["eq64"][0:16, :], ctx["idxG"], ctx["jcol"][:, bt:bt+1],
                    A.is_equal))
                state["v_eqb"] = v_eqb
                p.wait("tensor", "ve", v_eqb)
                p.wait("tensor", "se", p.cur("se"))
                for sl in range(4):
                    psum = banks[nb[0] % 2]; nb[0] += 1
                    p.wait("tensor", "ve", p.cur("ve"))

                    def mm(te, sl=sl, psum=psum):
                        return te.matmul(psum[0:1, :], ctx["onesc"][0:16, 0:1],
                                         ctx["eq64"][0:16, 512*sl:512*(sl+1)],
                                         start=True, stop=True)
                    v_mm = step(p, "tensor", "pe", mm)
                    p.wait("vector", "pe", v_mm)
                    step(p, "vector", "ve",
                         lambda v, sl=sl, psum=psum: v.tensor_copy(
                             ctx["nsel"][:, 512*sl:512*(sl+1)], psum[0:1, :]))
                v_ns = step(p, "vector", "ve", lambda v: v.tensor_copy(
                    ctx["nselb"], ctx["nsel"]))
                p.wait("scalar", "ve", v_ns)
                dma(p, "scalar", eq_dram.ap(), ctx["eq64"][0:16, :], "ds")
                p.wait_all("gpsimd", "ds")
                p.wait_all("scalar", "ds")
                dma(p, "scalar", ctx["g_rep"], wd["g_rep"].ap(), "ds")
                dma(p, "scalar", ctx["b_rep"], wd["b_rep"].ap(), "ds")
            elif k == 1:
                p.wait("gpsimd", "ve", state["v_eqb"])
                p.wait("scalar", "ve", state["v_eqb"])
                issue_pair(0)
                issue_pair(1)
            else:
                c = k - 2
                s_ = c % 2
                for t in state["toks"][c]:
                    p.waitt("vector", t)

                def avmf(v, c=c, s_=s_):
                    v.tensor_mul(ctx[f"avmP{c}"][0:64, :], avls[s_], eqls[s_])
                    return v.tensor_mul(ctx[f"avmP{c}"][64:128, :], avhs[s_],
                                        eqhs[s_])
                state["mul"][c] = step(p, "vector", "ve", avmf)
                if c + 2 < NCH:
                    issue_pair(c + 2)
            state["k"] = k + 1
            return True
        return feed

    def emit_e(bt, feed=None, preload_wo_hi=False):
        stat_m = ctx[f"stat_m{bt}"]
        stat_r = ctx[f"stat_r{bt}"]
        p.wait("gpsimd", "pe", p.cur("pe"))
        p.wait("gpsimd", "ve", p.cur("ve"))
        p.wait_all("gpsimd", "dx")
        for c in range(NCH):
            dma(p, "gpsimd", ctx[f"wo_p{c}"][:, :],
                wd["Wo_f"][128*c:128*(c+1), :], "dw")
        if preload_wo_hi:
            st["wo_pre_toks"] = [dma(p, "gpsimd", ctx[f"w_o{h}"][0:64, :],
                                     wd["W_O"][64*h:64*(h+1), :], "dw")
                                 for h in range(10, 16)]
        p.wait_all("tensor", "dw")
        p.wait_all("tensor", "ds")
        p.wait("tensor", "ve", p.cur("ve"))
        def bmrf(v):
            v.tensor_mul(ctx["bmr"], stat_m[:, :], stat_r[:, :])
            return v.tensor_scalar_mul(ctx["bmr"], ctx["bmr"], -1.0)
        v_bmr = step(p, "vector", "ve", bmrf)
        p.wait("scalar", "ve", v_bmr)

        pse = [ctx["ps_wo"].ap()[:, :], ctx["ps_big"].ap()[:, 0:1024]]
        xts = [ctx["xt0"], ctx["xt1"]]
        ats = [ctx["at0"], ctx["at1"]]
        yts = [ctx["ytile"], ctx["yt1"]]
        v_bank = {0: p.cur("ve"), 1: p.cur("ve")}
        v_free = {0: p.cur("ve"), 1: p.cur("ve")}
        v_stats = {0: p.cur("ve"), 1: p.cur("ve")}
        st_out = {}
        e_toks = {}

        def e_first(tb):
            b = tb % 2
            slot = tb if tb < 8 else 15 - tb
            off = 0 if tb < 8 else 128
            scol = slot * 256 + off
            p.wait("gpsimd", "ve", v_free[b])
            v_x = dma(p, "gpsimd", xts[b], x_own[bt, 128*tb:128*(tb+1), :], "dw")
            v_a = dma(p, "gpsimd", ats[b], attn_dram[128*tb:128*(tb+1), :], "dw")
            p.wait("tensor", "ve", v_bank[b])

            def omms(te, scol=scol, acc=pse[b]):
                r = None
                for half in range(2):
                    for c in range(NCH):
                        te.matmul(acc[:, 512*half:512*(half+1)],
                                  ctx[f"avmP{c}"][:, scol:scol+128],
                                  ctx[f"wo_p{c}"][:, 512*half:512*(half+1)],
                                  start=(c == 0), stop=False)
                    r = te.matmul(acc[:, 512*half:512*(half+1)],
                                  ctx["nselb"][:, scol:scol+128],
                                  ctx["bo_sb"][:, 512*half:512*(half+1)],
                                  start=False, stop=True)
                return r
            v_om = step(p, "tensor", "pe", omms)
            p.waitt("scalar", v_x)
            if tb >= 2:
                p.waitt("scalar", st_out[tb - 2])

            def y0f(s, b=b, tb=tb):
                return s.activation(yts[b], xts[b], AF.Identity,
                                    scale=stat_r[:, tb:tb+1],
                                    bias=ctx["bmr"][:, tb:tb+1])
            v_y0 = step(p, "scalar", "se", y0f)
            p.wait("vector", "pe", v_om)
            p.wait("vector", "se", v_y0)

            def yaf(v, b=b):
                v.tensor_add(yts[b], yts[b], pse[b])
                return v.reduce_sum(ctx[f"ssum{b}"], yts[b], AX.X)
            v_y = step(p, "vector", "ve", yaf)
            v_bank[b] = v_y
            e_toks[tb] = (v_a, v_y)

        def e_second(tb):
            b = tb % 2
            v_a, v_y = e_toks[tb]
            p.wait("scalar", "ve", v_y)
            p.wait("scalar", "ve", v_stats[b])
            v_sq = step(p, "scalar", "se", lambda s, b=b: s.activation(
                ctx["xq_e"], yts[b], AF.Square, accum_out=ctx[f"ssq{b}"]))
            p.wait("vector", "se", v_sq)

            def statf(v, b=b):
                v.tensor_scalar_mul(ctx[f"tvar{b}"], ctx[f"ssum{b}"], 1.0/1024)
                v.tensor_mul(ctx[f"tmsq{b}"], ctx[f"tvar{b}"], ctx[f"tvar{b}"])
                return v.scalar_tensor_tensor(ctx[f"tmsq{b}"], ctx[f"ssq{b}"],
                                              1.0/1024, ctx[f"tmsq{b}"],
                                              A.mult, A.subtract)
            v_st2 = step(p, "vector", "ve", statf)
            v_stats[b] = v_st2
            p.wait("scalar", "ve", v_st2)

            def rstd(s, b=b):
                s.activation(ctx[f"tln{b}"], ctx[f"tmsq{b}"], AF.Ln,
                             bias=ctx["epsc"][:, :])
                return s.activation(ctx[f"tln{b}"], ctx[f"tln{b}"], AF.Exp,
                                    scale=-0.5)
            v_rs = step(p, "scalar", "se", rstd)
            p.wait("vector", "se", v_rs)

            def b2f(v, b=b):
                v.tensor_mul(ctx[f"bias2{b}"], ctx[f"tvar{b}"], ctx[f"tln{b}"])
                return v.tensor_scalar_mul(ctx[f"bias2{b}"], ctx[f"bias2{b}"],
                                           -1.0)
            v_b2 = step(p, "vector", "ve", b2f)
            p.wait("scalar", "ve", v_b2)
            v_f1 = step(p, "scalar", "se", lambda s, b=b: s.activation(
                yts[b], yts[b], AF.Identity, scale=ctx[f"tln{b}"],
                bias=ctx[f"bias2{b}"]))
            p.wait("vector", "se", v_f1)
            p.waitt("vector", v_a)

            def fin2(v, b=b):
                v.tensor_mul(yts[b], yts[b], ctx["g_rep"])
                v.tensor_add(yts[b], yts[b], ctx["b_rep"])
                v.tensor_add(yts[b], yts[b], xts[b])
                return v.tensor_add(yts[b], yts[b], ats[b])
            v_f2 = step(p, "vector", "ve", fin2)
            v_free[b] = v_f2
            p.wait("sync", "ve", v_f2)
            st_out[tb] = dma(p, "sync", out_own[bt, 128*tb:128*(tb+1), :],
                             yts[b], "dx")

        for it in range(17):
            if it < 16:
                e_first(it)
            if it >= 1:
                e_second(it - 1)
            if feed is not None:
                feed()
        if feed is not None:
            while feed():
                pass
        p.waitt("scalar", st_out[14])
        p.waitt("scalar", st_out[15])


    feed0 = make_ln_feeder(0)
    emit_f1_tail(feed0)
    emit_a_mid(0)
    emit_p1b()
    emit_a_back(0, tail_hook=make_e_pro_feeder(0))
    feed1 = make_ln_feeder(1, alt=True)
    emit_e(0, feed1, preload_wo_hi=True)
    emit_a_mid(1, wo_hi_preloaded=True)
    emit_a_back(1, tail_hook=make_e_pro_feeder(1))
    emit_e(1)

    p.wait_all("gpsimd", "dx")
    p.wait_all("gpsimd", "dw")
    p.wait_all("gpsimd", "dg")
    return p


def kernel(L0_attn_x, L0_attn_W_DQ, L0_attn_W_UQ, L0_attn_W_QR, L0_attn_W_DKV,
           L0_attn_W_UK, L0_attn_W_UV, L0_attn_W_KR, L0_attn_W_O,
           L0_ffn_attn_mask, L0_ffn_Wg, L0_ffn_Wq, L0_ffn_bq, L0_ffn_Wkv,
           L0_ffn_bkv, L0_ffn_Wo, L0_ffn_bo, L0_ffn_ln_g, L0_ffn_ln_b):
    x = np.ascontiguousarray(np.asarray(L0_attn_x, dtype=np.float32))
    cst = host_constants()

    nc = bass.Bass("TRN2", target_bir_lowering=False)
    p = build(nc)
    with nc.Block() as blk:
        p.build(blk)

    W_UQ = np.asarray(L0_attn_W_UQ, dtype=np.float32)
    W_QRp = permute_qr_cols(np.asarray(L0_attn_W_QR, dtype=np.float32))
    uqp = np.zeros((D_LAT, 1568), np.float32)
    for h in range(16):
        uqp[:, 96*h:96*h+64] = W_UQ[:, 64*h:64*h+64]
        uqp[:, 96*h+64:96*h+96] = W_QRp[:, 32*h:32*h+32]
    uqp[:, 1536:1568] = W_UQ[:, 0:32]
    W_UK = np.asarray(L0_attn_W_UK, dtype=np.float32)
    ukp = np.concatenate([W_UK, W_UK[:, 0:64]], axis=1)
    krp = np.zeros((D_MODEL, 128), np.float32)
    krp[:, 64:96] = permute_qr_cols(np.asarray(L0_attn_W_KR, dtype=np.float32))
    shared = {
        "W_DQ": np.ascontiguousarray(L0_attn_W_DQ, dtype=np.float32),
        "W_UQP": np.ascontiguousarray(uqp),
        "W_DKV": np.ascontiguousarray(L0_attn_W_DKV, dtype=np.float32),
        "W_UKP": np.ascontiguousarray(ukp),
        "W_UV": np.ascontiguousarray(L0_attn_W_UV, dtype=np.float32),
        "W_KRP": np.ascontiguousarray(krp),
        "W_O": np.ascontiguousarray(L0_attn_W_O, dtype=np.float32),
        "Wg": np.ascontiguousarray(L0_ffn_Wg, dtype=np.float32),
        "Wq_r": np.ascontiguousarray(np.asarray(L0_ffn_Wq, dtype=np.float32)
                                     .transpose(1, 0, 2).reshape(D_MODEL, 16 * 64)),
        "Wkv_r": np.ascontiguousarray(np.asarray(L0_ffn_Wkv, dtype=np.float32)
                                      .transpose(1, 0, 2).reshape(D_MODEL, 16 * 128)),
        "Wo_f": np.ascontiguousarray(np.asarray(L0_ffn_Wo, dtype=np.float32)
                                     .reshape(16 * 64, 1024)),
        "bq_td": np.ascontiguousarray(np.asarray(L0_ffn_bq, dtype=np.float32).T),
        "bkv_td": np.ascontiguousarray(np.asarray(L0_ffn_bkv, dtype=np.float32).T),
        "g_rep": np.ascontiguousarray(np.broadcast_to(
            np.asarray(L0_ffn_ln_g, dtype=np.float32)[None, :], (128, 1024))),
        "b_rep": np.ascontiguousarray(np.broadcast_to(
            np.asarray(L0_ffn_ln_b, dtype=np.float32)[None, :], (128, 1024))),
        "bo_row": np.ascontiguousarray(
            np.asarray(L0_ffn_bo, dtype=np.float32)[None, :]),
    }
    for k, v in cst.items():
        shared["c_" + k] = np.ascontiguousarray(v)

    in_maps = []
    for c in range(NC):
        blocks = [c, 15 - c]
        m = dict(shared)
        m["x_own"] = np.ascontiguousarray(x[2 * c:2 * c + 2])
        xp = np.empty((16, NBLK, 128, D_MODEL), np.float32)
        for bi, tb in enumerate(blocks):
            xp[:, bi] = x[:, 128 * tb:128 * (tb + 1), :]
        m["x_pos"] = xp
        basearr = np.zeros((128, NBLK), np.float32)
        for bi in range(NBLK):
            basearr[:, bi] = bi * 2048 + np.arange(128)
        m["c_base"] = basearr
        jc = np.zeros((16, 2), np.float32)
        jc[:, 0] = 2 * c
        jc[:, 1] = 2 * c + 1
        m["c_jcol"] = jc
        in_maps.append(m)

    res = run_bass_kernel_spmd(nc, in_maps, core_ids=list(range(NC)))
    out = np.concatenate([res.results[c]["out_own"] for c in range(NC)], axis=0)
    return out


# version 29
# speedup vs baseline: 1.0591x; 1.0084x over previous
import numpy as np
import ml_dtypes
import concourse.bass as bass
import concourse.mybir as mybir
from concourse.bass_core_rust import IndirectOffsetOnAxis
from concourse.bass_utils import run_bass_kernel_spmd

BF16 = mybir.dt.bfloat16
F32 = mybir.dt.float32
U32 = mybir.dt.uint32
AF = mybir.ActivationFunctionType
A = mybir.AluOpType
AX = mybir.AxisListType
bf = ml_dtypes.bfloat16

D_MODEL, N_HEADS, D_H, D_HR, D_LAT = 1024, 16, 64, 32, 256
S, B, NC = 2048, 16, 8
SQ = 512
NKT, NQT, NCH = 16, 4, 8
NBLK, NPOS = 2, 256
NTOK = 16 * NPOS
INV_SQRT_DQK = 1.0 / np.sqrt(96.0)
CLIP_HI = float(np.exp(np.float32(80.0) * INV_SQRT_DQK))
CLIP_LO = float(np.exp(np.float32(-80.0) * INV_SQRT_DQK))
EPS = 1e-5
JBIG = 16.0


class Prog:
    ENGINES = ("sync", "scalar", "vector", "tensor", "gpsimd")

    def __init__(self, nc):
        self.nc = nc
        self.ops = {e: [] for e in self.ENGINES}
        self.sems = {}
        self.counts = {}
        self._rot = {}

    def sem(self, name):
        if name not in self.sems:
            self.sems[name] = self.nc.semaphore(name).__enter__()
            self.counts[name] = 0
        return self.sems[name]

    def emit(self, engine, fn):
        self.ops[engine].append(fn)

    def inc(self, name, by):
        h = self.sem(name)
        self.counts[name] += by
        return h, self.counts[name]

    def cur(self, name):
        self.sem(name)
        return self.counts[name]

    def wait(self, engine, name, val):
        h = self.sem(name)
        self.emit(engine, lambda e, h=h, v=val: e.wait_ge(h, v))

    def waitt(self, engine, token):
        name, v = token
        h = self.sem(name)
        self.emit(engine, lambda e, h=h, v=v: e.wait_ge(h, v))

    def wait_all(self, engine, family):
        for i in range(NROT):
            name = f"{family}.{i}"
            if name in self.sems and self.counts[name] > 0:
                h = self.sems[name]
                self.emit(engine, lambda e, h=h, v=self.counts[name]: e.wait_ge(h, v))

    def build(self, blk):
        for name in self.ENGINES:
            stream = self.ops[name]
            ser_sem = self.sem("_ser_" + name) if name in ("vector", "scalar") else None

            def make(stream, ser_sem):
                def run(eng):
                    e = SerEng(eng, ser_sem) if ser_sem is not None else eng
                    for fn in stream:
                        fn(e)
                return run
            getattr(blk, name)(make(stream, ser_sem))


_EMITTING = {
    "tensor_copy", "tensor_tensor", "tensor_add", "tensor_sub", "tensor_mul",
    "tensor_max", "tensor_scalar", "tensor_single_scalar", "tensor_scalar_add",
    "tensor_scalar_sub", "tensor_scalar_mul", "tensor_scalar_max",
    "tensor_scalar_min", "tensor_relu", "tensor_reduce", "reduce_max",
    "reduce_sum", "scalar_tensor_tensor", "tensor_tensor_scan", "memset",
    "memzero", "select", "copy_predicated", "transpose", "bn_stats", "bn_aggr",
    "max", "max_index", "pool", "pool_avg", "pool_max", "reciprocal",
    "stream_shuffle", "match_replace", "activation", "add", "mul", "sqrt",
    "square", "sign", "copy", "iota", "affine_select",
}


class SerEng:

    def __init__(self, eng, sem):
        self._eng = eng
        self._sem = sem
        self._n = 0
        self._waited = 0
        self._pend = []

    @staticmethod
    def _span(a):
        try:
            lst = a.ap
            if callable(lst):
                return None
            off = a.offset
            esz = mybir.dt.size(a.dtype)
            lo = off
            hi = off
            for st, cnt in lst[1:]:
                d = st * (cnt - 1)
                if d < 0:
                    lo += d
                else:
                    hi += d
            return (a.tensor.name, lo * esz, (hi + 1) * esz)
        except Exception:
            return None

    def _need(self, rds, wrs):
        need = 0
        for (nm, lo, hi) in wrs:
            for (pn, pl, ph, pw, seq) in self._pend:
                if pn == nm and lo < ph and pl < hi and seq > need:
                    need = seq
        for (nm, lo, hi) in rds:
            for (pn, pl, ph, pw, seq) in self._pend:
                if pw and pn == nm and lo < ph and pl < hi and seq > need:
                    need = seq
        return need

    def _wait_to(self, need):
        if need > self._waited:
            self._eng.wait_ge(self._sem, need)
            self._waited = need
            self._pend = [e for e in self._pend if e[4] > need]

    def flush(self):
        self._wait_to(self._n)

    def __getattr__(self, name):
        f = getattr(self._eng, name)
        if name not in _EMITTING:
            return f

        def wrapped(*a, **k):
            aps = []
            bad = False
            for x in a:
                if hasattr(x, "tensor"):
                    s = self._span(x)
                    if s is None:
                        bad = True
                    aps.append(s)
                else:
                    aps.append(None) if False else None
            kw_aps = []
            for kk, x in k.items():
                if hasattr(x, "tensor"):
                    s = self._span(x)
                    if s is None:
                        bad = True
                    kw_aps.append((kk, s))
            ap_pos = [s for s in aps if s is not None]
            if bad:
                self.flush()
                wrs, rds = [], []
            else:
                wrs = ap_pos[:1] + [s for kk, s in kw_aps if kk == "accum_out" and s]
                rds = ap_pos[1:] + [s for kk, s in kw_aps if kk != "accum_out" and s]
                self._wait_to(self._need(rds, wrs))
            inst = f(*a, **k)
            self._n += 1
            inst.then_inc(self._sem, 1)
            for s in wrs:
                self._pend.append((s[0], s[1], s[2], True, self._n))
            for s in rds:
                self._pend.append((s[0], s[1], s[2], False, self._n))
            if bad:
                self.flush()
            return inst
        return wrapped


NROT = 4


def _dma_common(p, engine, family, emit_fn):
    i = p._rot.get(family, 0)
    p._rot[family] = i + 1
    name = f"{family}.{i % NROT}"
    h, v = p.inc(name, 16)
    if v > 16:
        p.emit(engine, lambda e, h=h, w=v - 16: e.wait_ge(h, w))
    p.emit(engine, lambda e, h=h: emit_fn(e).then_inc(h, 16))
    return (name, v)


def dma(p, engine, out_ap, in_ap, family):
    return _dma_common(p, engine, family,
                       lambda e, o=out_ap, i=in_ap: e.dma_start(out=o, in_=i))


def dma_t(p, engine, out_ap, in_ap, family):
    return _dma_common(p, engine, family,
                       lambda e, o=out_ap, i=in_ap: e.dma_start_transpose(o, i))


def dma_ind(p, engine, out_ap, src_ap, idx_ap, family):
    return _dma_common(p, engine, family,
                       lambda e, o=out_ap, s=src_ap, x=idx_ap: e.indirect_dma_start(
                           o, None, s, IndirectOffsetOnAxis(x, 0)))


def step(p, engine, sem_name, fn, by=1):
    h, v = p.inc(sem_name, by)
    if engine in ("vector", "scalar"):
        def g(e, h=h, by=by, fn=fn):
            fn(e)
            e.flush()
            e.sem_inc(h, by)
        p.emit(engine, g)
    else:
        p.emit(engine, lambda e, h=h, fn=fn, by=by: fn(e).then_inc(h, by))
    return v


def host_constants():
    c = {}
    inv = 10000.0 ** (-np.arange(0, D_HR, 2, dtype=np.float64) / D_HR)
    ang = np.arange(S, dtype=np.float64)[:, None] * inv[None, :]
    cos = np.cos(ang).T.astype(np.float32)
    sin = np.sin(ang).T.astype(np.float32)
    cos32 = np.concatenate([cos, cos], 0)
    sin32 = np.concatenate([-sin, sin], 0)
    c["trig_c"] = np.tile(cos32, (4, 1)).astype(bf)
    c["trig_s"] = np.tile(sin32, (4, 1)).astype(bf)
    q = np.arange(SQ)[None, :]
    k = np.arange(128)[:, None]
    for j in range(4):
        c[f"dmask{j}"] = (q >= (k + 128 * j)).astype(bf)
    oh = np.zeros((16, 1024), bf)
    for h in range(16):
        oh[h, 64 * h:64 * (h + 1)] = 1
    c["onehot16"] = oh
    c["ident128"] = np.eye(128, dtype=bf)
    jt = np.zeros((16, NTOK), bf)
    for t in range(NTOK):
        jt[:, t] = (t // 128) % 16
    c["jtile"] = jt
    return c


def permute_qr_cols(w):
    nh = w.shape[1] // D_HR
    w = w.reshape(w.shape[0], nh, D_HR)
    perm = np.concatenate([np.arange(0, D_HR, 2), np.arange(1, D_HR, 2)])
    return np.ascontiguousarray(w[:, :, perm].reshape(w.shape[0], nh * D_HR))


def build(nc):
    p = Prog(nc)

    x_own = nc.dram_tensor("x_own", [2, S, D_MODEL], F32, kind="ExternalInput")
    x_pos = nc.dram_tensor("x_pos", [16, NBLK, 128, D_MODEL], F32, kind="ExternalInput")
    out_own = nc.dram_tensor("out_own", [2, S, D_MODEL], F32, kind="ExternalOutput")

    w_shapes = {"W_DQ": (D_MODEL, D_LAT), "W_UQP": (D_LAT, 1568),
                "W_DKV": (D_MODEL, D_LAT), "W_UKP": (D_LAT, 1088),
                "W_UV": (D_LAT, 1024),
                "W_KRP": (D_MODEL, 128), "W_O": (1024, D_MODEL),
                "Wg": (D_MODEL, 16), "Wq_r": (D_MODEL, 1024), "Wkv_r": (D_MODEL, 2048),
                "Wo_f": (16 * 64, 1024),
                "bq_td": (64, 16), "bkv_td": (128, 16),
                "g_rep": (128, 1024), "b_rep": (128, 1024), "bo_row": (1, 1024)}
    wd = {n: nc.dram_tensor(n, list(sh), F32, kind="ExternalInput")
          for n, sh in w_shapes.items()}
    cst = host_constants()
    cd = {k: nc.dram_tensor("c_" + k, list(v.shape), BF16, kind="ExternalInput")
          for k, v in cst.items()}
    base_d = nc.dram_tensor("c_base", [128, NBLK], F32, kind="ExternalInput")
    jcol_d = nc.dram_tensor("c_jcol", [16, 2], F32, kind="ExternalInput")

    xn_dram = nc.dram_tensor("xn_dram", [S, D_MODEL], BF16)
    xnF_dram = nc.dram_tensor("xnF_dram", [NTOK, D_MODEL], BF16)
    xnlo_dram = nc.dram_tensor("xnlo_dram", [NTOK, D_MODEL], BF16)
    lnz_dram = nc.dram_tensor("lnz_dram", [1, NTOK], F32)
    hg_dram = nc.dram_tensor("hg_dram", [NTOK, D_MODEL], BF16)
    idx_sc = nc.dram_tensor("idx_sc", [16, NPOS], BF16)
    attn_dram = nc.dram_tensor("attn_dram", [S, D_MODEL], F32)
    eq_dram = nc.dram_tensor("eq_dram", [16, S], BF16)
    rec_dram = nc.dram_tensor("rec_dram", [16, S], BF16)
    rsum_dram = nc.dram_tensor("rsum_dram", [16, S], BF16)
    ag1_in = nc.dram_tensor("ag1_in", [32, NPOS], F32)
    ag1_out = nc.dram_tensor("ag1_out", [NC * 32, NPOS], F32)
    ag2_in = nc.dram_tensor("ag2_in", [64, NTOK], BF16)
    ag2_out = nc.dram_tensor("ag2_out", [NC * 64, NTOK], BF16)

    ctx = {}

    def T(name, shape, dtype=BF16):
        h = nc.sbuf_tensor(name, shape, dtype).__enter__()
        ctx[name] = h
        return h

    for c in range(NCH):
        T(f"w_dq{c}", [128, D_LAT]); T(f"w_dkv{c}", [128, D_LAT])
    for l in range(2):
        T(f"w_qp{l}", [128, 1568])
        T(f"w_uk{l}", [128, 1088]); T(f"w_uv{l}", [128, 1024])
    T("wg_all", [128, 8 * 16])
    T("onesf", [16, 1], F32)
    for c in range(NCH):
        ctx[f"wg{c}"] = ctx["wg_all"].ap()[:, 16 * c:16 * (c + 1)]
    for j in range(4):
        T(f"dmask{j}", [128, SQ])
    T("onehot16", [16, 1024])
    T("onesc", [64, 1]); T("onesr", [1, 64])
    T("epsc", [128, 1], F32)
    T("base", [128, NBLK], F32)
    T("jcol", [16, 2], F32)
    T("bq_t", [64, 16], F32); T("bkv_t", [128, 16], F32)
    T("bo_sb", [1, 1024])
    T("ssum", [128, 1], F32); T("ssq", [128, 1], F32)
    T("tvar", [128, 1], F32); T("tln", [128, 1], F32); T("tmsq", [128, 1], F32)

    for b in range(2):
        T(f"stat_m{b}", [128, 16], F32); T(f"stat_r{b}", [128, 16], F32)
    T("mx", [16, NPOS], F32)
    T("idxf", [16, NPOS], F32)
    for b in range(NBLK):
        T(f"idxT{b}", [128, 16])
        T(f"rt{b}", [128, 16], F32)
        T(f"rtu{b}", [128, 16], U32)
    for par in range(2):
        rt = ctx[f"rt{par}"]
        ctx[f"ssum{par}"] = rt.ap()[:, 5:6]
        ctx[f"ssq{par}"] = rt.ap()[:, 6:7]
        ctx[f"tvar{par}"] = rt.ap()[:, 7:8]
        ctx[f"tln{par}"] = rt.ap()[:, 8:9]
        ctx[f"tmsq{par}"] = rt.ap()[:, 9:10]
        ctx[f"bias2{par}"] = rt.ap()[:, 10:11]
        ctx[f"lssum{par}"] = rt.ap()[:, 11:12]
        ctx[f"lssq{par}"] = rt.ap()[:, 12:13]
        ctx[f"ltvar{par}"] = rt.ap()[:, 13:14]
        ctx[f"ltln{par}"] = rt.ap()[:, 14:15]
        ctx[f"ltmsq{par}"] = rt.ap()[:, 15:16]
    ctx["bmr"] = ctx["rtu0"].ap().bitcast(F32)

    T("w16", [16, NPOS], F32)
    T("w16b", [16, NPOS])
    T("zf", [16, 1], F32)
    T("bias2", [128, 1], F32)

    T("AR1", [128, 8192])
    ctx["xa0"] = ctx["AR1"].ap().bitcast(F32)[:, 0:1024]
    ctx["xa1"] = ctx["AR1"].ap().bitcast(F32)[:, 1024:2048]
    ctx["zrow"] = ctx["AR1"].ap().bitcast(F32)[0:1, 0:4096]
    ctx["ot0"] = ctx["AR1"].ap().bitcast(F32)[:, 1024:2048]

    T("AR2", [128, 16384])
    ctx["lg"] = ctx["AR2"].ap().bitcast(F32)[0:16, 0:4096]
    ctx["M"] = ctx["lg"]
    ctx["egf"] = ctx["AR2"].ap().bitcast(F32)[0:16, 4096:8192]
    ctx["lnzbc"] = ctx["egf"]
    ctx["xnf32"] = ctx["AR2"].ap().bitcast(F32)[:, 0:1024]
    ctx["eg"] = ctx["AR2"].ap()[0:16, 8192:12288]
    ctx["jtile"] = ctx["AR2"].ap()[0:16, 12288:16384]
    for c in range(NCH):
        ctx[f"wkv{c}"] = ctx["AR2"].ap()[:, 2048 * c:2048 * (c + 1)]
    for h in range(16):
        ctx[f"w_o{h}"] = ctx["AR2"].ap()[:, 1024 * h:1024 * (h + 1)]

    T("AR3", [128, 16384])
    ctx["eq"] = ctx["AR3"].ap()[0:64, 0:4096]
    ctx["cand"] = ctx["AR3"].ap()[0:64, 4096:8192]
    ctx["qall"] = ctx["cand"]
    ctx["tt"] = ctx["eq"]
    ctx["attG"] = ctx["AR3"].ap().bitcast(F32)[0:16, 2048:4096]
    ctx["kvall"] = ctx["AR3"].ap()[:, 8192:12288]
    ctx["hgt0"] = ctx["AR3"].ap()[:, 12288:13312]
    ctx["hgt1"] = ctx["AR3"].ap()[:, 13312:14336]
    for hh in range(NCH):
        ctx[f"khead{hh}"] = ctx["AR3"].ap()[:, S * hh:S * (hh + 1)]
    ctx["avh"] = ctx["AR3"].ap()[0:64, 0:2048]
    ctx["avh_hi"] = ctx["AR3"].ap()[64:128, 0:2048]
    ctx["eq64"] = ctx["AR3"].ap()[0:64, 2048:4096]
    ctx["eq64_hi"] = ctx["AR3"].ap()[64:128, 2048:4096]
    ctx["idxG"] = ctx["AR3"].ap().bitcast(F32)[0:16, 2048:4096]
    ctx["nsel"] = ctx["AR3"].ap().bitcast(F32)[0:1, 4096:6144]
    ctx["nselb"] = ctx["AR3"].ap()[0:1, 12288:14336]
    ctx["xq_e"] = ctx["AR3"].ap()[:, 0:1024]
    ctx["xt0"] = ctx["AR2"].ap().bitcast(F32)[:, 0:1024]
    ctx["xt1"] = ctx["AR2"].ap().bitcast(F32)[:, 1024:2048]
    ctx["at0"] = ctx["AR2"].ap().bitcast(F32)[:, 2048:3072]
    ctx["at1"] = ctx["AR2"].ap().bitcast(F32)[:, 3072:4096]
    ctx["yt1"] = ctx["AR2"].ap().bitcast(F32)[:, 4096:5120]

    for c in range(NCH):
        T(f"xnT{c}", [128, 2 * S])
    for h2 in range(2):
        ctx[f"cqT{h2}"] = ctx[f"xnT{4 + h2}"].ap()[:, S:2 * S]
        ctx[f"ckvT{h2}"] = ctx[f"xnT{6 + h2}"].ap()[:, S:2 * S]
    for hh in range(NCH):
        ctx[f"qhead{hh}"] = ctx[f"xnT{hh}"].ap()[:, 0:S]
    for c in range(NCH):
        ctx[f"avmP{c}"] = ctx[f"xnT{c}"].ap()[:, 0:S]
        ctx[f"wo_p{c}"] = ctx["AR1"].ap()[:, 1024 * c:1024 * (c + 1)]

    T("attnU_t", [128, 2048])
    ctx["wgf"] = ctx["attnU_t"].ap().bitcast(F32)[:, 0:128]
    ctx["ident128"] = ctx["attnU_t"].ap()[:, 1024:1152]
    ctx["wglo_all"] = ctx["attnU_t"].ap()[:, 512:640]
    for c in range(NCH):
        ctx[f"wglo{c}"] = ctx["wglo_all"][:, 16 * c:16 * (c + 1)]
    for c in range(NCH):
        ctx[f"w_krp{c}"] = ctx["attnU_t"].ap()[:, 128 * c:128 * (c + 1)]
    for hh in range(4):
        ctx[f"attnU{hh}"] = ctx[f"xnT{hh}"].ap()[0:65, S:2 * S]
        ctx[f"attnU128_{hh}"] = ctx[f"xnT{hh}"].ap()[:, S:2 * S]
    ctx["attnU4"] = ctx["AR1"].ap()[0:65, 4096:6144]
    ctx["attnU128_4"] = ctx["AR1"].ap()[:, 4096:6144]
    ctx["attnU5"] = ctx["AR1"].ap()[0:65, 6144:8192]
    ctx["attnU128_5"] = ctx["AR1"].ap()[:, 6144:8192]
    ctx["attnU6"] = ctx["attnU_t"].ap()[0:65, 0:S]
    ctx["attnU128_6"] = ctx["attnU_t"].ap()[:, 0:S]
    ctx["attnU7"] = ctx["AR1"].ap()[0:65, 0:2048]
    ctx["attnU128_7"] = ctx["AR1"].ap()[:, 0:2048]

    T("vh_t", [128, 16 * 520 + 64])
    for kt in range(NKT):
        ctx[f"vh{kt}"] = ctx["vh_t"].ap()[:, 520 * kt:520 * (kt + 1)]
    ctx["g_rep"] = ctx["vh_t"].ap().bitcast(F32)[:, 0:1024]
    ctx["b_rep"] = ctx["vh_t"].ap().bitcast(F32)[:, 1024:2048]
    for c in range(NCH):
        ctx[f"wq{c}"] = ctx["vh_t"].ap()[:, 1024 * c:1024 * (c + 1)]
    ctx["krSt"] = ctx["AR3"].ap()[64:96, 0:2048]
    ctx["krPr"] = ctx["AR3"].ap()[64:96, 2048:4096]
    ctx["pQ0"] = ctx["vh_t"].ap()[64:96, 2048:4096]
    ctx["pQ1"] = ctx["vh_t"].ap()[64:96, 4096:6144]
    ctx["pQ2"] = ctx["vh_t"].ap()[64:96, 6144:8192]
    ctx["pQ3"] = ctx["vh_t"].ap()[64:96, 0:2048]

    T("P0", [128, 4 * SQ]); T("P1", [128, 4 * SQ])
    ctx["trig_c"] = ctx["P0"]; ctx["trig_s"] = ctx["P1"]
    ctx["xq0"] = ctx["P0"].ap()[:, 0:1024]
    ctx["xq1"] = ctx["P1"].ap()[:, 1024:2048]
    ctx["xn0"] = ctx["P0"].ap()[:, 1024:2048]
    ctx["xn1"] = ctx["P1"].ap()[:, 0:1024]
    ctx["xnlo"] = ctx["P1"].ap()[:, 1024:2048]
    ctx["ytile"] = ctx["P0"].ap().bitcast(F32)[:, 0:1024]
    ctx["xtile"] = ctx["P1"].ap().bitcast(F32)[:, 0:1024]
    ctx["ot1"] = ctx["P0"].ap().bitcast(F32)[:, 0:1024]
    ctx["rsumS"] = ctx["P0"].ap()[0:16, 0:2048]
    ctx["rsumS8"] = ctx["P0"].ap()[0:8, 0:2048]
    ctx["rec16n"] = ctx["P1"].ap()[0:16, 0:2048]
    ctx["rec8n"] = ctx["P1"].ap()[0:8, 0:2048]

    ps_big = nc.psum_tensor([128, 2048], F32).__enter__()
    ps_a = nc.psum_tensor([128, 512], F32).__enter__()
    ps_pv = nc.psum_tensor([128, 512], F32).__enter__()
    ps_wo = nc.psum_tensor([128, 1024], F32).__enter__()
    ctx.update(ps_big=ps_big, ps_a=ps_a, ps_pv=ps_pv, ps_wo=ps_wo)
    banks = [ps_a, ps_pv]
    nb = [0]

    for c in range(NCH):
        dma(p, "gpsimd", ctx[f"w_dq{c}"][:, :], wd["W_DQ"][128*c:128*(c+1), :], "dw")
        dma(p, "gpsimd", ctx[f"w_dkv{c}"][:, :], wd["W_DKV"][128*c:128*(c+1), :], "dw")
        dma(p, "gpsimd", ctx["wgf"][:, 16*c:16*(c+1)], wd["Wg"][128*c:128*(c+1), :], "dw")
    for l in range(2):
        dma(p, "gpsimd", ctx[f"w_qp{l}"][:, :], wd["W_UQP"][128*l:128*(l+1), :], "dw")
        dma(p, "gpsimd", ctx[f"w_uk{l}"][:, :], wd["W_UKP"][128*l:128*(l+1), :], "dw")
        dma(p, "gpsimd", ctx[f"w_uv{l}"][:, :], wd["W_UV"][128*l:128*(l+1), :], "dw")
    for c in range(NCH):
        dma(p, "gpsimd", ctx[f"wq{c}"], wd["Wq_r"][128*c:128*(c+1), :], "dw")
    for c in range(2, NCH):
        dma(p, "gpsimd", ctx[f"wkv{c}"][16:128, :],
            wd["Wkv_r"][128*c+16:128*(c+1), :], "dw")
    for k in ("dmask0", "dmask1", "dmask2", "dmask3", "onehot16"):
        dma(p, "gpsimd", ctx[k][:, :], cd[k].ap(), "dw")
    dma(p, "gpsimd", ctx["base"][:, :], base_d.ap(), "dw")
    dma(p, "gpsimd", ctx["jcol"][:, :], jcol_d.ap(), "dw")
    dma(p, "gpsimd", ctx["bq_t"][:, :], wd["bq_td"].ap(), "dw")
    dma(p, "gpsimd", ctx["bkv_t"][:, :], wd["bkv_td"].ap(), "dw")
    dma(p, "gpsimd", ctx["bo_sb"][:, :], wd["bo_row"].ap(), "dw")

    def init0(v):
        v.memset(ctx["epsc"][:, :], EPS)
        v.memset(ctx["onesc"][:, :], 1.0)
        v.memset(ctx["onesr"][:, :], 1.0)
        v.memset(ctx["onesf"][:, :], 1.0)
        return v.memset(ctx["vh_t"][:, 16 * 520:16 * 520 + 64], 0.0)
    step(p, "vector", "ve", init0)
    p.wait_all("vector", "dw")

    def wgsplit(v):
        v.tensor_copy(ctx["wg_all"][:, :], ctx["wgf"][:, :])
        return v.tensor_sub(ctx["wglo_all"][:, :], ctx["wgf"][:, :],
                            ctx["wg_all"][:, :])
    step(p, "vector", "ve", wgsplit)

    def ln_tile(xa, xn_out, m_ap=None, r_ap=None, par=0, pfx="", xq=None):
        sm = ctx[f"{pfx}ssum{par}"]
        sq = ctx[f"{pfx}ssq{par}"]
        tv = ctx[f"{pfx}tvar{par}"]
        tl = ctx[f"{pfx}tln{par}"]
        tm = ctx[f"{pfx}tmsq{par}"]
        if xq is None:
            xq = ctx[f"xq{par}"]
        step(p, "vector", "ve", lambda v, xa=xa, sm=sm: v.reduce_sum(sm, xa, AX.X))
        v_sq = step(p, "scalar", "se", lambda s, xa=xa, sq=sq, xq=xq: s.activation(
            xq, xa, AF.Square, accum_out=sq))
        p.wait("vector", "se", v_sq)
        if m_ap is None:
            m_ap = tv

        def stats(v, m_ap=m_ap, sm=sm, sq=sq, tm=tm):
            v.tensor_scalar_mul(m_ap, sm, 1.0 / 1024)
            v.tensor_mul(tm, m_ap, m_ap)
            return v.scalar_tensor_tensor(tm, sq, 1.0 / 1024, tm,
                                          A.mult, A.subtract)
        v_st = step(p, "vector", "ve", stats)
        p.wait("scalar", "ve", v_st)
        if r_ap is None:
            r_ap = tl

        def rstd(s, r_ap=r_ap, tm=tm, tl=tl):
            s.activation(tl, tm, AF.Ln, bias=ctx["epsc"][:, :])
            return s.activation(r_ap, tl, AF.Exp, scale=-0.5)
        v_rs = step(p, "scalar", "se", rstd)
        p.wait("vector", "se", v_rs)
        return step(p, "vector", "ve", lambda v, xa=xa, xn=xn_out, m=m_ap, r=r_ap:
                    v.tensor_scalar(xn, xa, m, r, A.subtract, A.mult))

    st1_toks, st2_toks = {}, {}
    for t in range(32):
        blk, j = t // 16, t % 16
        par = t % 2
        xa = ctx["xa0"] if par == 0 else ctx["xa1"]
        xn = ctx["xn0"] if par == 0 else ctx["xn1"]
        xnf = (ctx["AR2"].ap().bitcast(F32)[:, 0:1024] if par == 0
               else ctx["AR2"].ap().bitcast(F32)[:, 1024:2048])
        xnlo = (ctx["AR3"].ap()[:, 0:1024] if par == 0
                else ctx["AR3"].ap()[:, 1024:2048])
        if t >= 2:
            p.waitt("sync", st1_toks[t - 2])
            p.waitt("sync", st2_toks[t - 2])
        v_load = dma(p, "sync", xa, x_pos[j, blk, :, :], "dx")
        p.waitt("vector", v_load)
        p.waitt("scalar", v_load)
        v_no = ln_tile(xa, xnf, par=par)

        def split(v, xn=xn, xnf=xnf, xnlo=xnlo):
            v.tensor_copy(xn, xnf)
            return v.tensor_sub(xnlo, xnf, xn)
        v_sp = step(p, "vector", "ve", split)
        p.wait("sync", "ve", v_sp)
        st1_toks[t] = dma(p, "sync", xnF_dram[128*t:128*(t+1), :], xn, "dx")
        st2_toks[t] = dma(p, "sync", xnlo_dram[128*t:128*(t+1), :], xnlo, "dx")
    p.wait_all("sync", "dx")
    for c in range(NCH):
        dma_t(p, "sync", ctx[f"xnT{c}"][:, 0:NTOK], xnF_dram[:, 128*c:128*(c+1)], "dx")
    p.wait("gpsimd", "ve", p.cur("ve"))
    p.wait("gpsimd", "se", p.cur("se"))
    for c in range(2):
        dma(p, "gpsimd", ctx[f"wkv{c}"][16:128, :],
            wd["Wkv_r"][128*c+16:128*(c+1), :], "dw")
    p.wait_all("tensor", "dx")
    p.wait_all("tensor", "dw")
    p.wait("tensor", "ve", p.cur("ve"))

    f1_tok = {0: p.cur("ve"), 1: p.cur("ve")}
    for sl in range(NTOK // 512):
        bi = nb[0] % 2; psum = banks[bi]; nb[0] += 1
        p.wait("tensor", "ve", f1_tok[bi])

        def mm(te, sl=sl, psum=psum):
            r = None
            for c in range(NCH):
                te.matmul(psum[0:16, :], ctx[f"wg{c}"],
                          ctx[f"xnT{c}"][:, 512*sl:512*(sl+1)],
                          start=(c == 0), stop=False)
                r = te.matmul(psum[0:16, :], ctx[f"wglo{c}"],
                              ctx[f"xnT{c}"][:, 512*sl:512*(sl+1)],
                              start=False, stop=(c == NCH - 1))
            return r
        v_mm = step(p, "tensor", "pe", mm)
        p.wait("vector", "pe", v_mm)
        f1_tok[bi] = step(p, "vector", "ve", lambda v, sl=sl, psum=psum: v.tensor_copy(
            ctx["lg"][:, 512*sl:512*(sl+1)], psum[0:16, :]))
    p.wait("sync", "pe", p.cur("pe"))
    for c in range(NCH):
        dma_t(p, "sync", ctx[f"xnT{c}"][:, 0:NTOK], xnlo_dram[:, 128*c:128*(c+1)], "dx")
    p.wait_all("tensor", "dx")
    for sl in range(NTOK // 512):
        bi = nb[0] % 2; psum = banks[bi]; nb[0] += 1
        p.wait("tensor", "ve", f1_tok[bi])

        def mm(te, sl=sl, psum=psum):
            r = None
            for c in range(NCH):
                r = te.matmul(psum[0:16, :], ctx[f"wg{c}"],
                              ctx[f"xnT{c}"][:, 512*sl:512*(sl+1)],
                              start=(c == 0), stop=(c == NCH - 1))
            return r
        v_mm = step(p, "tensor", "pe", mm)
        p.wait("vector", "pe", v_mm)
        f1_tok[bi] = step(p, "vector", "ve", lambda v, sl=sl, psum=psum: v.tensor_add(
            ctx["lg"][:, 512*sl:512*(sl+1)], ctx["lg"][:, 512*sl:512*(sl+1)],
            psum[0:16, :]))
    p.wait("sync", "pe", p.cur("pe"))
    v_lg = p.cur("ve")
    p.wait("scalar", "ve", v_lg)
    v_eg = step(p, "scalar", "se", lambda s: s.activation(ctx["egf"], ctx["lg"], AF.Exp))
    p.wait("tensor", "se", v_eg)
    for sl in range(NTOK // 512):
        bi = nb[0] % 2; psum = banks[bi]; nb[0] += 1
        p.wait("tensor", "ve", f1_tok[bi])

        def mm(te, sl=sl, psum=psum):
            return te.matmul(psum[0:1, :], ctx["onesf"][:, :],
                             ctx["egf"][:, 512*sl:512*(sl+1)], start=True, stop=True)
        v_mm = step(p, "tensor", "pe", mm)
        p.wait("vector", "pe", v_mm)
        f1_tok[bi] = step(p, "vector", "ve", lambda v, sl=sl, psum=psum: v.tensor_copy(
            ctx["zrow"][:, 512*sl:512*(sl+1)], psum[0:1, :]))
    v_z = p.cur("ve")
    p.wait("scalar", "ve", v_z)
    v_lz = step(p, "scalar", "se", lambda s: s.activation(
        ctx["zrow"], ctx["zrow"], AF.Ln))
    p.wait("sync", "se", v_lz)
    dma(p, "sync", lnz_dram.ap(), ctx["zrow"], "dx")
    p.wait_all("sync", "dx")
    v_zb = dma(p, "sync", ctx["lnzbc"], lnz_dram.ap().partition_broadcast(16), "dx")
    p.waitt("vector", v_zb)
    v_M = step(p, "vector", "ve", lambda v: v.tensor_sub(ctx["M"], ctx["lg"],
                                                         ctx["lnzbc"]))
    p.wait("gpsimd", "ve", v_M)
    tok_jt = dma(p, "gpsimd", ctx["jtile"], cd["jtile"].ap(), "dw")
    p.waitt("vector", tok_jt)

    def red(v):
        ap = ctx["M"].rearrange("l (b j q) -> l b q j", b=NBLK, j=16)
        return v.tensor_reduce(ctx["mx"].ap().rearrange("l (b q) -> l b q", b=NBLK),
                               ap, AX.X, A.max)
    step(p, "vector", "ve", red)

    def eqf(v):
        apM = ctx["M"].rearrange("l (b j q) -> l b j q", b=NBLK, j=16)
        apmx = ctx["mx"].ap().rearrange("l (b q) -> l b () q", b=NBLK)
        apmx = apmx.broadcast_to([16, NBLK, 16, NPOS // NBLK])
        return v.tensor_tensor(
            ctx["eq"][0:16, :].rearrange("l (b j q) -> l b j q", b=NBLK, j=16),
            apM, apmx, A.is_ge)
    step(p, "vector", "ve", eqf)

    def candf(v):
        v.tensor_scalar(ctx["cand"][0:16, :], ctx["jtile"], JBIG, None, A.subtract)
        v.tensor_mul(ctx["cand"][0:16, :], ctx["cand"][0:16, :], ctx["eq"][0:16, :])
        return v.tensor_scalar(ctx["cand"][0:16, :], ctx["cand"][0:16, :],
                               JBIG, None, A.add)
    step(p, "vector", "ve", candf)

    def redmin(v):
        ap = ctx["cand"][0:16, :].rearrange("l (b j q) -> l b q j", b=NBLK, j=16)
        return v.tensor_reduce(ctx["idxf"].ap().rearrange("l (b q) -> l b q", b=NBLK),
                               ap, AX.X, A.min)
    step(p, "vector", "ve", redmin)
    step(p, "vector", "ve", lambda v: v.tensor_copy(ctx["eq"][0:16, 0:NPOS],
                                                    ctx["idxf"][:, :]))
    p.wait("sync", "ve", p.cur("ve"))
    dma(p, "sync", idx_sc.ap(), ctx["eq"][0:16, 0:NPOS], "dx")
    p.wait_all("sync", "dx")
    for b in range(NBLK):
        dma_t(p, "sync", ctx[f"idxT{b}"][:, 0:16], idx_sc[:, 128*b:128*(b+1)], "dx")
    p.wait_all("vector", "dx")
    for b in range(NBLK):
        def rtf(v, b=b):
            v.tensor_scalar(ctx[f"rt{b}"][:, :], ctx[f"idxT{b}"][:, 0:16],
                            128.0, None, A.mult)
            v.tensor_scalar_add(ctx[f"rt{b}"][:, :], ctx[f"rt{b}"][:, :],
                                ctx["base"][:, b:b+1])
            return v.tensor_copy(ctx[f"rtu{b}"][:, :], ctx[f"rt{b}"][:, :])
        step(p, "vector", "ve", rtf)
    v_rt = p.cur("ve")
    st = {}

    def emit_f1_tail(feed):
        p.wait("gpsimd", "ve", v_rt)
        t_id = dma(p, "gpsimd", ctx["ident128"], cd["ident128"].ap(), "dw")
        p.waitt("tensor", t_id)
        gbufs = ([ctx["AR3"].ap()[:, 12288 + 1024*i:12288 + 1024*(i+1)]
                  for i in range(4)] +
                 [ctx["AR3"].ap()[:, 4096 + 1024*i:4096 + 1024*(i+1)]
                  for i in range(4)])
        psb_slots = ([(ctx["ps_big"].ap().bitcast(BF16), 1024 * i)
                      for i in range(4)] +
                     [(ctx["ps_wo"].ap().bitcast(BF16), 1024 * i)
                      for i in range(2)])
        tp_toks = {}
        ev_toks = {}
        gi = 0
        for l in range(16):
            for b in range(NBLK):
                tok0 = l * NPOS + b * 128
                buf = gbufs[gi % 8]
                if gi >= 8:
                    p.wait("gpsimd", "pe", tp_toks[gi - 8])
                tg = dma_ind(p, "gpsimd", buf, xnF_dram.ap(),
                             ctx[f"rtu{b}"][:, l:l+1], f"dg{gi % 2}")
                p.waitt("tensor", tg)
                sb_, soff = psb_slots[gi % 6]
                if gi >= 6:
                    p.wait("tensor", "ve", ev_toks[gi - 6])

                def tps(te, buf=buf, sb_=sb_, soff=soff):
                    r = None
                    for f in range(8):
                        r = te.transpose(sb_[:, soff + 128*f:soff + 128*(f+1)],
                                         buf[:, 128*f:128*(f+1)],
                                         ctx["ident128"])
                    return r
                tp_toks[gi] = step(p, "tensor", "pe", tps)
                p.wait("vector", "pe", tp_toks[gi])

                def ev(v, tok0=tok0, sb_=sb_, soff=soff):
                    r = None
                    for c in range(NCH):
                        r = v.tensor_copy(ctx[f"xnT{c}"][:, tok0:tok0+128],
                                          sb_[:, soff + 128*c:soff + 128*(c+1)])
                    return r
                ev_toks[gi] = step(p, "vector", "ve", ev)
                if gi % 2 == 1:
                    feed()
                gi += 1
        while feed():
            pass
        for c in range(NCH):
            dma(p, "gpsimd", ctx[f"wkv{c}"][0:16, :],
                wd["Wkv_r"][128*c:128*c+16, :], "dw")
        p.wait_all("tensor", "dw")
        p.wait("tensor", "ve", p.cur("ve"))
        for l in range(16):
            bi = nb[0] % 2; psum = banks[bi]; nb[0] += 1
            p.wait("tensor", "ve", f1_tok[bi])

            def mmq(te, l=l, psum=psum):
                r = None
                for c in range(NCH):
                    r = te.matmul(psum[0:64, 0:NPOS],
                                  ctx[f"wq{c}"][:, 64*l:64*(l+1)],
                                  ctx[f"xnT{c}"][:, NPOS*l:NPOS*(l+1)],
                                  start=(c == 0), stop=(c == NCH - 1))
                return r
            v_mm = step(p, "tensor", "pe", mmq)
            p.wait("vector", "pe", v_mm)
            f1_tok[bi] = step(p, "vector", "ve",
                              lambda v, l=l, psum=psum: v.tensor_scalar_add(
                ctx["qall"][0:64, NPOS*l:NPOS*(l+1)], psum[0:64, 0:NPOS],
                ctx["bq_t"][:, l:l+1]))
            bi = nb[0] % 2; psum = banks[bi]; nb[0] += 1
            p.wait("tensor", "ve", f1_tok[bi])

            def mmkv(te, l=l, psum=psum):
                r = None
                for c in range(NCH):
                    r = te.matmul(psum[:, 0:NPOS],
                                  ctx[f"wkv{c}"][:, 128*l:128*(l+1)],
                                  ctx[f"xnT{c}"][:, NPOS*l:NPOS*(l+1)],
                                  start=(c == 0), stop=(c == NCH - 1))
                return r
            v_mm = step(p, "tensor", "pe", mmkv)
            p.wait("vector", "pe", v_mm)
            f1_tok[bi] = step(p, "vector", "ve",
                              lambda v, l=l, psum=psum: v.tensor_scalar_add(
                ctx["kvall"][:, NPOS*l:NPOS*(l+1)], psum[:, 0:NPOS],
                ctx["bkv_t"][:, l:l+1]))
        step(p, "vector", "ve", lambda v: v.tensor_mul(
            ctx["tt"][0:64, :], ctx["qall"][0:64, :], ctx["kvall"][0:64, :]))
        p.wait("tensor", "ve", p.cur("ve"))
        for sl in range(8):
            bi = nb[0] % 2; psum = banks[bi]; nb[0] += 1
            p.wait("tensor", "ve", f1_tok[bi])

            def mm(te, sl=sl, psum=psum):
                return te.matmul(psum[0:1, :], ctx["onesc"][0:64, :],
                                 ctx["tt"][0:64, 512*sl:512*(sl+1)],
                                 start=True, stop=True)
            v_mm = step(p, "tensor", "pe", mm)
            p.wait("vector", "pe", v_mm)
            f1_tok[bi] = step(p, "vector", "ve",
                              lambda v, sl=sl, psum=psum: v.tensor_copy(
                ctx["zrow"][:, 512*sl:512*(sl+1)], psum[0:1, :]))
        p.wait("sync", "ve", p.cur("ve"))
        dma(p, "sync", ag1_in.ap().rearrange("a q -> () (a q)")[:, 0:4096],
            ctx["zrow"], "dx")
        dma(p, "sync", ag1_in[16:32, :], ctx["idxf"][:, :], "dx")
        p.wait_all("gpsimd", "dx")

        h_cc, v_cc1 = p.inc("cc", 1)
        st["v_cc1"] = v_cc1
        p.emit("gpsimd", lambda e, h=h_cc: e.collective_compute(
            "AllGather", A.bypass, replica_groups=[list(range(NC))],
            ins=[ag1_in.ap().opt()], outs=[ag1_out.ap().opt()]).then_inc(h, 1))

    def emit_p1b():
        p.wait("gpsimd", "cc", st["v_cc1"])
        v_w16l = dma(p, "gpsimd", ctx["w16"][:, :], ag1_in[0:16, :], "dw")
        v_g1 = dma(p, "gpsimd", ctx["attG"].rearrange("l (s q) -> l s q", s=8),
                   ag1_out.ap().rearrange("(s r) q -> s r q", r=32)[:, 0:16, :]
                   .rearrange("s l q -> l s q"), "dw")
        p.waitt("vector", v_g1)
        step(p, "vector", "ve", lambda v: v.reduce_max(ctx["zf"][0:16, :],
                                                       ctx["attG"], AX.X))
        v_m = step(p, "vector", "ve", lambda v: v.tensor_scalar_mul(
            ctx["zf"][0:16, :], ctx["zf"][0:16, :], -0.125))
        p.wait("scalar", "ve", v_m)
        v_e = step(p, "scalar", "se", lambda s: s.activation(
            ctx["attG"], ctx["attG"], AF.Exp, bias=ctx["zf"][0:16, :], scale=0.125))
        p.wait("vector", "se", v_e)
        step(p, "vector", "ve", lambda v: v.reduce_sum(ctx["tvar"][0:16, :],
                                                       ctx["attG"], AX.X))
        v_rz = step(p, "vector", "ve", lambda v: v.reciprocal(
            ctx["tln"][0:16, :], ctx["tvar"][0:16, :]))
        p.wait("scalar", "ve", v_rz)
        p.waitt("scalar", v_w16l)
        v_e2 = step(p, "scalar", "se", lambda s: s.activation(
            ctx["w16"][:, :], ctx["w16"][:, :], AF.Exp, bias=ctx["zf"][0:16, :],
            scale=0.125))
        p.wait("vector", "se", v_e2)

        def normw(v):
            v.tensor_scalar_mul(ctx["w16"][:, :], ctx["w16"][:, :],
                                ctx["tln"][0:16, :])
            return v.tensor_copy(ctx["w16b"][:, :], ctx["w16"][:, :])
        v_w16 = step(p, "vector", "ve", normw)
        p.wait("tensor", "ve", v_w16)
        for l in range(16):
            psum = banks[nb[0] % 2]; nb[0] += 1
            p.wait("tensor", "ve", p.cur("ve"))

            def mmb(te, l=l, psum=psum):
                return te.matmul(psum[0:64, 0:NPOS],
                                 ctx["onehot16"][:, 64*l:64*(l+1)],
                                 ctx["w16b"][:, :], start=True, stop=True)
            v_mm = step(p, "tensor", "pe", mmb)
            p.wait("vector", "pe", v_mm)
            step(p, "vector", "ve", lambda v, l=l, psum=psum: v.tensor_mul(
                ctx["tt"][0:64, NPOS*l:NPOS*(l+1)],
                ctx["kvall"][64:128, NPOS*l:NPOS*(l+1)], psum[0:64, 0:NPOS]))
        v_av = p.cur("ve")
        p.wait("sync", "ve", v_av)
        dma(p, "sync", ag2_in.ap(), ctx["tt"][0:64, :], "dx")
        p.wait_all("gpsimd", "dx")
        h_cc, v_cc2 = p.inc("cc", 1)
        st["v_cc2"] = v_cc2
        p.emit("gpsimd", lambda e, h=h_cc: e.collective_compute(
            "AllGather", A.bypass, replica_groups=[list(range(NC))],
            ins=[ag2_in.ap().opt()], outs=[ag2_out.ap().opt()]).then_inc(h, 1))

    def make_ln_feeder(bt, alt=False):
        stat_m = ctx[f"stat_m{bt}"]
        stat_r = ctx[f"stat_r{bt}"]
        if alt:
            xas = [ctx["AR3"].ap().bitcast(F32)[:, 2048:3072],
                   ctx["AR3"].ap().bitcast(F32)[:, 3072:4096]]
            xns = [ctx["AR3"].ap()[:, 8192:9216], ctx["AR3"].ap()[:, 9216:10240]]
            xqs = [ctx["AR3"].ap()[:, 10240:11264], ctx["AR3"].ap()[:, 11264:12288]]
        else:
            xas = [ctx["xa0"], ctx["xa1"]]
            xns = [ctx["xn0"], ctx["xn1"]]
            xqs = [None, None]
        state = {"t": 0}
        lnst = {}
        st["lnst"] = lnst

        def feed():
            t = state["t"]
            if t >= 16:
                return False
            if t == 0:
                if not alt:
                    p.wait("scalar", "pe", p.cur("pe"))
                p.wait("scalar", "ve", p.cur("ve"))
                p.wait("sync", "ve", p.cur("ve"))
                p.wait_all("sync", "dx")
            xa = xas[t % 2]
            xn = xns[t % 2]
            if alt:
                eng, fam = "scalar", "ds"
            else:
                eng, fam = ("scalar", "ds") if t % 2 == 0 else ("sync", "dx")
            if t >= 2:
                p.waitt(eng, lnst[t - 2])
            v_load = dma(p, eng, xa, x_own[bt, 128*t:128*(t+1), :], fam)
            p.waitt("vector", v_load)
            p.waitt("scalar", v_load)
            v_no = ln_tile(xa, xn, m_ap=stat_m[:, t:t+1], r_ap=stat_r[:, t:t+1],
                           par=t % 2, pfx="l", xq=xqs[t % 2])
            p.wait("sync", "ve", v_no)
            lnst[t] = dma(p, "sync", xn_dram[128*t:128*(t+1), :], xn, "dx")
            state["t"] = t + 1
            return True
        return feed

    def emit_a_mid(bt, wo_hi_preloaded=False):
        p.wait("sync", "pe", p.cur("pe"))
        for tok in st.get("lnst", {}).values():
            p.waitt("sync", tok)
        for c in range(NCH):
            dma_t(p, "sync", ctx[f"xnT{c}"][:, 0:S], xn_dram[:, 128*c:128*(c+1)], "dx")
        p.wait_all("tensor", "dx")

        p.wait("gpsimd", "pe", p.cur("pe"))
        p.wait("gpsimd", "ve", p.cur("ve"))
        p.wait("gpsimd", "se", p.cur("se"))
        p.wait_all("gpsimd", "dx")
        for h in range(8):
            dma(p, "gpsimd", ctx[f"w_o{h}"][0:64, :], wd["W_O"][64*h:64*(h+1), :], "dw")
        p.wait_all("vector", "dw")

        def wo_zero(v):
            r = None
            for h in range(8):
                r = v.memset(ctx[f"w_o{h}"][64:128, :], 0.0)
            return r
        v_woz = step(p, "vector", "ve", wo_zero)

        def au_zero(v):
            r = None
            for hh in range(8):
                r = v.memset(ctx[f"attnU128_{hh}"][64:128, :], 0.0)
            return r
        v_auz = step(p, "vector", "ve", au_zero)

        p.wait("gpsimd", "ve", v_auz)
        krp_toks = [dma(p, "gpsimd", ctx[f"w_krp{c}"][:, :],
                        wd["W_KRP"][128*c:128*(c+1), :], "dw")
                    for c in range(NCH)]

        nbv = [0]
        bank_tok = {0: ("ve", p.cur("ve")), 1: ("ve", p.cur("ve"))}
        st["nbv"] = nbv; st["bank_tok"] = bank_tok

        def proj(dst_fn, w_fn, nchunks, rows=128, eng="vector"):
            bi = nbv[0] % 2; psum = banks[bi]; nbv[0] += 1
            sem, val = bank_tok[bi]
            p.wait("tensor", sem, val)

            def mms(te, psum=psum):
                r = None
                for i in range(nchunks):
                    wap, xap = w_fn(i)
                    r = te.matmul(psum[0:rows, :], wap, xap,
                                  start=(i == 0), stop=(i == nchunks - 1))
                return r
            v_mm = step(p, "tensor", "pe", mms)
            p.wait(eng, "pe", v_mm)
            sname = "ve" if eng == "vector" else "se"
            tok = step(p, eng, sname, lambda e, psum=psum: dst_fn(e, psum))
            bank_tok[bi] = (sname, tok)
        st["proj"] = proj

        for name, wpfx in (("cqT", "w_dq"), ("ckvT", "w_dkv")):
            for h2 in range(2):
                for qc in range(NQT):
                    def ev(s, psum, name=name, h2=h2, qc=qc):
                        return s.activation(ctx[name + str(h2)][:, SQ*qc:SQ*(qc+1)],
                                            psum[:, :], AF.Copy)
                    proj(ev, lambda i, h2=h2, qc=qc, wpfx=wpfx: (
                        ctx[f"{wpfx}{i}"][:, 128*h2:128*(h2+1)],
                        ctx[f"xnT{i}"][:, SQ*qc:SQ*(qc+1)]), NCH, eng="scalar")
        for t in krp_toks:
            p.waitt("tensor", t)
        for qc in range(NQT):
            def ev(v, psum, qc=qc):
                return v.tensor_copy(ctx["krSt"][:, SQ*qc:SQ*(qc+1)], psum[64:96, :])
            proj(ev, lambda i, qc=qc: (ctx[f"w_krp{i}"][:, :],
                                       ctx[f"xnT{i}"][:, SQ*qc:SQ*(qc+1)]), NCH)
        v_kr_done = p.cur("ve")
        p.wait_all("gpsimd", "dx")
        p.wait("gpsimd", "ve", v_kr_done)
        p.wait("gpsimd", "se", p.cur("se"))
        p.wait("gpsimd", "pe", p.cur("pe"))
        t_tc = dma(p, "gpsimd", ctx["trig_c"][:, :], cd["trig_c"].ap(), "dw")
        t_ts = dma(p, "gpsimd", ctx["trig_s"][:, :], cd["trig_s"].ap(), "dw")
        st["trig_toks"] = (t_tc, t_ts)
        t_k1 = dma(p, "gpsimd", ctx["krPr"][0:16, :], ctx["krSt"][16:32, :], "dw")
        t_k2 = dma(p, "gpsimd", ctx["krPr"][16:32, :], ctx["krSt"][0:16, :], "dw")
        for t in (t_tc, t_ts, t_k1, t_k2):
            p.waitt("vector", t)

        def rope_kr(v):
            v.tensor_mul(ctx["krSt"][:, :], ctx["krSt"][:, :],
                         ctx["trig_c"][64:96, :])
            v.tensor_mul(ctx["krPr"][:, :], ctx["krPr"][:, :],
                         ctx["trig_s"][64:96, :])
            return v.tensor_add(ctx["krSt"][:, :], ctx["krSt"][:, :],
                                ctx["krPr"][:, :])
        step(p, "vector", "ve", rope_kr)

        hi_end = 10 if wo_hi_preloaded else 16
        st["wo_hi_toks"] = [dma(p, "gpsimd", ctx[f"w_o{h}"][0:64, :],
                                wd["W_O"][64*h:64*(h+1), :], "dw")
                            for h in range(8, hi_end)]

    def emit_a_back(bt, tail_hook=None):
        proj = st["proj"]
        t_tc, t_ts = st["trig_toks"]
        bank_tok = st["bank_tok"]
        bank_tok[0] = bank_tok[1] = ("ve", p.cur("ve"))
        p.wait_all("vector", "dx")
        p.wait_all("scalar", "dx")

        def kr_fill(v):
            r = None
            for hh in range(NCH):
                if hh > 0:
                    v.tensor_copy(ctx[f"khead{hh}"][64:96, :], ctx["krSt"][:, :])
                r = v.memset(ctx[f"khead{hh}"][96:128, :], 0.0)
            return r
        step(p, "vector", "ve", kr_fill)

        for g in range(2):
            se_g0 = p.cur("se")
            nb_prev_rope = [p.cur("ve")] * 4
            if g == 1:
                for t in st.pop("wo_hi_toks", []) + st.pop("wo_pre_toks", []):
                    p.waitt("vector", t)

                def wo_zero_hi(v):
                    r = None
                    for h in range(8, 16):
                        r = v.memset(ctx[f"w_o{h}"][64:128, :], 0.0)
                    return r
                step(p, "vector", "ve", wo_zero_hi)
                p.wait_all("gpsimd", "dx")
                p.wait("gpsimd", "pe", p.cur("pe"))
                p.wait("gpsimd", "se", p.cur("se"))
                p.wait("gpsimd", "ve", p.cur("ve"))
                t_tc = dma(p, "gpsimd", ctx["trig_c"][:, :], cd["trig_c"].ap(), "dw")
                t_ts = dma(p, "gpsimd", ctx["trig_s"][:, :], cd["trig_s"].ap(), "dw")
            for hh in range(NCH):
                h = 8 * g + hh
                for qc in range(NQT):
                    def ev(s, psum, hh=hh, qc=qc):
                        return s.activation(
                            ctx[f"khead{hh}"][0:64, SQ*qc:SQ*(qc+1)],
                            psum[0:64, :], AF.Copy)
                    proj(ev, lambda l, h=h, qc=qc: (
                        ctx[f"w_uk{l}"][:, 64*h:64*h+128],
                        ctx[f"ckvT{l}"][:, SQ*qc:SQ*(qc+1)]), 2, eng="scalar")
            for hh in range(NCH):
                h = 8 * g + hh
                for qc in range(NQT):
                    def ev(s, psum, hh=hh, qc=qc):
                        return s.activation(
                            ctx[f"qhead{hh}"][0:96, SQ*qc:SQ*(qc+1)],
                            psum[0:96, :], AF.Copy)
                    proj(ev, lambda l, h=h, qc=qc: (
                        ctx[f"w_qp{l}"][:, 96*h:96*h+128],
                        ctx[f"cqT{l}"][:, SQ*qc:SQ*(qc+1)]), 2, eng="scalar")
                v_ev4 = p.cur("se")
                pq = ctx[f"pQ{hh % 4}"]
                p.wait("gpsimd", "se", v_ev4)
                p.wait("gpsimd", "ve", nb_prev_rope[hh % 4])
                p.wait("gpsimd", "se", se_g0)
                t1 = dma(p, "gpsimd", pq[0:16, :],
                         ctx[f"qhead{hh}"][80:96, :], "dw")
                t2 = dma(p, "gpsimd", pq[16:32, :],
                         ctx[f"qhead{hh}"][64:80, :], "dw")
                for t in (t_tc, t_ts, t1, t2):
                    p.waitt("vector", t)
                p.wait("vector", "se", v_ev4)

                def rope_q(v, hh=hh, pq=pq):
                    v.tensor_mul(ctx[f"qhead{hh}"][64:96, :],
                                 ctx[f"qhead{hh}"][64:96, :],
                                 ctx["trig_c"][64:96, :])
                    v.tensor_mul(pq[:, :], pq[:, :], ctx["trig_s"][64:96, :])
                    return v.tensor_add(ctx[f"qhead{hh}"][64:96, :],
                                        ctx[f"qhead{hh}"][64:96, :], pq[:, :])
                nb_prev_rope[hh % 4] = step(p, "vector", "ve", rope_q)
            p.wait("scalar", "ve", p.cur("ve"))
            for kt in range(NKT):
                def ev(s, psum, kt=kt):
                    dst = ctx[f"vh{kt}"].rearrange("k (h e) -> k h e", h=8)
                    src = psum.ap().rearrange("k (h e) -> k h e", h=8)
                    return s.activation(dst[:, :, 0:64], src[:, :, :], AF.Copy)
                proj(ev, lambda l, kt=kt, g=g: (
                    ctx[f"ckvT{l}"][:, 128*kt:128*(kt+1)],
                    ctx[f"w_uv{l}"][:, 512*g:512*(g+1)]), 2, eng="scalar")

            def vh_ones(v):
                r = None
                for kt in range(NKT):
                    dst = ctx[f"vh{kt}"].rearrange("k (h e) -> k h e", h=8)
                    r = v.memset(dst[:, :, 64:65], 1.0)
                return r
            step(p, "vector", "ve", vh_ones)
            v_rope_done = p.cur("ve")
            p.wait("tensor", "ve", v_rope_done)
            p.wait("tensor", "se", p.cur("se"))

            slots = [(ctx["ps_big"], 0), (ctx["ps_big"], 1024), (ctx["ps_wo"], 0)]
            psl = [(ctx["P0"], 0), (ctx["P0"], 1024), (ctx["P1"], 0), (ctx["P1"], 1024)]
            items = []
            for hh in range(8):
                for qt in range(NQT):
                    for k2 in range(2 * (qt + 1)):
                        items.append((hh, qt, k2))
            NI = len(items)
            v_sc = [0] * NI; v_ex = [0] * NI; v_cm = [0] * NI; v_pv = [0] * NI
            v_eu_prev = [p.cur("ve")]
            for i in range(NI + 1):
                if i < NI:
                    hh, qt, k2 = items[i]
                    kts = [2 * k2, 2 * k2 + 1]
                    sb, soff = slots[i % 3]
                    if i >= 3:
                        p.wait("tensor", "se", v_ex[i - 3])

                    def smms(te, kts=kts, hh=hh, qt=qt, sb=sb, soff=soff):
                        r = None
                        for t, kt in enumerate(kts):
                            r = te.matmul(sb[:, soff + SQ*t:soff + SQ*(t+1)],
                                          ctx[f"khead{hh}"][:, 128*kt:128*(kt+1)],
                                          ctx[f"qhead{hh}"][:, SQ*qt:SQ*(qt+1)],
                                          start=True, stop=True)
                        return r
                    v_sc[i] = step(p, "tensor", "pe", smms)
                    pb, poff = psl[i % 4]
                    p.wait("scalar", "pe", v_sc[i])
                    if i >= 4:
                        p.wait("scalar", "pe", v_pv[i - 4])

                    def expf(s, sb=sb, soff=soff, pb=pb, poff=poff):
                        return s.activation(pb[:, poff:poff + 1024],
                                            sb[:, soff:soff + 1024], AF.Exp,
                                            scale=INV_SQRT_DQK)
                    v_ex[i] = step(p, "scalar", "se", expf)
                    p.wait("vector", "se", v_ex[i])

                    def clipmask(v, pb=pb, poff=poff, kts=kts, qt=qt):
                        r = v.tensor_scalar(pb[:, poff:poff + 1024],
                                            pb[:, poff:poff + 1024],
                                            CLIP_HI, CLIP_LO, A.min, A.max)
                        for t, kt in enumerate(kts):
                            d = kt - 4 * qt
                            if 0 <= d < 4:
                                r = v.tensor_mul(pb[:, poff + SQ*t:poff + SQ*(t+1)],
                                                 pb[:, poff + SQ*t:poff + SQ*(t+1)],
                                                 ctx[f"dmask{d}"][:, :])
                        return r
                    v_cm[i] = step(p, "vector", "ve", clipmask)
                if i >= 1:
                    hh1, qt1, k21 = items[i - 1]
                    kts1 = [2 * k21, 2 * k21 + 1]
                    pb1, poff1 = psl[(i - 1) % 4]
                    p.wait("tensor", "ve", v_cm[i - 1])
                    if k21 == 0:
                        p.wait("tensor", "ve", v_eu_prev[0])

                    def pvmms(te, kts1=kts1, k21=k21, qt1=qt1, hh1=hh1,
                              pb1=pb1, poff1=poff1):
                        r = None
                        for t, kt in enumerate(kts1):
                            vsl = ctx["vh_t"].ap()[:, 520*kt + 65*hh1:
                                                   520*kt + 65*hh1 + 128]
                            r = te.matmul(ctx["ps_pv"][:, :], vsl,
                                          pb1[:, poff1 + SQ*t:poff1 + SQ*(t+1)],
                                          start=(k21 == 0 and t == 0),
                                          stop=(k21 == 2 * (qt1 + 1) - 1 and t == 1))
                        return r
                    v_pv[i - 1] = step(p, "tensor", "pe", pvmms)
                    if k21 == 2 * (qt1 + 1) - 1:
                        p.wait("vector", "pe", v_pv[i - 1])
                        v_eu = step(p, "vector", "ve",
                                    lambda v, hh1=hh1, qt1=qt1: v.tensor_copy(
                                        ctx[f"attnU{hh1}"][:, SQ*qt1:SQ*(qt1+1)],
                                        ctx["ps_pv"][0:65, :]))
                        v_eu_prev[0] = v_eu
                        if qt1 == NQT - 1:
                            h = 8 * g + hh1
                            p.wait("gpsimd", "ve", v_eu)
                            dma(p, "gpsimd", rsum_dram[h:h+1, :],
                                ctx[f"attnU{hh1}"][64:65, :], "dw")

            p.wait("vector", "pe", p.cur("pe"))
            v_ms = step(p, "vector", "ve", lambda v: v.memset(ctx["rsumS"], 1.0))
            p.wait("sync", "ve", v_ms)
            p.wait_all("sync", "dw")
            t_rs = dma(p, "sync", ctx["rsumS8"], rsum_dram[8*g:8*g+8, :], "dx")
            p.waitt("scalar", t_rs)
            p.wait("scalar", "ve", p.cur("ve"))

            def recf(s):
                lnf = ctx["ot0"][0:16, :]
                with nc.allow_low_precision("bf16 softmax normalizer"):
                    s.activation(lnf, ctx["rsumS"][:, 0:1024], AF.Ln)
                    s.activation(ctx["rec16n"][:, 0:1024], lnf, AF.Exp,
                                 scale=-1.0)
                    s.activation(lnf, ctx["rsumS"][:, 1024:2048], AF.Ln)
                    return s.activation(ctx["rec16n"][:, 1024:2048], lnf,
                                        AF.Exp, scale=-1.0)
            v_rec = step(p, "scalar", "se", recf)
            p.wait("sync", "se", v_rec)
            dma(p, "sync", rec_dram[8*g:8*g+8, :], ctx["rec8n"], "dx")
            p.wait_all("sync", "dx")
            nb_bufs = [ctx["AR3"].ap()[0:64, 0:2048], ctx["AR3"].ap()[0:64, 2048:4096]]
            nb_tok = [p.cur("ve"), p.cur("ve")]
            v_nm_last = p.cur("ve")
            for hh in range(8):
                h = 8 * g + hh
                buf = nb_bufs[hh % 2]
                p.wait("sync", "ve", nb_tok[hh % 2])
                t_b = dma(p, "sync", buf,
                          rec_dram[h:h+1, :].partition_broadcast(64), "dx")
                p.waitt("vector", t_b)
                v_nm = step(p, "vector", "ve", lambda v, hh=hh, buf=buf: v.tensor_mul(
                    ctx[f"attnU{hh}"][0:64, :], ctx[f"attnU{hh}"][0:64, :], buf))
                nb_tok[hh % 2] = v_nm
                v_nm_last = v_nm

            p.wait("tensor", "ve", v_nm_last)
            p.wait("tensor", "se", v_ex[NI - 1])
            accs = [ctx["ps_wo"].ap()[:, 0:1024], ctx["ps_big"].ap()[:, 0:1024]]
            wo_ev = [p.cur("ve"), p.cur("ve")]
            st_tok = {}
            for q128 in range(16):
                acc = accs[q128 % 2]
                p.wait("tensor", "ve", wo_ev[q128 % 2])

                def womms(te, q128=q128, g=g, acc=acc):
                    r = None
                    for half in range(2):
                        for hh in range(8):
                            h = 8 * g + hh
                            r = te.matmul(acc[:, 512*half:512*(half+1)],
                                          ctx[f"attnU128_{hh}"][:,
                                                                128*q128:128*(q128+1)],
                                          ctx[f"w_o{h}"][:, 512*half:512*(half+1)],
                                          start=(hh == 0), stop=(hh == 7))
                    return r
                v_wo = step(p, "tensor", "pe", womms)
                ot = ctx["ot0"] if q128 % 2 == 0 else ctx["ot1"]
                if g == 0:
                    p.wait("vector", "pe", v_wo)
                    if q128 >= 2:
                        p.waitt("vector", st_tok[q128 - 2])
                    v_ev = step(p, "vector", "ve",
                                lambda v, ot=ot, acc=acc: v.tensor_copy(ot, acc))
                else:
                    if q128 >= 2:
                        p.waitt("sync", st_tok[q128 - 2])
                    v_ld = dma(p, "sync", ot, attn_dram[128*q128:128*(q128+1), :], "dx")
                    p.waitt("vector", v_ld)
                    p.wait("vector", "pe", v_wo)
                    v_ev = step(p, "vector", "ve", lambda v, ot=ot, acc=acc: v.tensor_add(
                        ot, ot, acc))
                wo_ev[q128 % 2] = v_ev
                p.wait("sync", "ve", v_ev)
                st_tok[q128] = dma(p, "sync", attn_dram[128*q128:128*(q128+1), :], ot, "dx")
                if g == 1 and tail_hook is not None:
                    tail_hook()
            if g == 1 and tail_hook is not None:
                while tail_hook():
                    pass
            p.wait_all("vector", "dx")
            p.wait_all("tensor", "dx")

    def make_e_pro_feeder(bt):
        avls = [ctx["avh"], ctx["AR3"].ap()[0:64, 4096:6144]]
        avhs = [ctx["avh_hi"], ctx["AR3"].ap()[64:128, 4096:6144]]
        eqls = [ctx["eq64"], ctx["AR3"].ap()[0:64, 6144:8192]]
        eqhs = [ctx["eq64_hi"], ctx["AR3"].ap()[64:128, 6144:8192]]
        state = {"k": 0, "toks": {}, "mul": {}}

        def issue_pair(c):
            s_ = c % 2
            if c >= 2:
                p.wait("gpsimd", "ve", state["mul"][c - 2])
                p.wait("scalar", "ve", state["mul"][c - 2])
            v_lo = dma(p, "gpsimd", avls[s_].rearrange("d (s q) -> d s q", s=8),
                       ag2_out.ap().rearrange("(s d) (ll q) -> s d ll q", d=64, ll=16)
                       [:, :, 2*c, :].rearrange("s d q -> d s q"), "dw")
            v_hi = dma(p, "scalar", avhs[s_].rearrange("d (s q) -> d s q", s=8),
                       ag2_out.ap().rearrange("(s d) (ll q) -> s d ll q", d=64, ll=16)
                       [:, :, 2*c+1, :].rearrange("s d q -> d s q"), "ds")
            v_el = dma(p, "gpsimd", eqls[s_],
                       eq_dram[2*c:2*c+1, :].partition_broadcast(64), "dw")
            v_eh = dma(p, "scalar", eqhs[s_],
                       eq_dram[2*c+1:2*c+2, :].partition_broadcast(64), "ds")
            state["toks"][c] = (v_lo, v_hi, v_el, v_eh)

        def feed():
            k = state["k"]
            if k > 9:
                return False
            if k == 0:
                p.wait("gpsimd", "cc", st["v_cc2"])
                p.wait_all("gpsimd", "dx")
                p.wait("gpsimd", "ve", p.cur("ve"))
                p.wait("gpsimd", "se", p.cur("se"))
                p.wait("gpsimd", "pe", p.cur("pe"))
                p.wait("scalar", "pe", p.cur("pe"))
                v_ig = dma(p, "gpsimd",
                           ctx["idxG"].rearrange("l (s q) -> l s q", s=8),
                           ag1_out.ap().rearrange("(s r) q -> s r q", r=32)
                           [:, 16:32, :].rearrange("s l q -> l s q"), "dw")
                p.waitt("vector", v_ig)
                v_eqb = step(p, "vector", "ve",
                             lambda v, bt=bt: v.tensor_single_scalar(
                    ctx["eq64"][0:16, :], ctx["idxG"], ctx["jcol"][:, bt:bt+1],
                    A.is_equal))
                state["v_eqb"] = v_eqb
                p.wait("tensor", "ve", v_eqb)
                p.wait("tensor", "se", p.cur("se"))
                for sl in range(4):
                    psum = banks[nb[0] % 2]; nb[0] += 1
                    p.wait("tensor", "ve", p.cur("ve"))

                    def mm(te, sl=sl, psum=psum):
                        return te.matmul(psum[0:1, :], ctx["onesc"][0:16, 0:1],
                                         ctx["eq64"][0:16, 512*sl:512*(sl+1)],
                                         start=True, stop=True)
                    v_mm = step(p, "tensor", "pe", mm)
                    p.wait("vector", "pe", v_mm)
                    step(p, "vector", "ve",
                         lambda v, sl=sl, psum=psum: v.tensor_copy(
                             ctx["nsel"][:, 512*sl:512*(sl+1)], psum[0:1, :]))
                v_ns = step(p, "vector", "ve", lambda v: v.tensor_copy(
                    ctx["nselb"], ctx["nsel"]))
                p.wait("scalar", "ve", v_ns)
                dma(p, "scalar", eq_dram.ap(), ctx["eq64"][0:16, :], "ds")
                p.wait_all("gpsimd", "ds")
                p.wait_all("scalar", "ds")
                dma(p, "scalar", ctx["g_rep"], wd["g_rep"].ap(), "ds")
                dma(p, "scalar", ctx["b_rep"], wd["b_rep"].ap(), "ds")
            elif k == 1:
                p.wait("gpsimd", "ve", state["v_eqb"])
                p.wait("scalar", "ve", state["v_eqb"])
                issue_pair(0)
                issue_pair(1)
            else:
                c = k - 2
                s_ = c % 2
                for t in state["toks"][c]:
                    p.waitt("vector", t)

                def avmf(v, c=c, s_=s_):
                    v.tensor_mul(ctx[f"avmP{c}"][0:64, :], avls[s_], eqls[s_])
                    return v.tensor_mul(ctx[f"avmP{c}"][64:128, :], avhs[s_],
                                        eqhs[s_])
                state["mul"][c] = step(p, "vector", "ve", avmf)
                if c + 2 < NCH:
                    issue_pair(c + 2)
            state["k"] = k + 1
            return True
        return feed

    def emit_e(bt, feed=None, preload_wo_hi=False):
        stat_m = ctx[f"stat_m{bt}"]
        stat_r = ctx[f"stat_r{bt}"]
        p.wait("gpsimd", "pe", p.cur("pe"))
        p.wait("gpsimd", "ve", p.cur("ve"))
        p.wait_all("gpsimd", "dx")
        for c in range(NCH):
            dma(p, "gpsimd", ctx[f"wo_p{c}"][:, :],
                wd["Wo_f"][128*c:128*(c+1), :], "dw")
        if preload_wo_hi:
            st["wo_pre_toks"] = [dma(p, "gpsimd", ctx[f"w_o{h}"][0:64, :],
                                     wd["W_O"][64*h:64*(h+1), :], "dw")
                                 for h in range(10, 16)]
        p.wait_all("tensor", "dw")
        p.wait_all("tensor", "ds")
        p.wait("tensor", "ve", p.cur("ve"))
        def bmrf(v):
            v.tensor_mul(ctx["bmr"], stat_m[:, :], stat_r[:, :])
            return v.tensor_scalar_mul(ctx["bmr"], ctx["bmr"], -1.0)
        v_bmr = step(p, "vector", "ve", bmrf)
        p.wait("scalar", "ve", v_bmr)

        pse = [ctx["ps_wo"].ap()[:, :], ctx["ps_big"].ap()[:, 0:1024]]
        xts = [ctx["xt0"], ctx["xt1"]]
        ats = [ctx["at0"], ctx["at1"]]
        yts = [ctx["ytile"], ctx["yt1"]]
        v_bank = {0: p.cur("ve"), 1: p.cur("ve")}
        v_free = {0: p.cur("ve"), 1: p.cur("ve")}
        v_stats = {0: p.cur("ve"), 1: p.cur("ve")}
        st_out = {}
        e_toks = {}

        def e_first(tb):
            b = tb % 2
            slot = tb if tb < 8 else 15 - tb
            off = 0 if tb < 8 else 128
            scol = slot * 256 + off
            p.wait("gpsimd", "ve", v_free[b])
            v_x = dma(p, "gpsimd", xts[b], x_own[bt, 128*tb:128*(tb+1), :], "dw")
            v_a = dma(p, "gpsimd", ats[b], attn_dram[128*tb:128*(tb+1), :], "dw")
            p.wait("tensor", "ve", v_bank[b])

            def omms(te, scol=scol, acc=pse[b]):
                r = None
                for half in range(2):
                    for c in range(NCH):
                        te.matmul(acc[:, 512*half:512*(half+1)],
                                  ctx[f"avmP{c}"][:, scol:scol+128],
                                  ctx[f"wo_p{c}"][:, 512*half:512*(half+1)],
                                  start=(c == 0), stop=False)
                    r = te.matmul(acc[:, 512*half:512*(half+1)],
                                  ctx["nselb"][:, scol:scol+128],
                                  ctx["bo_sb"][:, 512*half:512*(half+1)],
                                  start=False, stop=True)
                return r
            v_om = step(p, "tensor", "pe", omms)
            p.waitt("scalar", v_x)
            if tb >= 2:
                p.waitt("scalar", st_out[tb - 2])

            def y0f(s, b=b, tb=tb):
                return s.activation(yts[b], xts[b], AF.Identity,
                                    scale=stat_r[:, tb:tb+1],
                                    bias=ctx["bmr"][:, tb:tb+1])
            v_y0 = step(p, "scalar", "se", y0f)
            p.wait("vector", "pe", v_om)
            p.wait("vector", "se", v_y0)

            def yaf(v, b=b):
                v.tensor_add(yts[b], yts[b], pse[b])
                return v.reduce_sum(ctx[f"ssum{b}"], yts[b], AX.X)
            v_y = step(p, "vector", "ve", yaf)
            v_bank[b] = v_y
            e_toks[tb] = (v_a, v_y)

        def e_second(tb):
            b = tb % 2
            v_a, v_y = e_toks[tb]
            p.wait("scalar", "ve", v_y)
            p.wait("scalar", "ve", v_stats[b])
            v_sq = step(p, "scalar", "se", lambda s, b=b: s.activation(
                ctx["xq_e"], yts[b], AF.Square, accum_out=ctx[f"ssq{b}"]))
            p.wait("vector", "se", v_sq)

            def statf(v, b=b):
                v.tensor_scalar_mul(ctx[f"tvar{b}"], ctx[f"ssum{b}"], 1.0/1024)
                v.tensor_mul(ctx[f"tmsq{b}"], ctx[f"tvar{b}"], ctx[f"tvar{b}"])
                return v.scalar_tensor_tensor(ctx[f"tmsq{b}"], ctx[f"ssq{b}"],
                                              1.0/1024, ctx[f"tmsq{b}"],
                                              A.mult, A.subtract)
            v_st2 = step(p, "vector", "ve", statf)
            v_stats[b] = v_st2
            p.wait("scalar", "ve", v_st2)

            def rstd(s, b=b):
                s.activation(ctx[f"tln{b}"], ctx[f"tmsq{b}"], AF.Ln,
                             bias=ctx["epsc"][:, :])
                return s.activation(ctx[f"tln{b}"], ctx[f"tln{b}"], AF.Exp,
                                    scale=-0.5)
            v_rs = step(p, "scalar", "se", rstd)
            p.wait("vector", "se", v_rs)

            def b2f(v, b=b):
                v.tensor_mul(ctx[f"bias2{b}"], ctx[f"tvar{b}"], ctx[f"tln{b}"])
                return v.tensor_scalar_mul(ctx[f"bias2{b}"], ctx[f"bias2{b}"],
                                           -1.0)
            v_b2 = step(p, "vector", "ve", b2f)
            p.wait("scalar", "ve", v_b2)
            v_f1 = step(p, "scalar", "se", lambda s, b=b: s.activation(
                yts[b], yts[b], AF.Identity, scale=ctx[f"tln{b}"],
                bias=ctx[f"bias2{b}"]))
            p.wait("vector", "se", v_f1)
            p.waitt("vector", v_a)

            def fin2(v, b=b):
                v.tensor_mul(yts[b], yts[b], ctx["g_rep"])
                v.tensor_add(yts[b], yts[b], ctx["b_rep"])
                v.tensor_add(yts[b], yts[b], xts[b])
                return v.tensor_add(yts[b], yts[b], ats[b])
            v_f2 = step(p, "vector", "ve", fin2)
            v_free[b] = v_f2
            p.wait("sync", "ve", v_f2)
            st_out[tb] = dma(p, "sync", out_own[bt, 128*tb:128*(tb+1), :],
                             yts[b], "dx")

        for it in range(17):
            if it < 16:
                e_first(it)
            if it >= 1:
                e_second(it - 1)
            if feed is not None:
                feed()
        if feed is not None:
            while feed():
                pass
        p.waitt("scalar", st_out[14])
        p.waitt("scalar", st_out[15])


    feed0 = make_ln_feeder(0)
    emit_f1_tail(feed0)
    emit_a_mid(0)
    emit_p1b()
    emit_a_back(0, tail_hook=make_e_pro_feeder(0))
    feed1 = make_ln_feeder(1, alt=True)
    emit_e(0, feed1, preload_wo_hi=True)
    emit_a_mid(1, wo_hi_preloaded=True)
    emit_a_back(1, tail_hook=make_e_pro_feeder(1))
    emit_e(1)

    p.wait_all("gpsimd", "dx")
    p.wait_all("gpsimd", "dw")
    p.wait_all("gpsimd", "dg")
    return p


def kernel(L0_attn_x, L0_attn_W_DQ, L0_attn_W_UQ, L0_attn_W_QR, L0_attn_W_DKV,
           L0_attn_W_UK, L0_attn_W_UV, L0_attn_W_KR, L0_attn_W_O,
           L0_ffn_attn_mask, L0_ffn_Wg, L0_ffn_Wq, L0_ffn_bq, L0_ffn_Wkv,
           L0_ffn_bkv, L0_ffn_Wo, L0_ffn_bo, L0_ffn_ln_g, L0_ffn_ln_b):
    x = np.ascontiguousarray(np.asarray(L0_attn_x, dtype=np.float32))
    cst = host_constants()

    nc = bass.Bass("TRN2", target_bir_lowering=False)
    p = build(nc)
    with nc.Block() as blk:
        p.build(blk)

    W_UQ = np.asarray(L0_attn_W_UQ, dtype=np.float32)
    W_QRp = permute_qr_cols(np.asarray(L0_attn_W_QR, dtype=np.float32))
    uqp = np.zeros((D_LAT, 1568), np.float32)
    for h in range(16):
        uqp[:, 96*h:96*h+64] = W_UQ[:, 64*h:64*h+64]
        uqp[:, 96*h+64:96*h+96] = W_QRp[:, 32*h:32*h+32]
    uqp[:, 1536:1568] = W_UQ[:, 0:32]
    W_UK = np.asarray(L0_attn_W_UK, dtype=np.float32)
    ukp = np.concatenate([W_UK, W_UK[:, 0:64]], axis=1)
    krp = np.zeros((D_MODEL, 128), np.float32)
    krp[:, 64:96] = permute_qr_cols(np.asarray(L0_attn_W_KR, dtype=np.float32))
    shared = {
        "W_DQ": np.ascontiguousarray(L0_attn_W_DQ, dtype=np.float32),
        "W_UQP": np.ascontiguousarray(uqp),
        "W_DKV": np.ascontiguousarray(L0_attn_W_DKV, dtype=np.float32),
        "W_UKP": np.ascontiguousarray(ukp),
        "W_UV": np.ascontiguousarray(L0_attn_W_UV, dtype=np.float32),
        "W_KRP": np.ascontiguousarray(krp),
        "W_O": np.ascontiguousarray(L0_attn_W_O, dtype=np.float32),
        "Wg": np.ascontiguousarray(L0_ffn_Wg, dtype=np.float32),
        "Wq_r": np.ascontiguousarray(np.asarray(L0_ffn_Wq, dtype=np.float32)
                                     .transpose(1, 0, 2).reshape(D_MODEL, 16 * 64)),
        "Wkv_r": np.ascontiguousarray(np.asarray(L0_ffn_Wkv, dtype=np.float32)
                                      .transpose(1, 0, 2).reshape(D_MODEL, 16 * 128)),
        "Wo_f": np.ascontiguousarray(np.asarray(L0_ffn_Wo, dtype=np.float32)
                                     .reshape(16 * 64, 1024)),
        "bq_td": np.ascontiguousarray(np.asarray(L0_ffn_bq, dtype=np.float32).T),
        "bkv_td": np.ascontiguousarray(np.asarray(L0_ffn_bkv, dtype=np.float32).T),
        "g_rep": np.ascontiguousarray(np.broadcast_to(
            np.asarray(L0_ffn_ln_g, dtype=np.float32)[None, :], (128, 1024))),
        "b_rep": np.ascontiguousarray(np.broadcast_to(
            np.asarray(L0_ffn_ln_b, dtype=np.float32)[None, :], (128, 1024))),
        "bo_row": np.ascontiguousarray(
            np.asarray(L0_ffn_bo, dtype=np.float32)[None, :]),
    }
    for k, v in cst.items():
        shared["c_" + k] = np.ascontiguousarray(v)

    in_maps = []
    for c in range(NC):
        blocks = [c, 15 - c]
        m = dict(shared)
        m["x_own"] = np.ascontiguousarray(x[2 * c:2 * c + 2])
        xp = np.empty((16, NBLK, 128, D_MODEL), np.float32)
        for bi, tb in enumerate(blocks):
            xp[:, bi] = x[:, 128 * tb:128 * (tb + 1), :]
        m["x_pos"] = xp
        basearr = np.zeros((128, NBLK), np.float32)
        for bi in range(NBLK):
            basearr[:, bi] = bi * 2048 + np.arange(128)
        m["c_base"] = basearr
        jc = np.zeros((16, 2), np.float32)
        jc[:, 0] = 2 * c
        jc[:, 1] = 2 * c + 1
        m["c_jcol"] = jc
        in_maps.append(m)

    res = run_bass_kernel_spmd(nc, in_maps, core_ids=list(range(NC)))
    out = np.concatenate([res.results[c]["out_own"] for c in range(NC)], axis=0)
    return out
